# version 08
# baseline (speedup 1.0000x reference)
import numpy as np
import ml_dtypes
import concourse.bass as bass
import concourse.mybir as mybir
from concourse.bass_utils import run_bass_kernel_spmd
from contextlib import ExitStack

F32 = mybir.dt.float32
BF16 = mybir.dt.bfloat16
AF = mybir.ActivationFunctionType
ALU = mybir.AluOpType
AX = mybir.AxisListType.X

NCORES = 8
SEQ = 2048
D = 1024
KC = 8
NH = 16
DK = 64
WIN = 256
NST = 4
STW = 512
NQT = 16
FFKV = 256
NE = 8
DFF = 4096
NFB = 8
FBW = 512


class SemT:
    __slots__ = ("h", "n")

    def __init__(self, h):
        self.h = h
        self.n = 0

    def inc(self, instr, k=1):
        instr.then_inc(self.h, k)
        self.n += k
        return instr


def a_nch(Q):
    return min(Q + 1, 3)


def qloc(h):
    return (h % 4) + 4 * (h // 8), 64 * ((h // 4) % 2)


def build_program(stop_phase=None, nseqs=2):
    nc = bass.Bass("TRN2", target_bir_lowering=False)

    dt = lambda name, shape, dty=F32, kind="ExternalInput": nc.dram_tensor(name, shape, dty, kind=kind).ap()
    x_d = dt("x", [128, KC, 2 * SEQ])
    out_d = dt("out", [128, KC, 2 * SEQ], F32, "ExternalOutput")
    wq_o = dt("wq_o", [128, KC, D])
    wk_o = dt("wk_o", [128, KC, D])
    wv_o = dt("wv_o", [128, KC, D])
    wo_o = dt("wo_o", [128, KC, D])
    wq_f = dt("wq_f", [128, KC, D])
    wk_f = dt("wk_f", [128, KC, FFKV])
    wv_f = dt("wv_f", [128, KC, FFKV])
    wo_f = dt("wo_f", [128, KC, D])
    w1n_d = dt("w1n", [128, KC])
    w2n_d = dt("w2n", [128, KC])
    rw_d = dt("rw", [128, KC, NE])
    rb_d = dt("rb", [NE, 1])
    w1m = dt("w1m", [NE, NFB, 128, KC, FBW])
    w2m = dt("w2m", [NE, NFB, 128, KC, FBW])
    w3m = dt("w3m", [NE, NFB, 128, 4, D])
    b1t_d = dt("b1t", [128, NE, 32])
    b2t_d = dt("b2t", [128, NE, 32])
    b3t_d = dt("b3t", [128, NE, KC])
    cos_d = dt("cos_t", [128, 4, SEQ])
    sin_d = dt("sin_t", [128, 4, SEQ])
    id_f_d = dt("id_f", [128, 128])
    ones_bf_d = dt("ones_bf", [128, 1], BF16)
    ones_f_d = dt("ones_f", [1, 128])
    comb_scr_h = nc.dram_tensor("comb_scr", [NE, 2 * SEQ], F32, kind="Internal")
    comb_scr = comb_scr_h.ap()
    mD_d = dt("mD", [128, 4, 512], BF16)
    mT_d = dt("mT", [128, 4, 512], BF16)
    mA_d = dt("mA", [128, 6, 512], BF16)

    g = ExitStack()
    sbuf = lambda name, shape, dty=F32: g.enter_context(nc.sbuf_tensor(name, shape, dty))
    RES = sbuf("RES", [128, KC * SEQ])
    id_bf = None
    id_f = sbuf("id_f_s", [128, 128])
    ones_bf = sbuf("ones_bf_s", [128, 1], BF16)
    ones_f = sbuf("ones_f_s", [1, 128])
    w1n = sbuf("w1n_s", [128, KC])
    w2n = sbuf("w2n_s", [128, KC])
    rb_sb = sbuf("rb_s", [NE, 1])

    sems = {}
    for nm in ["LD", "CAST", "STAT", "ROWEV", "ROW1", "RSTD", "MR", "BC", "Y0",
               "PMM", "PEV", "SMM", "SEV", "MSK", "NM", "P", "PN", "PTMM",
               "PTEV", "AVMM", "OEV", "WMM", "X1", "WL1", "WL2", "WL3",
               "H1", "H2", "EO", "SIL", "HST", "EPI", "CBROW", "CBMM", "CBEV",
               "PAIRD", "STO", "AUX", "AUX2", "AUX3", "RADD", "T1", "ZC", "RMM",
               "W1B", "W2B", "CROWB", "SEVD", "CTM", "CRC", "CBM", "CBC2",
               "STM", "STEV", "PEXP", "AVM"]:
        sems[nm] = SemT(g.enter_context(nc.semaphore(nm)))

    S = sems

    def base():
        return {k: s.n for k, s in sems.items()}

    def store_block(seq):
        with nc.Block() as blk:
            @blk.sync
            def _(sy):
                S["STO"].inc(
                    sy.dma_start(out=out_d[:, :, seq * SEQ:(seq + 1) * SEQ],
                                 in_=RES[:].rearrange("p (k t) -> p k t", k=KC)), 16)

    with g:
        for seq in range(nseqs):
            tok0 = seq * SEQ

            with nc.Block() as blk:
                @blk.sync
                def _(sy, tok0=tok0, seq=seq):
                    if seq > 0:
                        sy.wait_ge(S["STO"].h, S["STO"].n)
                    S["LD"].inc(sy.dma_start(out=RES[:].rearrange("p (k t) -> p k t", k=KC),
                                             in_=x_d[:, :, tok0:tok0 + SEQ]), 16)
                    if seq == 0:
                        for dst, srcd in ((id_f, id_f_d),
                                          (ones_bf, ones_bf_d), (ones_f, ones_f_d),
                                          (w1n, w1n_d), (w2n, w2n_d), (rb_sb, rb_d)):
                            S["LD"].inc(sy.dma_start(out=dst[:], in_=srcd), 16)
                    sy.wait_ge(S["LD"].h, S["LD"].n)

            phase_A(nc, S, RES, id_bf, id_f, ones_bf, ones_f,
                    wq_o, wk_o, wv_o, wo_o, mA_d, tag=f"s{seq}")

            if stop_phase == "A":
                store_block(seq)
                continue

            seq_es = ExitStack()
            QZ = seq_es.enter_context(nc.sbuf_tensor(f"s{seq}QZ", [128, KC * SEQ], BF16))
            with seq_es:
                b_es = ExitStack()
                Kf = b_es.enter_context(nc.sbuf_tensor(f"s{seq}Kf", [128, 2 * SEQ], BF16))
                Vf = b_es.enter_context(nc.sbuf_tensor(f"s{seq}Vf", [128, 16 * FFKV], BF16))
                with b_es:
                    phase_B1(nc, S, RES, ones_bf, ones_f, w1n, QZ, Kf, Vf,
                             wq_f, wk_f, wv_f, cos_d, sin_d, tag=f"s{seq}B1")
                    phase_B2(nc, S, RES, id_bf, id_f, ones_bf, ones_f, w2n, rb_sb,
                             QZ, Kf, Vf, comb_scr, seq, wo_f, rw_d, mD_d, mT_d,
                             tag=f"s{seq}B2")
                if stop_phase == "B":
                    store_block(seq)
                    continue
                phase_C(nc, S, RES, ones_f, QZ, comb_scr_h, w1m, w2m, w3m,
                        b1t_d, b2t_d, b3t_d, seq, tag=f"s{seq}C")
                store_block(seq)

        with nc.Block() as blk:
            @blk.sync
            def _(sy):
                for nm in ("STO", "LD", "WL1", "WL2", "WL3", "W1B", "W2B",
                           "MSK", "CBROW", "CROWB", "EO"):
                    if S[nm].n:
                        sy.wait_ge(S[nm].h, S[nm].n)

    return nc


def phase_A(nc, S, RES, id_bf, id_f, ones_bf, ones_f, wq_o, wk_o, wv_o, wo_o,
            mA_d, tag=""):
    es = ExitStack()
    sbuf = lambda name, shape, dty=F32: es.enter_context(nc.sbuf_tensor(tag + name, shape, dty))
    psum = lambda name, shape, dty=F32: es.enter_context(nc.psum_tensor(tag + name, shape, dty))
    wq_s = sbuf("Awq", [128, KC * D], BF16)
    wk_s = sbuf("Awk", [128, KC * D], BF16)
    wv_s = sbuf("Awv", [128, KC * D], BF16)
    wo_s = sbuf("Awo", [128, KC * D], BF16)
    Kb = [sbuf(f"AK{i}", [128, KC * STW], BF16) for i in range(2)]
    Vb = [sbuf(f"AV{i}", [128, 4 * D], BF16) for i in range(2)]
    Qst = sbuf("AQ", [128, KC * STW], BF16)
    Y0 = sbuf("AY0", [128, KC * STW], BF16)
    Ost = sbuf("AO", [128, KC * STW], BF16)

    STs = [sbuf(f"ASTs{i}", [128, 512]) for i in range(2)]
    PT2 = [sbuf(f"APT2{i}", [128, 512], BF16) for i in range(2)]
    c_col = sbuf("Accol", [128, 8])
    c_row = sbuf("Acrow", [1, 512])
    c_bcs = sbuf("Acbcs", [128, 512])


    mA_s = sbuf("AmA", [128, 6 * 512], BF16)
    mF_s = mA_s[:, 3 * 512 + 127: 3 * 512 + 511]
    m0_s = mA_s[:, 3 * 512 + 383: 3 * 512 + 511]
    m1_s = mA_s[:, 3 * 512 + 255: 3 * 512 + 511]
    r_mean = sbuf("Arm", [1, STW])
    r_msq = sbuf("Arq", [1, STW])
    r_var = c_row
    r_rstd = r_msq
    r_nmax = [sbuf(f"Anm{i}", [128, 1]) for i in range(2)]
    r_sum = [sbuf(f"Asum{i}", [128, 1]) for i in range(2)]
    r_lns = [sbuf(f"Alns{i}", [128, 1]) for i in range(2)]
    psC = psum("ApsC", [128, 512])
    psD = psum("ApsD", [128, 512])
    psS0 = psum("ApsS", [128, 384])
    STb = [psum(f"AST{i}", [128, 512]) for i in range(2)]
    AVb = psum("AAV", [64, 512])
    CBCb = psum("ACBC", [128, 512])
    miscb = psum("Amisc", [128, 512])
    ps_crow = miscb[0:1, :]

    xb = [Y0[:, 0:STW], Y0[:, STW:2 * STW]]
    xq = [Y0[:, 2 * STW:3 * STW], Y0[:, 3 * STW:4 * STW]]

    psS = [psS0, psC]

    b0 = {k: s.n for k, s in S.items()}

    def kt_range(st):
        kts = list(range(max(0, 4 * st - 2), 4 * st + 4))
        kts.remove(4 * st)
        return [4 * st] + kts

    def kt_cols(st, kt):
        di = kt - 4 * st + 2
        c0, c1 = {0: (0, 128), 1: (0, 256), 2: (0, 512), 3: (0, 512),
                  4: (256, 512), 5: (384, 512)}[di]
        return di, c0, c1 - c0

    v_nkt = [len(kt_range(st)) for st in range(NST) for h in range(NH)]
    kt_pref = [0]
    for n in v_nkt:
        kt_pref.append(kt_pref[-1] + n)

    with es, nc.Block() as blk:
        @blk.gpsimd
        def _(gp):
            for w_d, w_s in ((wq_o, wq_s), (wk_o, wk_s), (wv_o, wv_s), (wo_o, wo_s)):
                S["WL1"].inc(gp.dma_start(out=w_s[:], in_=w_d.rearrange("p k d -> p (k d)")), 16)
            S["WL1"].inc(gp.dma_start(out=mA_s[:].rearrange("p (i c) -> p i c", i=6), in_=mA_d), 16)

        @blk.tensor
        def _(t):
            t.wait_ge(S["WL1"].h, b0["WL1"] + 16 * 5)

            def stats_pe(st, h):
                for qt in range(4):
                    w = (st * 16 + h) * 4 + qt
                    Q = st * 4 + qt
                    nch = a_nch(Q)
                    kt0 = Q - nch + 1
                    if w >= 2:
                        t.wait_ge(S["P"].h, b0["P"] + w - 1)
                    col = 0
                    seg_start = kt0
                    lhs = Qst[64 * (h % 2):64 * (h % 2) + 64,
                              (h // 2) * STW + qt * 128:(h // 2) * STW + qt * 128 + 128]
                    while seg_start <= Q:
                        seg_st = seg_start // 4
                        seg_end = min(Q, seg_st * 4 + 3)
                        n = (seg_end - seg_start + 1)
                        kb = Kb[seg_st % 2]
                        rel = seg_start - seg_st * 4
                        rhs = kb[64 * (h % 2):64 * (h % 2) + 64,
                                 (h // 2) * STW + rel * 128:(h // 2) * STW + rel * 128 + n * 128]
                        ins = t.matmul(psS[w % 2][:, col:col + n * 128], lhs, rhs,
                                       start=True, stop=True)
                        col += n * 128
                        seg_start = seg_end + 1
                    S["SMM"].inc(ins, 1)

            def t2_pe(st, h):
                v = st * 16 + h
                t.wait_ge(S["PN"].h, b0["PN"] + 4 * (v + 1))
                if v >= 1:
                    t.wait_ge(S["CRC"].h, b0["CRC"] + v)
                for qt in range(4):
                    ins = t.transpose(ps_crow[0:1, qt * 128:(qt + 1) * 128],
                                      c_col[:, (v % 2) * 4 + qt:(v % 2) * 4 + qt + 1], id_f[:])
                S["CTM"].inc(ins, 1)
                t.wait_ge(S["CRC"].h, b0["CRC"] + v + 1)
                if v >= 1:
                    t.wait_ge(S["CBC2"].h, b0["CBC2"] + v)
                S["CBM"].inc(t.matmul(CBCb[:], ones_f[:], c_row[:], start=True, stop=True), 1)
                kts = list(kt_range(st))
                nkt = len(kts)
                for i in range(nkt + 2):
                    if i < nkt:
                        kt = kts[i]
                        di, c0, wc = kt_cols(st, kt)
                        m = kt_pref[v] + i
                        if m >= 2:
                            t.wait_ge(S["STEV"].h, b0["STEV"] + m - 1)
                        s_k = kt // 4
                        rhs = Qst[64 * (h % 2):64 * (h % 2) + 64,
                                  (h // 2) * STW + c0:(h // 2) * STW + c0 + wc]
                        lhsK = Kb[s_k % 2][64 * (h % 2):64 * (h % 2) + 64,
                                           (h // 2) * STW + (kt % 4) * 128:
                                           (h // 2) * STW + (kt % 4) * 128 + 128]
                        S["STM"].inc(t.matmul(STb[m % 2][:, c0:c0 + wc], lhsK, rhs,
                                              start=True, stop=True), 1)
                    if i >= 2:
                        ka = kts[i - 2]
                        di, c0, wc = kt_cols(st, ka)
                        m = kt_pref[v] + i - 2
                        t.wait_ge(S["PEXP"].h, b0["PEXP"] + m + 1)
                        if i == 2:
                            t.wait_ge(S["OEV"].h, b0["OEV"] + v)
                        s_k = ka // 4
                        S["AVM"].inc(t.matmul(AVb[0:64, c0:c0 + wc],
                                              Vb[s_k % 2][:, (ka % 4) * D + 64 * h:
                                                          (ka % 4) * D + 64 * h + 64],
                                              PT2[m % 2][:, c0:c0 + wc],
                                              start=(i == 2), stop=(i == nkt + 1)), 1)

            for st in range(NST):
                stb = st * STW
                if st >= 1:
                    t.wait_ge(S["Y0"].h, b0["Y0"] + st * 8)
                    t.wait_ge(S["X1"].h, b0["X1"] + st * 8)
                    t.wait_ge(S["P"].h, b0["P"] + st * 64)
                for kc in range(KC):
                    t.wait_ge(S["CAST"].h, b0["CAST"] + st * 16 + kc * 2 + 2)
                    i1 = t.matmul(psC[0:1, :], ones_bf[:], xb[kc % 2],
                                  start=(kc == 0), stop=(kc == KC - 1))
                    i2 = t.matmul(psD[0:1, :], ones_bf[:], xq[kc % 2],
                                  start=(kc == 0), stop=(kc == KC - 1))
                    S["STAT"].inc(i1, 1)
                    S["STAT"].inc(i2, 1)
                t.wait_ge(S["MR"].h, b0["MR"] + st + 1)
                i1 = t.matmul(psC[:], ones_f[:], r_rstd[:], start=True, stop=True)
                i2 = t.matmul(psD[:], ones_f[:], r_mean[:], start=True, stop=True)
                S["BC"].inc(i2, 1)
                t.wait_ge(S["Y0"].h, b0["Y0"] + st * 8 + 8)
                for gi in range(24):
                    pb = (psC, psD)[gi % 2]
                    if gi >= 2:
                        t.wait_ge(S["PEV"].h, b0["PEV"] + st * 24 + gi - 1)
                    if gi < 16:
                        w_s = wq_s if gi < 8 else wk_s
                        oc = gi % 8
                        for kc in range(KC):
                            ins = t.matmul(pb[:], w_s[:, kc * D + oc * 128: kc * D + oc * 128 + 128],
                                           Y0[:, kc * STW:(kc + 1) * STW],
                                           start=(kc == 0), stop=(kc == KC - 1))
                    else:
                        vi = gi - 16
                        tsub, half = vi // 2, vi % 2
                        for kc in range(KC):
                            ins = t.matmul(pb[:],
                                           Y0[:, kc * STW + tsub * 128: kc * STW + tsub * 128 + 128],
                                           wv_s[:, kc * D + half * 512: kc * D + half * 512 + 512],
                                           start=(kc == 0), stop=(kc == KC - 1))
                    S["PMM"].inc(ins, 1)
                t.wait_ge(S["PEV"].h, b0["PEV"] + st * 24 + 24)
                for h in range(NH):
                    stats_pe(st, h)
                    if h >= 1:
                        t2_pe(st, h - 1)
                t2_pe(st, NH - 1)
                t.wait_ge(S["OEV"].h, b0["OEV"] + st * 16 + 16)
                for oc in range(KC):
                    pb = (psC, psD)[oc % 2]
                    if oc >= 2:
                        t.wait_ge(S["X1"].h, b0["X1"] + st * 8 + oc - 1)
                    for kc in range(KC):
                        ins = t.matmul(pb[:], wo_s[:, kc * D + oc * 128: kc * D + oc * 128 + 128],
                                       Ost[:, kc * STW:(kc + 1) * STW],
                                       start=(kc == 0), stop=(kc == KC - 1))
                    S["WMM"].inc(ins, 1)

        @blk.scalar
        def _(sc):
            sc.wait_ge(S["WL1"].h, b0["WL1"] + 16 * 5)

            def stats_act(st, h):
                for qt in range(4):
                    w = (st * 16 + h) * 4 + qt
                    Q = st * 4 + qt
                    nch = a_nch(Q)
                    wdt = nch * 128
                    sc.wait_ge(S["NM"].h, b0["NM"] + w + 1)
                    sc.activation(out=psS[w % 2][:, 0:wdt], in_=psS[w % 2][:, 0:wdt],
                                  func=AF.Exp, bias=r_nmax[w % 2][:], accum_out=r_sum[w % 2][:])
                    sc.drain()
                    S["P"].inc(sc.activation(out=r_lns[w % 2][:], in_=r_sum[w % 2][:], func=AF.Ln))

            def t2_act(st, h):
                v = st * 16 + h
                sc.wait_ge(S["CTM"].h, b0["CTM"] + v + 1)
                S["CRC"].inc(sc.activation(out=c_row[:], in_=ps_crow, func=AF.Copy))
                sc.wait_ge(S["CBM"].h, b0["CBM"] + v + 1)
                if v >= 1:
                    sc.wait_ge(S["STEV"].h, b0["STEV"] + kt_pref[v])
                S["CBC2"].inc(sc.activation(out=c_bcs[:], in_=CBCb[:], func=AF.Copy))
                nkt = v_nkt[v]
                kts = list(kt_range(st))
                for i in range(nkt):
                    kt = kts[i]
                    di, c0, wc = kt_cols(st, kt)
                    m = kt_pref[v] + i
                    sc.wait_ge(S["STEV"].h, b0["STEV"] + m + 1)
                    if m >= 2:
                        sc.wait_ge(S["AVM"].h, b0["AVM"] + m - 1)
                    S["PEXP"].inc(sc.activation(out=PT2[m % 2][:, c0:c0 + wc],
                                                in_=STs[m % 2][:, c0:c0 + wc], func=AF.Exp))
                sc.wait_ge(S["AVM"].h, b0["AVM"] + kt_pref[v + 1])
                dst = Ost[64 * (h % 2):64 * (h % 2) + 64, (h // 2) * STW:(h // 2 + 1) * STW]
                S["OEV"].inc(sc.activation(out=dst, in_=AVb[:], func=AF.Copy))

            for st in range(NST):
                stb = st * STW
                if st >= 1:
                    sc.wait_ge(S["PMM"].h, b0["PMM"] + st * 24)
                for kc in range(KC):
                    if kc >= 2:
                        sc.wait_ge(S["STAT"].h, b0["STAT"] + st * 16 + (kc - 2) * 2 + 2)
                    src2 = RES[:, kc * SEQ + stb: kc * SEQ + stb + STW]
                    S["CAST"].inc(sc.activation(out=xb[kc % 2], in_=src2, func=AF.Copy))
                    S["CAST"].inc(sc.activation(out=xq[kc % 2], in_=src2, func=AF.Square))
                sc.wait_ge(S["STAT"].h, b0["STAT"] + st * 16 + 16)
                sc.activation(out=r_mean[:], in_=psC[0:1, :], func=AF.Copy, scale=1.0 / 1024.0)
                S["ROWEV"].inc(sc.activation(out=r_msq[:], in_=psD[0:1, :], func=AF.Copy, scale=1.0 / 1024.0))
                sc.wait_ge(S["ROW1"].h, b0["ROW1"] + st + 1)
                sc.activation(out=r_var[:], in_=r_var[:], func=AF.Ln)
                sc.drain()
                S["RSTD"].inc(sc.activation(out=r_rstd[:], in_=r_var[:], func=AF.Exp, scale=-0.5))
                for gi in range(24):
                    sc.wait_ge(S["PMM"].h, b0["PMM"] + st * 24 + gi + 1)
                    pb = (psC, psD)[gi % 2]
                    if gi < 8:
                        dest = Qst[:, gi * STW:(gi + 1) * STW]
                        ins = sc.activation(out=dest, in_=pb[:], func=AF.Copy, scale=0.125)
                    elif gi < 16:
                        oc = gi - 8
                        dest = Kb[st % 2][:, oc * STW:(oc + 1) * STW]
                        ins = sc.activation(out=dest, in_=pb[:], func=AF.Copy)
                    else:
                        vi = gi - 16
                        tsub, half = vi // 2, vi % 2
                        dest = Vb[st % 2][:, tsub * D + half * 512: tsub * D + half * 512 + 512]
                        ins = sc.activation(out=dest, in_=pb[:], func=AF.Copy)
                    S["PEV"].inc(ins)
                for h in range(NH):
                    stats_act(st, h)
                    if h >= 1:
                        t2_act(st, h - 1)
                t2_act(st, NH - 1)

        @blk.vector
        def _(v):
            v.wait_ge(S["WL1"].h, b0["WL1"] + 16 * 5)

            def c_write(w2):
                vv, qt2 = w2 // 4, w2 % 4
                v.wait_ge(S["P"].h, b0["P"] + w2 + 1)
                if qt2 == 0 and vv >= 2:
                    v.wait_ge(S["CTM"].h, b0["CTM"] + vv - 1)
                S["PN"].inc(v.tensor_sub(c_col[:, (vv % 2) * 4 + qt2:(vv % 2) * 4 + qt2 + 1],
                                         r_nmax[w2 % 2][:], r_lns[w2 % 2][:]))

            def stats_dve(st, h):
                for qt in range(4):
                    w = (st * 16 + h) * 4 + qt
                    Q = st * 4 + qt
                    nch = a_nch(Q)
                    wdt = nch * 128
                    msk = (m0_s, m1_s, mF_s)[nch - 1]
                    v.wait_ge(S["SMM"].h, b0["SMM"] + w + 1)
                    if w >= 2:
                        v.wait_ge(S["P"].h, b0["P"] + w - 1)
                    v.tensor_add(psS[w % 2][:, 0:wdt], psS[w % 2][:, 0:wdt], msk)
                    v.drain()
                    S["NM"].inc(v.reduce_max(r_nmax[w % 2][:], psS[w % 2][:, 0:wdt],
                                             axis=AX, negate=True))
                    if w >= 1:
                        c_write(w - 1)

            def t2_dve(st, h):
                v2 = st * 16 + h
                kts = list(kt_range(st))
                for i, kt in enumerate(kts):
                    di, c0, wc = kt_cols(st, kt)
                    m = kt_pref[v2] + i
                    v.wait_ge(S["STM"].h, b0["STM"] + m + 1)
                    v.wait_ge(S["CBC2"].h, b0["CBC2"] + v2 + 1)
                    if m >= 2:
                        v.wait_ge(S["PEXP"].h, b0["PEXP"] + m - 1)
                    v.tensor_add(STs[m % 2][:, c0:c0 + wc], STb[m % 2][:, c0:c0 + wc],
                                 c_bcs[:, c0:c0 + wc])
                    v.drain()
                    S["STEV"].inc(v.tensor_add(STs[m % 2][:, c0:c0 + wc],
                                               STs[m % 2][:, c0:c0 + wc],
                                               mA_s[:, di * 512 + c0: di * 512 + c0 + wc]))

            for st in range(NST):
                stb = st * STW
                v.wait_ge(S["ROWEV"].h, b0["ROWEV"] + st + 1)
                if st >= 1:
                    v.wait_ge(S["CBM"].h, b0["CBM"] + 16 * st)
                v.tensor_mul(r_var[:], r_mean[:], r_mean[:])
                v.drain()
                S["ROW1"].inc(v.scalar_tensor_tensor(out=r_var[:], in0=r_msq[:],
                                                     scalar=1e-5, in1=r_var[:],
                                                     op0=ALU.add, op1=ALU.subtract))
                v.wait_ge(S["RSTD"].h, b0["RSTD"] + st + 1)
                S["MR"].inc(v.tensor_mul(r_mean[:], r_mean[:], r_rstd[:]))
                v.wait_ge(S["BC"].h, b0["BC"] + st + 1)
                if st >= 1:
                    v.wait_ge(S["PEXP"].h, b0["PEXP"] + kt_pref[16 * st])
                for kc in range(KC):
                    src2 = RES[:, kc * SEQ + stb: kc * SEQ + stb + STW]
                    v.tensor_mul(STs[0][:], src2, psC[:])
                    v.drain()
                    S["Y0"].inc(v.tensor_sub(Y0[:, kc * STW:(kc + 1) * STW], STs[0][:], psD[:]))
                    v.drain()
                for h in range(NH):
                    stats_dve(st, h)
                    if h == NH - 1:
                        c_write((st * 16 + h) * 4 + 3)
                    if h >= 1:
                        t2_dve(st, h - 1)
                t2_dve(st, NH - 1)
                for oc in range(KC):
                    pb = (psC, psD)[oc % 2]
                    v.wait_ge(S["WMM"].h, b0["WMM"] + st * 8 + oc + 1)
                    dst = RES[:, oc * SEQ + stb: oc * SEQ + stb + STW]
                    S["X1"].inc(v.tensor_add(dst, dst, pb[:]))


def phase_B1(nc, S, RES, ones_bf, ones_f, w1n, QZ, Kf, Vf,
             wq_f, wk_f, wv_f, cos_d, sin_d, tag):
    es = ExitStack()
    sbuf = lambda name, shape, dty=F32: es.enter_context(nc.sbuf_tensor(tag + name, shape, dty))
    psum = lambda name, shape, dty=F32: es.enter_context(nc.psum_tensor(tag + name, shape, dty))
    wqf_s = sbuf("wqf", [128, KC * D], BF16)
    wkf_s = sbuf("wkf", [128, KC * FFKV], BF16)
    wvf_s = sbuf("wvf", [128, KC * FFKV], BF16)
    cos_s = sbuf("cos", [128, 4 * STW])
    sin_s = sbuf("sin", [128, 4 * STW])
    y_st = sbuf("y", [128, KC * STW])
    xr = sbuf("xr", [128, KC * STW], BF16)
    xb = [sbuf(f"xb{i}", [128, STW], BF16) for i in range(2)]
    xq = [sbuf(f"xq{i}", [128, STW], BF16) for i in range(2)]
    zE = sbuf("zE", [128, STW])
    zO = sbuf("zO", [128, STW])
    tA = sbuf("tA", [128, STW])
    tB = sbuf("tB", [128, STW])
    tC = sbuf("tC", [128, STW])
    tD = sbuf("tD", [128, STW])
    r_mean = sbuf("rm", [1, STW])
    r_msq = sbuf("rq", [1, STW])
    r_var = sbuf("rv", [1, STW])
    r_rstd = sbuf("rr", [1, STW])
    psA = psum("psA", [128, 512])
    psB = psum("psB", [128, 512])
    psC = psum("psC", [128, 512])
    psD = psum("psD", [128, 512])

    b0 = {k: s.n for k, s in S.items()}
    NPJ = 14

    with es, nc.Block() as blk:
        @blk.gpsimd
        def _(gp):
            for w_d, w_s in ((wq_f, wqf_s), (wk_f, wkf_s), (wv_f, wvf_s)):
                S["WL1"].inc(gp.dma_start(out=w_s[:], in_=w_d.rearrange("p k d -> p (k d)")), 16)

        @blk.sync
        def _(sy):
            for st in range(NST):
                if st >= 1:
                    sy.wait_ge(S["T1"].h, b0["T1"] + st * 8)
                S["LD"].inc(sy.dma_start(out=cos_s[:].rearrange("p (c t) -> p c t", c=4),
                                         in_=cos_d[:, :, st * STW:(st + 1) * STW]), 16)
                S["LD"].inc(sy.dma_start(out=sin_s[:].rearrange("p (c t) -> p c t", c=4),
                                         in_=sin_d[:, :, st * STW:(st + 1) * STW]), 16)

        @blk.scalar
        def _(sc):
            for st in range(NST):
                stb = st * STW
                if st >= 1:
                    sc.wait_ge(S["STAT"].h, b0["STAT"] + st * 24)
                for kc in range(KC):
                    if kc >= 2:
                        sc.wait_ge(S["STAT"].h, b0["STAT"] + st * 24 + (kc - 2) * 2 + 2)
                    src = RES[:, kc * SEQ + stb: kc * SEQ + stb + STW]
                    S["CAST"].inc(sc.activation(out=xb[kc % 2][:], in_=src, func=AF.Copy))
                    S["CAST"].inc(sc.activation(out=xq[kc % 2][:], in_=src, func=AF.Square))
                sc.wait_ge(S["STAT"].h, b0["STAT"] + st * 24 + 16)
                sc.activation(out=r_mean[:], in_=psA[0:1, :], func=AF.Copy, scale=1.0 / 1024.0)
                S["ROWEV"].inc(sc.activation(out=r_msq[:], in_=psB[0:1, :], func=AF.Copy, scale=1.0 / 1024.0))
                sc.wait_ge(S["ROW1"].h, b0["ROW1"] + st * 2 + 1)
                sc.activation(out=r_var[:], in_=r_var[:], func=AF.Ln)
                sc.drain()
                S["RSTD"].inc(sc.activation(out=r_rstd[:], in_=r_var[:], func=AF.Exp, scale=-0.5))
                for kc in range(KC):
                    sc.wait_ge(S["Y0"].h, b0["Y0"] + st * 8 + kc + 1)
                    if kc >= 2:
                        sc.wait_ge(S["STAT"].h, b0["STAT"] + st * 24 + 16 + (kc - 2) + 1)
                    S["CAST"].inc(sc.activation(out=xq[kc % 2][:],
                                                in_=y_st[:, kc * STW:(kc + 1) * STW],
                                                func=AF.Square))
                sc.wait_ge(S["ROW1"].h, b0["ROW1"] + st * 2 + 2)
                sc.activation(out=r_var[:], in_=r_var[:], func=AF.Ln)
                sc.drain()
                S["RSTD"].inc(sc.activation(out=r_rstd[:], in_=r_var[:], func=AF.Exp, scale=-0.5))
                for gi in range(NPJ):
                    sc.wait_ge(S["PMM"].h, b0["PMM"] + st * NPJ + gi + 1)
                    pb = (psC, psD)[gi % 2]
                    if gi < 8:
                        ins = sc.activation(out=QZ[:, gi * SEQ + stb: gi * SEQ + stb + STW],
                                            in_=pb[:], func=AF.Copy, scale=0.125)
                    elif gi < 10:
                        oc = gi - 8
                        ins = sc.activation(out=Kf[:, oc * SEQ + stb: oc * SEQ + stb + STW],
                                            in_=pb[:], func=AF.Copy)
                    else:
                        tsub = gi - 10
                        ins = sc.activation(out=Vf[:, (st * 4 + tsub) * FFKV: (st * 4 + tsub + 1) * FFKV],
                                            in_=pb[:, 0:FFKV], func=AF.Copy)
                    S["PEV"].inc(ins)

        @blk.tensor
        def _(t):
            t.wait_ge(S["WL1"].h, b0["WL1"] + 48)
            for st in range(NST):
                if st >= 1:
                    t.wait_ge(S["Y0"].h, b0["Y0"] + st * 8)
                    t.wait_ge(S["T1"].h, b0["T1"] + st * 8)
                for kc in range(KC):
                    t.wait_ge(S["CAST"].h, b0["CAST"] + st * 24 + kc * 2 + 2)
                    i1 = t.matmul(psA[0:1, :], ones_bf[:], xb[kc % 2][:],
                                  start=(kc == 0), stop=(kc == KC - 1))
                    i2 = t.matmul(psB[0:1, :], ones_bf[:], xq[kc % 2][:],
                                  start=(kc == 0), stop=(kc == KC - 1))
                    S["STAT"].inc(i1, 1)
                    S["STAT"].inc(i2, 1)
                t.wait_ge(S["MR"].h, b0["MR"] + st + 1)
                i1 = t.matmul(psA[:], ones_f[:], r_rstd[:], start=True, stop=True)
                i2 = t.matmul(psB[:], ones_f[:], r_mean[:], start=True, stop=True)
                S["BC"].inc(i2, 1)
                t.wait_ge(S["Y0"].h, b0["Y0"] + st * 8 + 8)
                for kc in range(KC):
                    t.wait_ge(S["CAST"].h, b0["CAST"] + st * 24 + 16 + kc + 1)
                    ins = t.matmul(psB[0:1, :], ones_bf[:], xq[kc % 2][:],
                                   start=(kc == 0), stop=(kc == KC - 1))
                    S["STAT"].inc(ins, 1)
                t.wait_ge(S["RSTD"].h, b0["RSTD"] + st * 2 + 2)
                S["BC"].inc(t.matmul(psA[:], ones_f[:], r_rstd[:], start=True, stop=True), 1)
                t.wait_ge(S["T1"].h, b0["T1"] + st * 8 + 8)
                if st >= 1:
                    t.wait_ge(S["PEV"].h, b0["PEV"] + st * NPJ)
                for gi in range(NPJ):
                    pb = (psC, psD)[gi % 2]
                    if gi >= 2:
                        t.wait_ge(S["PEV"].h, b0["PEV"] + st * NPJ + gi - 1)
                    if gi < 8:
                        for kc in range(KC):
                            ins = t.matmul(pb[:], wqf_s[:, kc * D + gi * 128: kc * D + gi * 128 + 128],
                                           xr[:, kc * STW:(kc + 1) * STW],
                                           start=(kc == 0), stop=(kc == KC - 1))
                    elif gi < 10:
                        oc = gi - 8
                        for kc in range(KC):
                            ins = t.matmul(pb[:],
                                           wkf_s[:, kc * FFKV + oc * 128: kc * FFKV + oc * 128 + 128],
                                           xr[:, kc * STW:(kc + 1) * STW],
                                           start=(kc == 0), stop=(kc == KC - 1))
                    else:
                        tsub = gi - 10
                        for kc in range(KC):
                            ins = t.matmul(pb[:, 0:FFKV],
                                           xr[:, kc * STW + tsub * 128: kc * STW + tsub * 128 + 128],
                                           wvf_s[:, kc * FFKV: (kc + 1) * FFKV],
                                           start=(kc == 0), stop=(kc == KC - 1))
                    S["PMM"].inc(ins, 1)

        @blk.vector
        def _(v):
            for st in range(NST):
                stb = st * STW
                v.wait_ge(S["ROWEV"].h, b0["ROWEV"] + st + 1)
                v.tensor_mul(r_var[:], r_mean[:], r_mean[:])
                v.drain()
                S["ROW1"].inc(v.scalar_tensor_tensor(out=r_var[:], in0=r_msq[:],
                                                     scalar=1e-5, in1=r_var[:],
                                                     op0=ALU.add, op1=ALU.subtract))
                v.wait_ge(S["RSTD"].h, b0["RSTD"] + st * 2 + 1)
                S["MR"].inc(v.tensor_mul(r_mean[:], r_mean[:], r_rstd[:]))
                v.wait_ge(S["BC"].h, b0["BC"] + st * 2 + 1)
                if st >= 1:
                    v.wait_ge(S["CAST"].h, b0["CAST"] + st * 24)
                for kc in range(KC):
                    src = RES[:, kc * SEQ + stb: kc * SEQ + stb + STW]
                    v.tensor_mul(tA[:], src, psA[:])
                    v.drain()
                    S["Y0"].inc(v.tensor_sub(y_st[:, kc * STW:(kc + 1) * STW], tA[:], psB[:]))
                    v.drain()
                v.wait_ge(S["STAT"].h, b0["STAT"] + st * 24 + 24)
                S["ROW1"].inc(v.tensor_scalar(out=r_var[:], in0=psB[0:1, :],
                                              scalar1=1.0 / 1024.0, scalar2=1e-6,
                                              op0=ALU.mult, op1=ALU.add))
                v.wait_ge(S["BC"].h, b0["BC"] + st * 2 + 2)
                v.wait_ge(S["LD"].h, b0["LD"] + 32 * (st + 1))
                if st >= 1:
                    v.wait_ge(S["PMM"].h, b0["PMM"] + st * NPJ)
                for c in range(4):
                    yE = y_st[:, c * STW:(c + 1) * STW]
                    yO = y_st[:, (4 + c) * STW:(5 + c) * STW]
                    v.tensor_mul(zE[:], yE, psA[:])
                    v.tensor_mul(zO[:], yO, psA[:])
                    v.drain()
                    v.tensor_scalar(out=zE[:], in0=zE[:], scalar1=w1n[:, c:c + 1],
                                    scalar2=None, op0=ALU.mult)
                    v.tensor_scalar(out=zO[:], in0=zO[:], scalar1=w1n[:, 4 + c:5 + c],
                                    scalar2=None, op0=ALU.mult)
                    v.drain()
                    cc = cos_s[:, c * STW:(c + 1) * STW]
                    ss = sin_s[:, c * STW:(c + 1) * STW]
                    v.tensor_mul(tA[:], zE[:], cc)
                    v.tensor_mul(tB[:], zO[:], ss)
                    v.tensor_mul(tC[:], zE[:], ss)
                    v.tensor_mul(tD[:], zO[:], cc)
                    v.drain()
                    S["T1"].inc(v.tensor_sub(xr[:, c * STW:(c + 1) * STW], tA[:], tB[:]), 1)
                    S["T1"].inc(v.tensor_add(xr[:, (4 + c) * STW:(5 + c) * STW], tC[:], tD[:]), 1)
                    v.drain()


def phase_B2(nc, S, RES, id_bf, id_f, ones_bf, ones_f, w2n, rb_sb, QZ, Kf, Vf,
             comb_scr, seq, wo_f, rw_d, mD_d, mT_d, tag):
    es = ExitStack()
    sbuf = lambda name, shape, dty=F32: es.enter_context(nc.sbuf_tensor(tag + name, shape, dty))
    psum = lambda name, shape, dty=F32: es.enter_context(nc.psum_tensor(tag + name, shape, dty))
    wof_s = sbuf("wof", [128, KC * D], BF16)
    rw_s = sbuf("rw", [128, KC * NE])
    mD_s = sbuf("mD", [128, 4 * 512], BF16)
    mT_s = sbuf("mT", [128, 4 * 512], BF16)
    Ssb = [sbuf(f"Ss{i}", [128, SEQ]) for i in range(2)]
    STsb = sbuf("STsb", [128, 1024])
    STs = [STsb[:, 0:512], STsb[:, 512:1024]]
    PT2b = sbuf("PT2b", [128, 1024], BF16)
    PT2 = [PT2b[:, 0:512], PT2b[:, 512:1024]]
    c_col = sbuf("ccol", [128, 8])
    c_row = sbuf("crow", [1, 512])
    c_bcs = sbuf("cbcs", [128, 512])
    Ost = sbuf("O", [128, KC * STW], BF16)
    y_st = sbuf("y", [128, KC * STW])
    comb_stage = sbuf("cstg", [NE, 128])
    hb = [sbuf(f"hb{i}", [128, STW], BF16) for i in range(2)]
    xqs = [sbuf(f"xqs{i}", [128, STW], BF16) for i in range(2)]
    t1 = [sbuf(f"t1{i}", [128, STW]) for i in range(2)]
    lg_sb = sbuf("lg", [NE, STW])
    lt_sb = sbuf("lt", [128, 32])
    ctT = sbuf("ctT", [128, 32])
    m1r = sbuf("m1r", [128, 1])
    m2r = sbuf("m2r", [128, 1])
    d12 = sbuf("d12", [128, 1])
    w1r = sbuf("w1r", [128, 1])
    w2r = sbuf("w2r", [128, 1])
    eq1 = sbuf("eq1", [128, 8])
    eq2 = sbuf("eq2", [128, 8])
    l2s = sbuf("l2s", [128, 8])
    r_mean = sbuf("rm", [1, STW])
    r_msq = sbuf("rq", [1, STW])
    r_var = sbuf("rv", [1, STW])
    r_rstd = sbuf("rr", [1, STW])
    r_nmax = [sbuf(f"nm{i}", [128, 1]) for i in range(2)]
    r_sum = [sbuf(f"sum{i}", [128, 1]) for i in range(2)]
    r_lns = [sbuf(f"lns{i}", [128, 1]) for i in range(2)]
    psS = [psum(f"pss{i}", [128, 512]) for i in range(2)]
    STb = [psum(f"stb{i}", [128, 512]) for i in range(2)]
    AVb = [psum(f"avb{i}", [64, 512]) for i in range(2)]
    CBCb = psum("cbcb", [128, 512])
    miscb = psum("miscb", [128, 512])
    ps_crow = miscb[0:1, :]
    ps_r = miscb[0:NE, :]
    ps_lt = miscb[:, 16:24]
    ps_ct = miscb[0:NE, 128:256]

    b0 = {k: s.n for k, s in S.items()}

    sub_list = []
    blk_eng = []
    blk_cnt = []
    blk_pref = []
    acnt = dcnt = 0
    a_thru_sub = []
    for st in range(NST):
        for h in range(NH):
            for qt in range(4):
                gq = st * 4 + qt
                nb = gq // 4 + 1
                sub_list.append((st, h, qt, gq, nb))
                blk_pref.append(len(blk_eng))
                for b in range(nb):
                    if b == nb - 1:
                        blk_eng.append("D"); dcnt += 1; blk_cnt.append(dcnt)
                    else:
                        blk_eng.append("A"); acnt += 1; blk_cnt.append(acnt)
                a_thru_sub.append(acnt)
    blk_pref.append(len(blk_eng))
    NSUB = len(sub_list)
    v_nkt = [4 * (st + 1) for st in range(NST) for h in range(NH)]
    kt_pref = [0]
    for n in v_nkt:
        kt_pref.append(kt_pref[-1] + n)

    def blk_wait(eng, g):
        if g < 0:
            return
        nm = "SEV" if blk_eng[g] == "A" else "SEVD"
        eng.wait_ge(S[nm].h, b0[nm] + blk_cnt[g])

    with es, nc.Block() as blk:
        @blk.gpsimd
        def _(gp):
            S["WL1"].inc(gp.dma_start(out=wof_s[:], in_=wo_f.rearrange("p k d -> p (k d)")), 16)
            for st in range(NST):
                stb = st * STW
                for oc in range(KC):
                    gp.wait_ge(S["H1"].h, b0["H1"] + st * 8 + oc + 1)
                    src2 = RES[:, oc * SEQ + stb: oc * SEQ + stb + STW]
                    S["X1"].inc(gp.tensor_add(src2, src2, y_st[:, oc * STW:(oc + 1) * STW]))
                for oc in range(KC):
                    gp.wait_ge(S["RADD"].h, b0["RADD"] + st * 8 + oc + 1)
                    S["T1"].inc(gp.tensor_scalar(out=t1[oc % 2][:], in0=t1[oc % 2][:],
                                                 scalar1=w2n[:, oc:oc + 1], scalar2=None,
                                                 op0=ALU.mult))

        @blk.sync
        def _(sy):
            S["LD"].inc(sy.dma_start(out=rw_s[:], in_=rw_d.rearrange("p k e -> p (k e)")), 16)
            S["LD"].inc(sy.dma_start(out=mD_s[:].rearrange("p (i c) -> p i c", i=4), in_=mD_d), 16)
            S["LD"].inc(sy.dma_start(out=mT_s[:].rearrange("p (i c) -> p i c", i=4), in_=mT_d), 16)
            for st in range(NST):
                for tl in range(4):
                    sy.wait_ge(S["AUX"].h, b0["AUX"] + st * 4 + tl + 1)
                    S["EO"].inc(sy.dma_start(
                        out=comb_scr[0:NE, seq * SEQ + st * STW + tl * 128:
                                     seq * SEQ + st * STW + tl * 128 + 128],
                        in_=comb_stage[:]), 16)
            sy.wait_ge(S["EO"].h, S["EO"].n)

        @blk.tensor
        def _(t):
            t.wait_ge(S["LD"].h, b0["LD"] + 48)
            t.wait_ge(S["WL1"].h, b0["WL1"] + 16)

            def stats_pe(st, h):
                for qt in range(4):
                    w = (st * 16 + h) * 4 + qt
                    gq = st * 4 + qt
                    nb = gq // 4 + 1
                    kend = (gq + 1) * 128
                    qch, qba = qloc(h)
                    g = h // 4
                    lhs = QZ[qba:qba + 64,
                             qch * SEQ + gq * 128: qch * SEQ + gq * 128 + 128]
                    for b in range(nb):
                        gb = blk_pref[w] + b
                        blk_wait(t, gb - 2)
                        wdt = min(512, kend - b * 512)
                        rhs = Kf[64 * (g % 2):64 * (g % 2) + 64,
                                 (g // 2) * SEQ + b * 512:(g // 2) * SEQ + b * 512 + wdt]
                        S["SMM"].inc(t.matmul(psS[gb % 2][:, 0:wdt], lhs, rhs,
                                              start=True, stop=True), 1)

            def t2_pe(st, h):
                v = st * 16 + h
                qch, qba = qloc(h)
                g = h // 4
                t.wait_ge(S["PN"].h, b0["PN"] + 4 * (v + 1))
                if v >= 1:
                    t.wait_ge(S["CRC"].h, b0["CRC"] + v)
                if h == 0 and st >= 1:
                    t.wait_ge(S["SIL"].h, b0["SIL"] + st)
                    t.wait_ge(S["AUX3"].h, b0["AUX3"] + st * 4)
                    t.wait_ge(S["AUX"].h, b0["AUX"] + st * 4)
                for qt in range(4):
                    ins = t.transpose(ps_crow[0:1, qt * 128:(qt + 1) * 128],
                                      c_col[:, (v % 2) * 4 + qt:(v % 2) * 4 + qt + 1], id_f[:])
                S["CTM"].inc(ins, 1)
                t.wait_ge(S["CRC"].h, b0["CRC"] + v + 1)
                if v >= 1:
                    t.wait_ge(S["CBC2"].h, b0["CBC2"] + v)
                S["CBM"].inc(t.matmul(CBCb[:], ones_f[:], c_row[:], start=True, stop=True), 1)
                nkt = v_nkt[v]
                for kt in range(nkt + 2):
                    if kt < nkt:
                        m = kt_pref[v] + kt
                        if m >= 2:
                            t.wait_ge(S["STEV"].h, b0["STEV"] + m - 1)
                        rhs = QZ[qba:qba + 64, qch * SEQ + st * STW: qch * SEQ + (st + 1) * STW]
                        lhsK = Kf[64 * (g % 2):64 * (g % 2) + 64,
                                  (g // 2) * SEQ + kt * 128:(g // 2) * SEQ + kt * 128 + 128]
                        S["STM"].inc(t.matmul(STb[m % 2][:], lhsK, rhs, start=True, stop=True), 1)
                    if kt >= 2:
                        ka = kt - 2
                        m = kt_pref[v] + ka
                        t.wait_ge(S["PEXP"].h, b0["PEXP"] + m // 2 + 1)
                        if ka == 0:
                            t.wait_ge(S["OEV"].h, b0["OEV"] + max(0, v - 1))
                        S["AVM"].inc(t.matmul(AVb[v % 2][:],
                                              Vf[:, ka * FFKV + 64 * g: ka * FFKV + 64 * g + 64],
                                              PT2[m % 2],
                                              start=(ka == 0), stop=(ka == nkt - 1)), 1)

            for st in range(NST):
                if st >= 1:
                    t.wait_ge(S["H1"].h, b0["H1"] + st * 8)
                    t.wait_ge(S["T1"].h, b0["T1"] + st * 8)
                for h in range(NH):
                    stats_pe(st, h)
                    if h >= 1:
                        t2_pe(st, h - 1)
                t2_pe(st, NH - 1)
                stb = st * STW
                t.wait_ge(S["STEV"].h, b0["STEV"] + kt_pref[st * 16 + 16])
                if st >= 1:
                    t.wait_ge(S["Y0"].h, b0["Y0"] + st * 8)
                for kc in range(KC):
                    t.wait_ge(S["CAST"].h, b0["CAST"] + st * 24 + kc * 2 + 2)
                    i1 = t.matmul(STb[0][0:1, :], ones_bf[:], hb[kc % 2][:],
                                  start=(kc == 0), stop=(kc == KC - 1))
                    i2 = t.matmul(STb[1][0:1, :], ones_bf[:], xqs[kc % 2][:],
                                  start=(kc == 0), stop=(kc == KC - 1))
                    S["STAT"].inc(i1, 1)
                    S["STAT"].inc(i2, 1)
                t.wait_ge(S["MR"].h, b0["MR"] + st + 1)
                i1 = t.matmul(STb[0][:], ones_f[:], r_rstd[:], start=True, stop=True)
                i2 = t.matmul(STb[1][:], ones_f[:], r_mean[:], start=True, stop=True)
                S["BC"].inc(i2, 1)
                t.wait_ge(S["OEV"].h, b0["OEV"] + st * 16 + 16)
                if st >= 1:
                    t.wait_ge(S["H1"].h, b0["H1"] + st * 8)
                for oc in range(KC):
                    pb = psS[oc % 2]
                    if oc >= 2:
                        t.wait_ge(S["H1"].h, b0["H1"] + st * 8 + oc - 1)
                    for kc in range(KC):
                        ins = t.matmul(pb[:], wof_s[:, kc * D + oc * 128: kc * D + oc * 128 + 128],
                                       Ost[:, kc * STW:(kc + 1) * STW],
                                       start=(kc == 0), stop=(kc == KC - 1))
                    S["WMM"].inc(ins, 1)
                t.wait_ge(S["H1"].h, b0["H1"] + st * 8 + 8)
                for kc in range(KC):
                    t.wait_ge(S["CAST"].h, b0["CAST"] + st * 24 + 16 + kc + 1)
                    ins = t.matmul(STb[1][0:1, :], ones_bf[:], xqs[kc % 2][:],
                                   start=(kc == 0), stop=(kc == KC - 1))
                    S["STAT"].inc(ins, 1)
                t.wait_ge(S["RSTD"].h, b0["RSTD"] + st * 2 + 2)
                t.wait_ge(S["Y0"].h, b0["Y0"] + st * 8 + 8)
                S["BC"].inc(t.matmul(STb[0][:], ones_f[:], r_rstd[:], start=True, stop=True), 1)
                if st >= 1:
                    t.wait_ge(S["AUX2"].h, b0["AUX2"] + st)
                for oc in range(KC):
                    t.wait_ge(S["T1"].h, b0["T1"] + st * 8 + oc + 1)
                    ins = t.matmul(ps_r, rw_s[:, oc * NE:(oc + 1) * NE], t1[oc % 2][:],
                                   start=(oc == 0), stop=(oc == KC - 1))
                    S["RMM"].inc(ins, 1)
                t.wait_ge(S["AUX2"].h, b0["AUX2"] + st + 1)
                for tl in range(4):
                    if st * 4 + tl >= 1:
                        t.wait_ge(S["AUX3"].h, b0["AUX3"] + st * 4 + tl)
                        t.wait_ge(S["AUX"].h, b0["AUX"] + st * 4 + tl)
                    S["H2"].inc(t.transpose(ps_lt, lg_sb[:, tl * 128:(tl + 1) * 128], id_f[0:8, 0:8]), 1)
                    t.wait_ge(S["CBMM"].h, b0["CBMM"] + st * 4 + tl + 1)
                    S["EPI"].inc(t.transpose(ps_ct, ctT[:, tl * 8:(tl + 1) * 8], id_f[:]), 1)

        @blk.scalar
        def _(sc):
            def stats_act(st, h):
                for qt in range(4):
                    w = (st * 16 + h) * 4 + qt
                    gq = st * 4 + qt
                    nb = gq // 4 + 1
                    kend = (gq + 1) * 128
                    for b in range(nb - 1):
                        gb = blk_pref[w] + b
                        sc.wait_ge(S["SMM"].h, b0["SMM"] + gb + 1)
                        if w >= 2:
                            sc.wait_ge(S["P"].h, b0["P"] + w - 1)
                        S["SEV"].inc(sc.activation(out=Ssb[w % 2][:, b * 512:(b + 1) * 512],
                                                   in_=psS[gb % 2][:], func=AF.Copy))
                    sc.wait_ge(S["NM"].h, b0["NM"] + w + 1)
                    sc.activation(out=Ssb[w % 2][:, 0:kend], in_=Ssb[w % 2][:, 0:kend],
                                  func=AF.Exp, bias=r_nmax[w % 2][:], accum_out=r_sum[w % 2][:])
                    sc.drain()
                    S["P"].inc(sc.activation(out=r_lns[w % 2][:], in_=r_sum[w % 2][:], func=AF.Ln))

            def t2_act(st, h):
                v = st * 16 + h
                sc.wait_ge(S["CTM"].h, b0["CTM"] + v + 1)
                S["CRC"].inc(sc.activation(out=c_row[:], in_=ps_crow, func=AF.Copy))
                sc.wait_ge(S["CBM"].h, b0["CBM"] + v + 1)
                if v >= 1:
                    sc.wait_ge(S["STEV"].h, b0["STEV"] + kt_pref[v])
                S["CBC2"].inc(sc.activation(out=c_bcs[:], in_=CBCb[:], func=AF.Copy))
                nkt = v_nkt[v]
                for kt in range(0, nkt, 2):
                    m = kt_pref[v] + kt
                    sc.wait_ge(S["STEV"].h, b0["STEV"] + m + 2)
                    if m >= 2:
                        sc.wait_ge(S["AVM"].h, b0["AVM"] + m)
                    S["PEXP"].inc(sc.activation(out=PT2b[:], in_=STsb[:], func=AF.Exp))
                sc.wait_ge(S["AVM"].h, b0["AVM"] + kt_pref[v + 1])
                dst = Ost[64 * (h % 2):64 * (h % 2) + 64, (h // 2) * STW:(h // 2 + 1) * STW]
                S["OEV"].inc(sc.activation(out=dst, in_=AVb[v % 2][:], func=AF.Copy))

            for st in range(NST):
                for h in range(NH):
                    stats_act(st, h)
                    if h >= 1:
                        t2_act(st, h - 1)
                t2_act(st, NH - 1)
                stb = st * STW
                if st >= 1:
                    sc.wait_ge(S["STAT"].h, b0["STAT"] + st * 24)
                for kc in range(KC):
                    if kc >= 2:
                        sc.wait_ge(S["STAT"].h, b0["STAT"] + st * 24 + (kc - 2) * 2 + 2)
                    src2 = RES[:, kc * SEQ + stb: kc * SEQ + stb + STW]
                    S["CAST"].inc(sc.activation(out=hb[kc % 2][:], in_=src2, func=AF.Copy))
                    S["CAST"].inc(sc.activation(out=xqs[kc % 2][:], in_=src2, func=AF.Square))
                sc.wait_ge(S["STAT"].h, b0["STAT"] + st * 24 + 16)
                sc.activation(out=r_mean[:], in_=STb[0][0:1, :], func=AF.Copy, scale=1.0 / 1024.0)
                S["ROWEV"].inc(sc.activation(out=r_msq[:], in_=STb[1][0:1, :], func=AF.Copy, scale=1.0 / 1024.0))
                sc.wait_ge(S["ROW1"].h, b0["ROW1"] + st * 2 + 1)
                sc.activation(out=r_var[:], in_=r_var[:], func=AF.Ln)
                sc.drain()
                S["RSTD"].inc(sc.activation(out=r_rstd[:], in_=r_var[:], func=AF.Exp, scale=-0.5))
                for kc in range(KC):
                    sc.wait_ge(S["H1"].h, b0["H1"] + st * 8 + kc + 1)
                    if kc >= 2:
                        sc.wait_ge(S["STAT"].h, b0["STAT"] + st * 24 + 16 + (kc - 2) + 1)
                    S["CAST"].inc(sc.activation(out=xqs[kc % 2][:],
                                                in_=y_st[:, kc * STW:(kc + 1) * STW],
                                                func=AF.Square))
                sc.wait_ge(S["ROW1"].h, b0["ROW1"] + st * 2 + 2)
                sc.activation(out=r_var[:], in_=r_var[:], func=AF.Ln)
                sc.drain()
                S["RSTD"].inc(sc.activation(out=r_rstd[:], in_=r_var[:], func=AF.Exp, scale=-0.5))
                for oc in range(KC):
                    sc.wait_ge(S["T1"].h, b0["T1"] + st * 8 + oc + 1)
                    S["ZC"].inc(sc.activation(out=QZ[:, oc * SEQ + stb: oc * SEQ + stb + STW],
                                              in_=t1[oc % 2][:], func=AF.Copy))
                sc.wait_ge(S["RMM"].h, b0["RMM"] + st * 8 + 8)
                S["SIL"].inc(sc.activation(out=lg_sb[:], in_=ps_r, func=AF.Copy))
                if st >= 1:
                    sc.wait_ge(S["CBMM"].h, b0["CBMM"] + st * 4)
                for tl in range(4):
                    sc.wait_ge(S["H2"].h, b0["H2"] + st * 4 + tl + 1)
                    S["AUX3"].inc(sc.activation(out=lt_sb[:, tl * 8:(tl + 1) * 8], in_=ps_lt, func=AF.Copy))
                    sc.wait_ge(S["HST"].h, b0["HST"] + st * 4 + tl + 1)
                    S["CBEV"].inc(sc.activation(out=w1r[:], in_=d12[:], func=AF.Sigmoid))
                    sc.wait_ge(S["EPI"].h, b0["EPI"] + st * 4 + tl + 1)
                    if st * 4 + tl >= 1:
                        sc.wait_ge(S["EO"].h, b0["EO"] + 16 * (st * 4 + tl))
                    S["AUX"].inc(sc.activation(out=comb_stage[:], in_=ps_ct, func=AF.Copy))

        @blk.vector
        def _(v):
            v.wait_ge(S["LD"].h, b0["LD"] + 48)

            def c_write(w2):
                vv, qt2 = w2 // 4, w2 % 4
                v.wait_ge(S["P"].h, b0["P"] + w2 + 1)
                if qt2 == 0 and vv >= 2:
                    v.wait_ge(S["CTM"].h, b0["CTM"] + vv - 1)
                S["PN"].inc(v.tensor_sub(c_col[:, (vv % 2) * 4 + qt2:(vv % 2) * 4 + qt2 + 1],
                                         r_nmax[w2 % 2][:], r_lns[w2 % 2][:]))

            def stats_dve(st, h):
                for qt in range(4):
                    w = (st * 16 + h) * 4 + qt
                    gq = st * 4 + qt
                    nb = gq // 4 + 1
                    kend = (gq + 1) * 128
                    dlt = gq % 4
                    lw = kend - (nb - 1) * 512
                    gb = blk_pref[w] + nb - 1
                    v.wait_ge(S["SMM"].h, b0["SMM"] + gb + 1)
                    if w >= 2:
                        v.wait_ge(S["P"].h, b0["P"] + w - 1)
                    S["SEVD"].inc(v.tensor_add(Ssb[w % 2][:, (nb - 1) * 512:(nb - 1) * 512 + lw],
                                               psS[gb % 2][:, 0:lw],
                                               mD_s[:, dlt * 512: dlt * 512 + lw]))
                    v.drain()
                    v.wait_ge(S["SEV"].h, b0["SEV"] + a_thru_sub[w])
                    S["NM"].inc(v.reduce_max(r_nmax[w % 2][:], Ssb[w % 2][:, 0:kend],
                                             axis=AX, negate=True))
                    if w >= 1:
                        c_write(w - 1)

            def t2_dve(st, h):
                v2 = st * 16 + h
                nkt = v_nkt[v2]
                for kt in range(nkt):
                    m = kt_pref[v2] + kt
                    v.wait_ge(S["STM"].h, b0["STM"] + m + 1)
                    v.wait_ge(S["CBC2"].h, b0["CBC2"] + v2 + 1)
                    if m >= 2:
                        v.wait_ge(S["PEXP"].h, b0["PEXP"] + (m - 2) // 2 + 1)
                    dlt = kt - 4 * st
                    if dlt >= 0:
                        v.tensor_add(STs[m % 2], STb[m % 2][:], c_bcs[:])
                        v.drain()
                        ins = v.tensor_add(STs[m % 2], STs[m % 2],
                                           mT_s[:, dlt * 512:(dlt + 1) * 512])
                    else:
                        ins = v.tensor_add(STs[m % 2], STb[m % 2][:], c_bcs[:])
                    S["STEV"].inc(ins)

            for st in range(NST):
                for h in range(NH):
                    stats_dve(st, h)
                    if h == NH - 1:
                        c_write((st * 16 + h) * 4 + 3)
                    if h >= 1:
                        t2_dve(st, h - 1)
                t2_dve(st, NH - 1)
                stb = st * STW
                v.wait_ge(S["ROWEV"].h, b0["ROWEV"] + st + 1)
                v.tensor_mul(r_var[:], r_mean[:], r_mean[:])
                v.drain()
                S["ROW1"].inc(v.scalar_tensor_tensor(out=r_var[:], in0=r_msq[:],
                                                     scalar=1e-5, in1=r_var[:],
                                                     op0=ALU.add, op1=ALU.subtract))
                v.wait_ge(S["RSTD"].h, b0["RSTD"] + st * 2 + 1)
                S["MR"].inc(v.tensor_mul(r_mean[:], r_mean[:], r_rstd[:]))
                v.wait_ge(S["BC"].h, b0["BC"] + st * 2 + 1)
                if st >= 1:
                    v.wait_ge(S["CAST"].h, b0["CAST"] + st * 24)
                    v.wait_ge(S["ZC"].h, b0["ZC"] + st * 8)
                    v.wait_ge(S["RMM"].h, b0["RMM"] + st * 8)
                for oc2 in range(0, KC, 2):
                    for oc in (oc2, oc2 + 1):
                        src2 = RES[:, oc * SEQ + stb: oc * SEQ + stb + STW]
                        v.tensor_mul(t1[oc % 2][:], src2, STb[0][:])
                    v.drain()
                    for oc in (oc2, oc2 + 1):
                        ych = y_st[:, oc * STW:(oc + 1) * STW]
                        S["Y0"].inc(v.tensor_sub(ych, t1[oc % 2][:], STb[1][:]))
                    v.drain()
                    for oc in (oc2, oc2 + 1):
                        ych = y_st[:, oc * STW:(oc + 1) * STW]
                        v.wait_ge(S["WMM"].h, b0["WMM"] + st * 8 + oc + 1)
                        S["H1"].inc(v.tensor_add(ych, ych, psS[oc % 2][:]))
                    v.drain()
                v.wait_ge(S["STAT"].h, b0["STAT"] + st * 24 + 24)
                S["ROW1"].inc(v.tensor_scalar(out=r_var[:], in0=STb[1][0:1, :],
                                              scalar1=1.0 / 1024.0, scalar2=1e-6,
                                              op0=ALU.mult, op1=ALU.add))
                v.wait_ge(S["BC"].h, b0["BC"] + st * 2 + 2)
                for oc in range(KC):
                    if oc >= 2:
                        v.wait_ge(S["RMM"].h, b0["RMM"] + st * 8 + oc - 1)
                        v.wait_ge(S["ZC"].h, b0["ZC"] + st * 8 + oc - 1)
                    ych = y_st[:, oc * STW:(oc + 1) * STW]
                    S["RADD"].inc(v.tensor_mul(t1[oc % 2][:], ych, STb[0][:]))
                v.wait_ge(S["SIL"].h, b0["SIL"] + st + 1)
                S["AUX2"].inc(v.tensor_scalar(out=lg_sb[:], in0=lg_sb[:],
                                              scalar1=rb_sb[0:NE, 0:1], scalar2=None,
                                              op0=ALU.add))
                for tl in range(4):
                    v.wait_ge(S["AUX3"].h, b0["AUX3"] + st * 4 + tl + 1)
                    if st * 4 + tl >= 4:
                        v.wait_ge(S["EPI"].h, b0["EPI"] + st * 4 + tl - 3)
                    lg = lt_sb[:, tl * 8:(tl + 1) * 8]
                    v.reduce_max(m1r[:], lg, axis=AX)
                    v.drain()
                    v.tensor_scalar(out=eq1[:], in0=lg, scalar1=m1r[:], scalar2=None, op0=ALU.is_ge)
                    v.drain()
                    v.scalar_tensor_tensor(out=l2s[:], in0=eq1[:], scalar=-1e30, in1=lg,
                                           op0=ALU.mult, op1=ALU.add)
                    v.drain()
                    v.reduce_max(m2r[:], l2s[:], axis=AX)
                    v.drain()
                    v.tensor_scalar(out=eq2[:], in0=l2s[:], scalar1=m2r[:], scalar2=None, op0=ALU.is_ge)
                    S["HST"].inc(v.tensor_sub(d12[:], m1r[:], m2r[:]))
                    v.wait_ge(S["CBEV"].h, b0["CBEV"] + st * 4 + tl + 1)
                    v.tensor_scalar(out=w2r[:], in0=w1r[:], scalar1=-1.0, scalar2=1.0,
                                    op0=ALU.mult, op1=ALU.add)
                    v.drain()
                    v.tensor_scalar(out=eq1[:], in0=eq1[:], scalar1=w1r[:], scalar2=None, op0=ALU.mult)
                    v.tensor_scalar(out=eq2[:], in0=eq2[:], scalar1=w2r[:], scalar2=None, op0=ALU.mult)
                    v.drain()
                    S["CBMM"].inc(v.tensor_add(ctT[:, tl * 8:(tl + 1) * 8], eq1[:], eq2[:]))


def phase_C(nc, S, RES, ones_f, QZ, comb_scr_h, w1m, w2m, w3m,
            b1t_d, b2t_d, b3t_d, seq, tag):
    es = ExitStack()
    sbuf = lambda name, shape, dty=F32: es.enter_context(nc.sbuf_tensor(tag + name, shape, dty))
    psum = lambda name, shape, dty=F32: es.enter_context(nc.psum_tensor(tag + name, shape, dty))
    w1b = [sbuf(f"w1b{i}", [128, 2 * KC * FBW], BF16) for i in range(2)]
    w2b = [sbuf(f"w2b{i}", [128, 2 * KC * FBW], BF16) for i in range(2)]
    w3b = sbuf("w3b", [128, 8 * D], BF16)
    hpair = sbuf("hp", [128, 8 * FBW], BF16)
    s1 = [sbuf(f"s1{i}", [128, FBW]) for i in range(2)]
    tmp = [sbuf(f"tmp{i}", [128, FBW]) for i in range(2)]
    comb_bc = [sbuf(f"cbc{i}", [128, FBW]) for i in range(2)]
    b1t_s = sbuf("b1t", [128, NE * 32])
    b2t_s = sbuf("b2t", [128, NE * 32])
    b3t_s = sbuf("b3t", [128, NE * KC])
    psh1 = [psum(f"ph1{i}", [128, 512]) for i in range(2)]
    psh2 = [psum(f"ph2{i}", [128, 512]) for i in range(2)]
    pseo = [psum(f"peo{i}", [128, 512]) for i in range(4)]

    b0 = {k: s.n for k, s in S.items()}
    NU = 128

    with es, nc.Block() as blk:
        @blk.gpsimd
        def _(gp):
            for d, s_ in ((b1t_d, b1t_s), (b2t_d, b2t_s), (b3t_d, b3t_s)):
                S["MSK"].inc(gp.dma_start(out=s_[:], in_=d.rearrange("p a b -> p (a b)")), 16)
            for q in range(33):
                if q < 32:
                    e, p = q // 4, q % 4
                    if q >= 2:
                        gp.wait_ge(S["PAIRD"].h, b0["PAIRD"] + q - 1)
                    s1sem = S["WL1"] if q % 2 == 0 else S["W1B"]
                    s2sem = S["WL2"] if q % 2 == 0 else S["W2B"]
                    for s_ in range(2):
                        fb = 2 * p + s_
                        s1sem.inc(gp.dma_start(
                            out=w1b[q % 2][:, s_ * KC * FBW:(s_ + 1) * KC * FBW].rearrange(
                                "p (k c) -> p k c", k=KC),
                            in_=w1m[e, fb]), 16)
                    for s_ in range(2):
                        fb = 2 * p + s_
                        s2sem.inc(gp.dma_start(
                            out=w2b[q % 2][:, s_ * KC * FBW:(s_ + 1) * KC * FBW].rearrange(
                                "p (k c) -> p k c", k=KC),
                            in_=w2m[e, fb]), 16)
                if q >= 1:
                    qq = q - 1
                    for u in range(qq * 4, qq * 4 + 4):
                        e_, p_, tt = u // 16, (u // 4) % 4, u % 4
                        for gidx in range(8):
                            dc = (gidx % 4) + 4 * (gidx // 4)
                            dcv = [0, 1, 2, 3, 4, 5, 6, 7][gidx]
                            gp.wait_ge(S["EPI"].h, b0["EPI"] + 8 * u + gidx + 1)
                            dst = RES[:, dcv * SEQ + tt * FBW: dcv * SEQ + tt * FBW + FBW]
                            S["RADD"].inc(gp.tensor_add(dst, dst, tmp[gidx % 2][:]))
                if q < 32:
                    if q >= 1:
                        gp.wait_ge(S["EO"].h, b0["EO"] + 32 * q)
                    e, p = q // 4, q % 4
                    S["WL3"].inc(gp.dma_start(
                        out=w3b[:].rearrange("p (f k d) -> p f k d", f=2, k=4),
                        in_=w3m[e, 2 * p:2 * p + 2].rearrange("f p k d -> p f k d")), 16)

        @blk.sync
        def _(sy):
            for u in range(NU):
                e, tt = u // 16, u % 4
                if u >= 2:
                    sy.wait_ge(S["EPI"].h, b0["EPI"] + 8 * (u - 1))
                csem = S["CBROW"] if u % 2 == 0 else S["CROWB"]
                bcast = bass.AP(comb_scr_h, e * 2 * SEQ + seq * SEQ + tt * FBW,
                                [[0, 128], [1, FBW]])
                csem.inc(sy.dma_start(out=comb_bc[u % 2][:], in_=bcast), 16)

        @blk.tensor
        def _(t):
            t.wait_ge(S["MSK"].h, b0["MSK"] + 48)
            for u in range(NU):
                e, p, tt = u // 16, (u // 4) % 4, u % 4
                q = u // 4
                if tt == 0:
                    s1sem = "WL1" if q % 2 == 0 else "W1B"
                    s2sem = "WL2" if q % 2 == 0 else "W2B"
                    t.wait_ge(S[s1sem].h, b0[s1sem] + 32 * (q // 2 + 1))
                    t.wait_ge(S[s2sem].h, b0[s2sem] + 32 * (q // 2 + 1))
                for s_ in range(2):
                    for fc in range(4):
                        j = 8 * u + 4 * s_ + fc
                        if j >= 2:
                            t.wait_ge(S["SIL"].h, b0["SIL"] + j - 1)
                        for kc in range(KC):
                            ins = t.matmul(psh1[j % 2][:],
                                           w1b[q % 2][:, (s_ * KC + kc) * FBW + fc * 128:
                                                       (s_ * KC + kc) * FBW + fc * 128 + 128],
                                           QZ[:, kc * SEQ + tt * FBW: kc * SEQ + tt * FBW + FBW],
                                           start=(kc == 0), stop=(kc == KC - 1))
                        S["H1"].inc(ins, 1)
                        if j >= 2:
                            t.wait_ge(S["HST"].h, b0["HST"] + j - 1)
                        for kc in range(KC):
                            ins = t.matmul(psh2[j % 2][:],
                                           w2b[q % 2][:, (s_ * KC + kc) * FBW + fc * 128:
                                                       (s_ * KC + kc) * FBW + fc * 128 + 128],
                                           QZ[:, kc * SEQ + tt * FBW: kc * SEQ + tt * FBW + FBW],
                                           start=(kc == 0), stop=(kc == KC - 1))
                        S["H2"].inc(ins, 1)
                if tt == 3:
                    S["PAIRD"].inc(t.nop(), 1)
                t.wait_ge(S["HST"].h, b0["HST"] + 8 * (u + 1))
                t.wait_ge(S["WL3"].h, b0["WL3"] + 16 * (q + 1))
                for r in range(2):
                    t.wait_ge(S["EPI"].h, b0["EPI"] + 8 * u + 4 * r)
                    for dci in range(4):
                        dc = 4 * r + dci
                        for ks in range(8):
                            ins = t.matmul(pseo[dci][:],
                                           w3b[:, ks * D + dc * 128: ks * D + dc * 128 + 128],
                                           hpair[:, ks * FBW:(ks + 1) * FBW],
                                           start=(ks == 0), stop=(ks == KC - 1))
                        S["EO"].inc(ins, 1)

        @blk.scalar
        def _(sc):
            sc.wait_ge(S["MSK"].h, b0["MSK"] + 48)
            for u in range(NU):
                e, p, tt = u // 16, (u // 4) % 4, u % 4
                for s_ in range(2):
                    for fc in range(4):
                        j = 8 * u + 4 * s_ + fc
                        fbch = (2 * p + s_) * 4 + fc
                        sc.wait_ge(S["H1"].h, b0["H1"] + j + 1)
                        if j >= 2:
                            sc.wait_ge(S["HST"].h, b0["HST"] + j - 1)
                        S["SIL"].inc(sc.activation(out=s1[j % 2][:], in_=psh1[j % 2][:],
                                                   func=AF.Silu,
                                                   bias=b1t_s[:, e * 32 + fbch: e * 32 + fbch + 1]))

        @blk.vector
        def _(v):
            v.wait_ge(S["MSK"].h, b0["MSK"] + 48)
            for u in range(NU):
                e, p, tt = u // 16, (u // 4) % 4, u % 4
                for s_ in range(2):
                    for fc in range(4):
                        j = 8 * u + 4 * s_ + fc
                        fbch = (2 * p + s_) * 4 + fc
                        v.wait_ge(S["H2"].h, b0["H2"] + j + 1)
                        v.wait_ge(S["SIL"].h, b0["SIL"] + j + 1)
                        S["HST"].inc(v.scalar_tensor_tensor(
                            out=hpair[:, (s_ * 4 + fc) * FBW:(s_ * 4 + fc + 1) * FBW],
                            in0=psh2[j % 2][:],
                            scalar=b2t_s[:, e * 32 + fbch: e * 32 + fbch + 1],
                            in1=s1[j % 2][:], op0=ALU.add, op1=ALU.mult))
                for r in range(2):
                    for dci in range(4):
                        dc = 4 * r + dci
                        gidx = 8 * u + 4 * r + dci
                        v.wait_ge(S["EO"].h, b0["EO"] + gidx + 1)
                        if r == 0 and dci == 0:
                            csem = "CBROW" if u % 2 == 0 else "CROWB"
                            v.wait_ge(S[csem].h, b0[csem] + 16 * (u // 2 + 1))
                        if gidx >= 2:
                            v.wait_ge(S["RADD"].h, b0["RADD"] + gidx - 1)
                        if p == 0:
                            ins = v.scalar_tensor_tensor(
                                out=tmp[gidx % 2][:], in0=pseo[dci][:],
                                scalar=b3t_s[:, e * KC + dc: e * KC + dc + 1],
                                in1=comb_bc[u % 2][:], op0=ALU.add, op1=ALU.mult)
                        else:
                            ins = v.tensor_mul(tmp[gidx % 2][:], pseo[dci][:], comb_bc[u % 2][:])
                        S["EPI"].inc(ins)


def _tile_w(w):
    n = w.shape[1]
    return np.ascontiguousarray(w.reshape(KC, 128, n).transpose(1, 0, 2))


def kernel(L0_attn_hidden, L0_attn_Wq, L0_attn_Wk, L0_attn_Wv, L0_attn_Wo,
           L0_ffn_norm1_w, L0_ffn_Wq, L0_ffn_Wk, L0_ffn_Wv, L0_ffn_Wo,
           L0_ffn_norm2_w, L0_ffn_router_w, L0_ffn_router_b, L0_ffn_W1,
           L0_ffn_b1, L0_ffn_W2, L0_ffn_b2, L0_ffn_W3, L0_ffn_b3,
           _stop=None):
    sig = np.concatenate([np.arange(0, D, 2), np.arange(1, D, 2)])
    sinv = np.argsort(sig)

    x = np.asarray(L0_attn_hidden, dtype=np.float32)
    f32 = lambda a: np.asarray(a, dtype=np.float32)

    shared = {}
    shared["wq_o"] = _tile_w(np.ascontiguousarray(f32(L0_attn_Wq).T[sig]))
    shared["wk_o"] = _tile_w(np.ascontiguousarray(f32(L0_attn_Wk).T[sig]))
    shared["wv_o"] = _tile_w(np.ascontiguousarray(f32(L0_attn_Wv).T[sig]))
    shared["wo_o"] = _tile_w(np.ascontiguousarray(f32(L0_attn_Wo).T[:, sig]))
    qperm = np.empty(D, dtype=np.int64)
    for h_ in range(16):
        chn = (h_ % 4) + 4 * (h_ // 8)
        bas = 64 * ((h_ // 4) % 2)
        qperm[chn * 128 + bas: chn * 128 + bas + 64] = np.arange(64 * h_, 64 * h_ + 64)
    shared["wq_f"] = _tile_w(np.ascontiguousarray(f32(L0_ffn_Wq)[:, qperm]))
    shared["wk_f"] = _tile_w(f32(L0_ffn_Wk))
    shared["wv_f"] = _tile_w(f32(L0_ffn_Wv))
    shared["wo_f"] = _tile_w(np.ascontiguousarray(f32(L0_ffn_Wo)[:, sig]))
    shared["w1n"] = np.ascontiguousarray(f32(L0_ffn_norm1_w)[sig].reshape(KC, 128).T)
    shared["w2n"] = np.ascontiguousarray(f32(L0_ffn_norm2_w)[sig].reshape(KC, 128).T)
    shared["rw"] = _tile_w(np.ascontiguousarray(f32(L0_ffn_router_w)[sig]))
    shared["rb"] = np.ascontiguousarray(f32(L0_ffn_router_b).reshape(NE, 1))
    w1p = f32(L0_ffn_W1)[:, sig, :]
    w2p = f32(L0_ffn_W2)[:, sig, :]
    w3p = f32(L0_ffn_W3)[:, :, sig]
    shared["w1m"] = np.ascontiguousarray(
        w1p.reshape(NE, KC, 128, NFB, FBW).transpose(0, 3, 2, 1, 4))
    shared["w2m"] = np.ascontiguousarray(
        w2p.reshape(NE, KC, 128, NFB, FBW).transpose(0, 3, 2, 1, 4))
    shared["w3m"] = np.ascontiguousarray(
        w3p.reshape(NE, NFB, 4, 128, D).transpose(0, 1, 3, 2, 4))
    shared["b1t"] = np.ascontiguousarray(
        f32(L0_ffn_b1).reshape(NE, 32, 128).transpose(2, 0, 1))
    shared["b2t"] = np.ascontiguousarray(
        f32(L0_ffn_b2).reshape(NE, 32, 128).transpose(2, 0, 1))
    shared["b3t"] = np.ascontiguousarray(
        f32(L0_ffn_b3)[:, sig].reshape(NE, KC, 128).transpose(2, 0, 1))
    half = D // 2
    theta = (1.0 / (np.float32(10000.0) **
                    (np.arange(half, dtype=np.float32) / np.float32(half)))).astype(np.float32)
    pos = np.arange(SEQ, dtype=np.float32)[:, None]
    ang = (pos * theta[None, :]).astype(np.float32)
    cos = np.cos(ang).astype(np.float32).T
    sin = np.sin(ang).astype(np.float32).T
    shared["cos_t"] = np.ascontiguousarray(cos.reshape(4, 128, SEQ).transpose(1, 0, 2))
    shared["sin_t"] = np.ascontiguousarray(sin.reshape(4, 128, SEQ).transpose(1, 0, 2))
    shared["id_f"] = np.eye(128, dtype=np.float32)
    shared["ones_bf"] = np.ones((128, 1), dtype=ml_dtypes.bfloat16)
    shared["ones_f"] = np.ones((1, 128), dtype=np.float32)
    p = np.arange(128)[:, None]
    c0 = np.arange(128)[None, :]
    cD = np.arange(512)[None, :]
    mD = np.stack([np.where(cD <= dlt + p, 0.0, -1e9).astype(ml_dtypes.bfloat16)
                   for dlt in (0, 128, 256, 384)], axis=1)
    shared["mD"] = np.ascontiguousarray(mD)
    mT = np.stack([np.where(cD >= dlt + p, 0.0, -1e9).astype(ml_dtypes.bfloat16)
                   for dlt in (0, 128, 256, 384)], axis=1)
    shared["mT"] = np.ascontiguousarray(mT)
    mA = np.stack([np.where((cD >= dlt + p) & (cD <= dlt + p + 255), 0.0, -1e9).astype(ml_dtypes.bfloat16)
                   for dlt in (-256, -128, 0, 128, 256, 384)], axis=1)
    shared["mA"] = np.ascontiguousarray(mA)

    in_maps = []
    for c in range(NCORES):
        xc = x[2 * c:2 * c + 2]
        xc = xc.transpose(2, 0, 1).reshape(D, 2 * SEQ)[sig]
        xc = np.ascontiguousarray(xc.reshape(KC, 128, 2 * SEQ).transpose(1, 0, 2))
        m = dict(shared)
        m["x"] = xc
        in_maps.append(m)

    nc = build_program(stop_phase=_stop)
    res = run_bass_kernel_spmd(nc, in_maps, core_ids=list(range(NCORES)))

    out = np.empty((16, SEQ, D), dtype=np.float32)
    for c in range(NCORES):
        oc = res.results[c]["out"]
        oc = oc.transpose(1, 0, 2).reshape(D, 2, SEQ)
        out[2 * c:2 * c + 2] = oc[sinv].transpose(1, 2, 0)
    return out


# version 09
# speedup vs baseline: 1.0728x; 1.0728x over previous
import numpy as np
import ml_dtypes
import concourse.bass as bass
import concourse.mybir as mybir
from concourse.bass_utils import run_bass_kernel_spmd
from contextlib import ExitStack

F32 = mybir.dt.float32
BF16 = mybir.dt.bfloat16
AF = mybir.ActivationFunctionType
ALU = mybir.AluOpType
AX = mybir.AxisListType.X

NCORES = 8
SEQ = 2048
D = 1024
KC = 8
NH = 16
DK = 64
WIN = 256
NST = 4
STW = 512
NQT = 16
FFKV = 256
NE = 8
DFF = 4096
NFB = 8
FBW = 512


class SemT:
    __slots__ = ("h", "n")

    def __init__(self, h):
        self.h = h
        self.n = 0

    def inc(self, instr, k=1):
        instr.then_inc(self.h, k)
        self.n += k
        return instr


def a_nch(Q):
    return min(Q + 1, 3)


def qloc(h):
    return (h % 4) + 4 * (h // 8), 64 * ((h // 4) % 2)


def build_program(stop_phase=None, nseqs=2):
    nc = bass.Bass("TRN2", target_bir_lowering=False)

    dt = lambda name, shape, dty=F32, kind="ExternalInput": nc.dram_tensor(name, shape, dty, kind=kind).ap()
    x_d = dt("x", [128, KC, 2 * SEQ])
    out_d = dt("out", [128, KC, 2 * SEQ], F32, "ExternalOutput")
    wq_o = dt("wq_o", [128, KC, D])
    wk_o = dt("wk_o", [128, KC, D])
    wv_o = dt("wv_o", [128, KC, D])
    wo_o = dt("wo_o", [128, KC, D])
    wq_f = dt("wq_f", [128, KC, D])
    wk_f = dt("wk_f", [128, KC, FFKV])
    wv_f = dt("wv_f", [128, KC, FFKV])
    wo_f = dt("wo_f", [128, KC, D])
    w1n_d = dt("w1n", [128, KC])
    w2n_d = dt("w2n", [128, KC])
    rw_d = dt("rw", [128, KC, NE])
    rb_d = dt("rb", [NE, 1])
    w1m = dt("w1m", [NE, NFB, 128, KC, FBW])
    w2m = dt("w2m", [NE, NFB, 128, KC, FBW])
    w3m = dt("w3m", [NE, NFB, 128, 4, D])
    b1t_d = dt("b1t", [128, NE, 32])
    b2t_d = dt("b2t", [128, NE, 32])
    b3t_d = dt("b3t", [128, NE, KC])
    cos_d = dt("cos_t", [128, 4, SEQ])
    sin_d = dt("sin_t", [128, 4, SEQ])
    id_f_d = dt("id_f", [128, 128])
    ones_bf_d = dt("ones_bf", [128, 1], BF16)
    ones_f_d = dt("ones_f", [1, 128])
    comb_scr_h = nc.dram_tensor("comb_scr", [NE, 2 * SEQ], F32, kind="Internal")
    comb_scr = comb_scr_h.ap()
    mD_d = dt("mD", [128, 4, 512], BF16)
    mT_d = dt("mT", [128, 4, 512], BF16)
    mA_d = dt("mA", [128, 6, 512], BF16)

    g = ExitStack()
    sbuf = lambda name, shape, dty=F32: g.enter_context(nc.sbuf_tensor(name, shape, dty))
    RES = sbuf("RES", [128, KC * SEQ])
    id_bf = None
    id_f = sbuf("id_f_s", [128, 128])
    ones_bf = sbuf("ones_bf_s", [128, 1], BF16)
    ones_f = sbuf("ones_f_s", [1, 128])
    w1n = sbuf("w1n_s", [128, KC])
    w2n = sbuf("w2n_s", [128, KC])
    rb_sb = sbuf("rb_s", [NE, 1])

    sems = {}
    for nm in ["LD", "CAST", "STAT", "ROWEV", "ROW1", "RSTD", "MR", "BC", "Y0",
               "PMM", "PEV", "SMM", "SEV", "MSK", "NM", "P", "PN", "PTMM",
               "PTEV", "AVMM", "OEV", "WMM", "X1", "WL1", "WL2", "WL3",
               "H1", "H2", "EO", "SIL", "HST", "EPI", "CBROW", "CBMM", "CBEV",
               "PAIRD", "STO", "AUX", "AUX2", "AUX3", "RADD", "T1", "ZC", "RMM",
               "W1B", "W2B", "CROWB", "SEVD", "CTM", "CRC", "CBM", "CBC2",
               "STM", "STEV", "PEXP", "AVM"]:
        sems[nm] = SemT(g.enter_context(nc.semaphore(nm)))

    S = sems

    def base():
        return {k: s.n for k, s in sems.items()}

    def store_block(seq):
        with nc.Block() as blk:
            @blk.sync
            def _(sy):
                S["STO"].inc(
                    sy.dma_start(out=out_d[:, :, seq * SEQ:(seq + 1) * SEQ],
                                 in_=RES[:].rearrange("p (k t) -> p k t", k=KC)), 16)

    with g:
        for seq in range(nseqs):
            tok0 = seq * SEQ

            with nc.Block() as blk:
                @blk.sync
                def _(sy, tok0=tok0, seq=seq):
                    if seq > 0:
                        sy.wait_ge(S["STO"].h, S["STO"].n)
                    S["LD"].inc(sy.dma_start(out=RES[:].rearrange("p (k t) -> p k t", k=KC),
                                             in_=x_d[:, :, tok0:tok0 + SEQ]), 16)
                    if seq == 0:
                        for dst, srcd in ((id_f, id_f_d),
                                          (ones_bf, ones_bf_d), (ones_f, ones_f_d),
                                          (w1n, w1n_d), (w2n, w2n_d), (rb_sb, rb_d)):
                            S["LD"].inc(sy.dma_start(out=dst[:], in_=srcd), 16)
                    sy.wait_ge(S["LD"].h, S["LD"].n)

            phase_A(nc, S, RES, id_bf, id_f, ones_bf, ones_f,
                    wq_o, wk_o, wv_o, wo_o, mA_d, tag=f"s{seq}")

            if stop_phase == "A":
                store_block(seq)
                continue

            seq_es = ExitStack()
            QZ = seq_es.enter_context(nc.sbuf_tensor(f"s{seq}QZ", [128, KC * SEQ], BF16))
            with seq_es:
                b_es = ExitStack()
                Kf = b_es.enter_context(nc.sbuf_tensor(f"s{seq}Kf", [128, 2 * SEQ], BF16))
                Vf = b_es.enter_context(nc.sbuf_tensor(f"s{seq}Vf", [128, 16 * FFKV], BF16))
                with b_es:
                    phase_B1(nc, S, RES, ones_bf, ones_f, w1n, QZ, Kf, Vf,
                             wq_f, wk_f, wv_f, cos_d, sin_d, tag=f"s{seq}B1")
                    phase_B2(nc, S, RES, id_bf, id_f, ones_bf, ones_f, w2n, rb_sb,
                             QZ, Kf, Vf, comb_scr, seq, wo_f, rw_d, mD_d, mT_d,
                             tag=f"s{seq}B2")
                if stop_phase == "B":
                    store_block(seq)
                    continue
                phase_C(nc, S, RES, ones_f, QZ, comb_scr_h, w1m, w2m, w3m,
                        b1t_d, b2t_d, b3t_d, seq, tag=f"s{seq}C")
                store_block(seq)

        with nc.Block() as blk:
            @blk.sync
            def _(sy):
                for nm in ("STO", "LD", "WL1", "WL2", "WL3", "W1B", "W2B",
                           "MSK", "CBROW", "CROWB", "EO"):
                    if S[nm].n:
                        sy.wait_ge(S[nm].h, S[nm].n)

    return nc


def phase_A(nc, S, RES, id_bf, id_f, ones_bf, ones_f, wq_o, wk_o, wv_o, wo_o,
            mA_d, tag=""):
    es = ExitStack()
    sbuf = lambda name, shape, dty=F32: es.enter_context(nc.sbuf_tensor(tag + name, shape, dty))
    psum = lambda name, shape, dty=F32: es.enter_context(nc.psum_tensor(tag + name, shape, dty))
    wq_s = sbuf("Awq", [128, KC * D], BF16)
    wk_s = sbuf("Awk", [128, KC * D], BF16)
    wv_s = sbuf("Awv", [128, KC * D], BF16)
    wo_s = sbuf("Awo", [128, KC * D], BF16)
    Kb = [sbuf(f"AK{i}", [128, KC * STW], BF16) for i in range(2)]
    Vb = [sbuf(f"AV{i}", [128, 4 * D], BF16) for i in range(2)]
    Qst = sbuf("AQ", [128, KC * STW], BF16)
    Y0 = sbuf("AY0", [128, KC * STW], BF16)
    Ost = sbuf("AO", [128, KC * STW], BF16)

    STs = [sbuf(f"ASTs{i}", [128, 512]) for i in range(2)]
    PT2 = [sbuf(f"APT2{i}", [128, 512], BF16) for i in range(2)]
    c_col = sbuf("Accol", [128, 8])
    c_row = sbuf("Acrow", [1, 512])
    c_bcs = sbuf("Acbcs", [128, 512])


    mA_s = sbuf("AmA", [128, 6 * 512], BF16)
    mF_s = mA_s[:, 3 * 512 + 127: 3 * 512 + 511]
    m0_s = mA_s[:, 3 * 512 + 383: 3 * 512 + 511]
    m1_s = mA_s[:, 3 * 512 + 255: 3 * 512 + 511]
    r_mean = sbuf("Arm", [1, STW])
    r_msq = sbuf("Arq", [1, STW])
    r_var = c_row
    r_rstd = r_msq
    r_nmax = [sbuf(f"Anm{i}", [128, 1]) for i in range(2)]
    r_sum = [sbuf(f"Asum{i}", [128, 1]) for i in range(2)]
    r_lns = [sbuf(f"Alns{i}", [128, 1]) for i in range(2)]
    psC = psum("ApsC", [128, 512])
    psD = psum("ApsD", [128, 512])
    psS0 = psum("ApsS", [128, 384])
    STb = [psum(f"AST{i}", [128, 512]) for i in range(2)]
    AVb = psum("AAV", [64, 512])
    CBCb = psum("ACBC", [128, 512])
    miscb = psum("Amisc", [128, 512])
    ps_crow = miscb[0:1, :]

    xb = [Y0[:, 0:STW], Y0[:, STW:2 * STW]]
    xq = [Y0[:, 2 * STW:3 * STW], Y0[:, 3 * STW:4 * STW]]

    psS = [psS0, psC]

    b0 = {k: s.n for k, s in S.items()}

    def kt_range(st):
        kts = list(range(max(0, 4 * st - 2), 4 * st + 4))
        kts.remove(4 * st)
        return [4 * st] + kts

    def kt_cols(st, kt):
        di = kt - 4 * st + 2
        c0, c1 = {0: (0, 128), 1: (0, 256), 2: (0, 512), 3: (0, 512),
                  4: (256, 512), 5: (384, 512)}[di]
        return di, c0, c1 - c0

    v_nkt = [len(kt_range(st)) for st in range(NST) for h in range(NH)]
    kt_pref = [0]
    for n in v_nkt:
        kt_pref.append(kt_pref[-1] + n)

    with es, nc.Block() as blk:
        @blk.gpsimd
        def _(gp):
            for w_d, w_s in ((wq_o, wq_s), (wk_o, wk_s), (wv_o, wv_s), (wo_o, wo_s)):
                S["WL1"].inc(gp.dma_start(out=w_s[:], in_=w_d.rearrange("p k d -> p (k d)")), 16)
            S["WL1"].inc(gp.dma_start(out=mA_s[:].rearrange("p (i c) -> p i c", i=6), in_=mA_d), 16)

        @blk.tensor
        def _(t):
            t.wait_ge(S["WL1"].h, b0["WL1"] + 16 * 5)

            def stats_pe(st, h):
                for qt in range(4):
                    w = (st * 16 + h) * 4 + qt
                    Q = st * 4 + qt
                    nch = a_nch(Q)
                    kt0 = Q - nch + 1
                    if w >= 2:
                        t.wait_ge(S["P"].h, b0["P"] + w - 1)
                    col = 0
                    seg_start = kt0
                    lhs = Qst[64 * (h % 2):64 * (h % 2) + 64,
                              (h // 2) * STW + qt * 128:(h // 2) * STW + qt * 128 + 128]
                    while seg_start <= Q:
                        seg_st = seg_start // 4
                        seg_end = min(Q, seg_st * 4 + 3)
                        n = (seg_end - seg_start + 1)
                        kb = Kb[seg_st % 2]
                        rel = seg_start - seg_st * 4
                        rhs = kb[64 * (h % 2):64 * (h % 2) + 64,
                                 (h // 2) * STW + rel * 128:(h // 2) * STW + rel * 128 + n * 128]
                        ins = t.matmul(psS[w % 2][:, col:col + n * 128], lhs, rhs,
                                       start=True, stop=True)
                        col += n * 128
                        seg_start = seg_end + 1
                    S["SMM"].inc(ins, 1)

            def t2_pe(st, h):
                v = st * 16 + h
                t.wait_ge(S["PN"].h, b0["PN"] + 4 * (v + 1))
                if v >= 1:
                    t.wait_ge(S["CRC"].h, b0["CRC"] + v)
                for qt in range(4):
                    ins = t.transpose(ps_crow[0:1, qt * 128:(qt + 1) * 128],
                                      c_col[:, (v % 2) * 4 + qt:(v % 2) * 4 + qt + 1], id_f[:])
                S["CTM"].inc(ins, 1)
                t.wait_ge(S["CRC"].h, b0["CRC"] + v + 1)
                if v >= 1:
                    t.wait_ge(S["CBC2"].h, b0["CBC2"] + v)
                S["CBM"].inc(t.matmul(CBCb[:], ones_f[:], c_row[:], start=True, stop=True), 1)
                kts = list(kt_range(st))
                nkt = len(kts)
                for i in range(nkt + 2):
                    if i < nkt:
                        kt = kts[i]
                        di, c0, wc = kt_cols(st, kt)
                        m = kt_pref[v] + i
                        if m >= 2:
                            t.wait_ge(S["STEV"].h, b0["STEV"] + m - 1)
                        s_k = kt // 4
                        rhs = Qst[64 * (h % 2):64 * (h % 2) + 64,
                                  (h // 2) * STW + c0:(h // 2) * STW + c0 + wc]
                        lhsK = Kb[s_k % 2][64 * (h % 2):64 * (h % 2) + 64,
                                           (h // 2) * STW + (kt % 4) * 128:
                                           (h // 2) * STW + (kt % 4) * 128 + 128]
                        S["STM"].inc(t.matmul(STb[m % 2][:, c0:c0 + wc], lhsK, rhs,
                                              start=True, stop=True), 1)
                    if i >= 2:
                        ka = kts[i - 2]
                        di, c0, wc = kt_cols(st, ka)
                        m = kt_pref[v] + i - 2
                        t.wait_ge(S["PEXP"].h, b0["PEXP"] + m + 1)
                        if i == 2:
                            t.wait_ge(S["OEV"].h, b0["OEV"] + v)
                        s_k = ka // 4
                        S["AVM"].inc(t.matmul(AVb[0:64, c0:c0 + wc],
                                              Vb[s_k % 2][:, (ka % 4) * D + 64 * h:
                                                          (ka % 4) * D + 64 * h + 64],
                                              PT2[m % 2][:, c0:c0 + wc],
                                              start=(i == 2), stop=(i == nkt + 1)), 1)

            for st in range(NST):
                stb = st * STW
                if st >= 1:
                    t.wait_ge(S["Y0"].h, b0["Y0"] + st * 8)
                    t.wait_ge(S["X1"].h, b0["X1"] + st * 8)
                    t.wait_ge(S["P"].h, b0["P"] + st * 64)
                for kc in range(KC):
                    t.wait_ge(S["CAST"].h, b0["CAST"] + st * 16 + kc * 2 + 2)
                    i1 = t.matmul(psC[0:1, :], ones_bf[:], xb[kc % 2],
                                  start=(kc == 0), stop=(kc == KC - 1))
                    i2 = t.matmul(psD[0:1, :], ones_bf[:], xq[kc % 2],
                                  start=(kc == 0), stop=(kc == KC - 1))
                    S["STAT"].inc(i1, 1)
                    S["STAT"].inc(i2, 1)
                t.wait_ge(S["MR"].h, b0["MR"] + st + 1)
                i1 = t.matmul(psC[:], ones_f[:], r_rstd[:], start=True, stop=True)
                i2 = t.matmul(psD[:], ones_f[:], r_mean[:], start=True, stop=True)
                S["BC"].inc(i2, 1)
                t.wait_ge(S["Y0"].h, b0["Y0"] + st * 8 + 8)
                for gi in range(24):
                    pb = (psC, psD)[gi % 2]
                    if gi >= 2:
                        t.wait_ge(S["PEV"].h, b0["PEV"] + st * 24 + gi - 1)
                    if gi < 16:
                        w_s = wq_s if gi < 8 else wk_s
                        oc = gi % 8
                        for kc in range(KC):
                            ins = t.matmul(pb[:], w_s[:, kc * D + oc * 128: kc * D + oc * 128 + 128],
                                           Y0[:, kc * STW:(kc + 1) * STW],
                                           start=(kc == 0), stop=(kc == KC - 1))
                    else:
                        vi = gi - 16
                        tsub, half = vi // 2, vi % 2
                        for kc in range(KC):
                            ins = t.matmul(pb[:],
                                           Y0[:, kc * STW + tsub * 128: kc * STW + tsub * 128 + 128],
                                           wv_s[:, kc * D + half * 512: kc * D + half * 512 + 512],
                                           start=(kc == 0), stop=(kc == KC - 1))
                    S["PMM"].inc(ins, 1)
                t.wait_ge(S["PEV"].h, b0["PEV"] + st * 24 + 24)
                for h in range(NH):
                    stats_pe(st, h)
                    if h >= 1:
                        t2_pe(st, h - 1)
                t2_pe(st, NH - 1)
                t.wait_ge(S["OEV"].h, b0["OEV"] + st * 16 + 16)
                for oc in range(KC):
                    pb = (psC, psD)[oc % 2]
                    if oc >= 2:
                        t.wait_ge(S["X1"].h, b0["X1"] + st * 8 + oc - 1)
                    for kc in range(KC):
                        ins = t.matmul(pb[:], wo_s[:, kc * D + oc * 128: kc * D + oc * 128 + 128],
                                       Ost[:, kc * STW:(kc + 1) * STW],
                                       start=(kc == 0), stop=(kc == KC - 1))
                    S["WMM"].inc(ins, 1)

        @blk.scalar
        def _(sc):
            sc.wait_ge(S["WL1"].h, b0["WL1"] + 16 * 5)

            def stats_act(st, h):
                for qt in range(4):
                    w = (st * 16 + h) * 4 + qt
                    Q = st * 4 + qt
                    nch = a_nch(Q)
                    wdt = nch * 128
                    sc.wait_ge(S["NM"].h, b0["NM"] + w + 1)
                    sc.activation(out=psS[w % 2][:, 0:wdt], in_=psS[w % 2][:, 0:wdt],
                                  func=AF.Exp, bias=r_nmax[w % 2][:], accum_out=r_sum[w % 2][:])
                    sc.drain()
                    S["P"].inc(sc.activation(out=r_lns[w % 2][:], in_=r_sum[w % 2][:], func=AF.Ln))

            def t2_act(st, h):
                v = st * 16 + h
                sc.wait_ge(S["CTM"].h, b0["CTM"] + v + 1)
                S["CRC"].inc(sc.activation(out=c_row[:], in_=ps_crow, func=AF.Copy))
                sc.wait_ge(S["CBM"].h, b0["CBM"] + v + 1)
                if v >= 1:
                    sc.wait_ge(S["STEV"].h, b0["STEV"] + kt_pref[v])
                S["CBC2"].inc(sc.activation(out=c_bcs[:], in_=CBCb[:], func=AF.Copy))
                nkt = v_nkt[v]
                kts = list(kt_range(st))
                for i in range(nkt):
                    kt = kts[i]
                    di, c0, wc = kt_cols(st, kt)
                    m = kt_pref[v] + i
                    sc.wait_ge(S["STEV"].h, b0["STEV"] + m + 1)
                    if m >= 2:
                        sc.wait_ge(S["AVM"].h, b0["AVM"] + m - 1)
                    S["PEXP"].inc(sc.activation(out=PT2[m % 2][:, c0:c0 + wc],
                                                in_=STs[m % 2][:, c0:c0 + wc], func=AF.Exp))
                sc.wait_ge(S["AVM"].h, b0["AVM"] + kt_pref[v + 1])
                dst = Ost[64 * (h % 2):64 * (h % 2) + 64, (h // 2) * STW:(h // 2 + 1) * STW]
                S["OEV"].inc(sc.activation(out=dst, in_=AVb[:], func=AF.Copy))

            for st in range(NST):
                stb = st * STW
                if st >= 1:
                    sc.wait_ge(S["PMM"].h, b0["PMM"] + st * 24)
                for kc in range(KC):
                    if kc >= 2:
                        sc.wait_ge(S["STAT"].h, b0["STAT"] + st * 16 + (kc - 2) * 2 + 2)
                    src2 = RES[:, kc * SEQ + stb: kc * SEQ + stb + STW]
                    S["CAST"].inc(sc.activation(out=xb[kc % 2], in_=src2, func=AF.Copy))
                    S["CAST"].inc(sc.activation(out=xq[kc % 2], in_=src2, func=AF.Square))
                sc.wait_ge(S["STAT"].h, b0["STAT"] + st * 16 + 16)
                sc.activation(out=r_mean[:], in_=psC[0:1, :], func=AF.Copy, scale=1.0 / 1024.0)
                S["ROWEV"].inc(sc.activation(out=r_msq[:], in_=psD[0:1, :], func=AF.Copy, scale=1.0 / 1024.0))
                sc.wait_ge(S["ROW1"].h, b0["ROW1"] + st + 1)
                sc.activation(out=r_var[:], in_=r_var[:], func=AF.Ln)
                sc.drain()
                S["RSTD"].inc(sc.activation(out=r_rstd[:], in_=r_var[:], func=AF.Exp, scale=-0.5))
                for gi in range(24):
                    sc.wait_ge(S["PMM"].h, b0["PMM"] + st * 24 + gi + 1)
                    pb = (psC, psD)[gi % 2]
                    if gi < 8:
                        dest = Qst[:, gi * STW:(gi + 1) * STW]
                        ins = sc.activation(out=dest, in_=pb[:], func=AF.Copy, scale=0.125)
                    elif gi < 16:
                        oc = gi - 8
                        dest = Kb[st % 2][:, oc * STW:(oc + 1) * STW]
                        ins = sc.activation(out=dest, in_=pb[:], func=AF.Copy)
                    else:
                        vi = gi - 16
                        tsub, half = vi // 2, vi % 2
                        dest = Vb[st % 2][:, tsub * D + half * 512: tsub * D + half * 512 + 512]
                        ins = sc.activation(out=dest, in_=pb[:], func=AF.Copy)
                    S["PEV"].inc(ins)
                for h in range(NH):
                    stats_act(st, h)
                    if h >= 1:
                        t2_act(st, h - 1)
                t2_act(st, NH - 1)

        @blk.vector
        def _(v):
            v.wait_ge(S["WL1"].h, b0["WL1"] + 16 * 5)

            def c_write(w2):
                vv, qt2 = w2 // 4, w2 % 4
                v.wait_ge(S["P"].h, b0["P"] + w2 + 1)
                if qt2 == 0 and vv >= 2:
                    v.wait_ge(S["CTM"].h, b0["CTM"] + vv - 1)
                S["PN"].inc(v.tensor_sub(c_col[:, (vv % 2) * 4 + qt2:(vv % 2) * 4 + qt2 + 1],
                                         r_nmax[w2 % 2][:], r_lns[w2 % 2][:]))

            def stats_dve(st, h):
                for qt in range(4):
                    w = (st * 16 + h) * 4 + qt
                    Q = st * 4 + qt
                    nch = a_nch(Q)
                    wdt = nch * 128
                    msk = (m0_s, m1_s, mF_s)[nch - 1]
                    v.wait_ge(S["SMM"].h, b0["SMM"] + w + 1)
                    if w >= 2:
                        v.wait_ge(S["P"].h, b0["P"] + w - 1)
                    v.tensor_add(psS[w % 2][:, 0:wdt], psS[w % 2][:, 0:wdt], msk)
                    v.drain()
                    S["NM"].inc(v.reduce_max(r_nmax[w % 2][:], psS[w % 2][:, 0:wdt],
                                             axis=AX, negate=True))
                    if w >= 1:
                        c_write(w - 1)

            def t2_dve(st, h):
                v2 = st * 16 + h
                kts = list(kt_range(st))
                for i, kt in enumerate(kts):
                    di, c0, wc = kt_cols(st, kt)
                    m = kt_pref[v2] + i
                    v.wait_ge(S["STM"].h, b0["STM"] + m + 1)
                    v.wait_ge(S["CBC2"].h, b0["CBC2"] + v2 + 1)
                    if m >= 2:
                        v.wait_ge(S["PEXP"].h, b0["PEXP"] + m - 1)
                    v.tensor_add(STs[m % 2][:, c0:c0 + wc], STb[m % 2][:, c0:c0 + wc],
                                 c_bcs[:, c0:c0 + wc])
                    v.drain()
                    S["STEV"].inc(v.tensor_add(STs[m % 2][:, c0:c0 + wc],
                                               STs[m % 2][:, c0:c0 + wc],
                                               mA_s[:, di * 512 + c0: di * 512 + c0 + wc]))

            for st in range(NST):
                stb = st * STW
                v.wait_ge(S["ROWEV"].h, b0["ROWEV"] + st + 1)
                if st >= 1:
                    v.wait_ge(S["CBM"].h, b0["CBM"] + 16 * st)
                v.tensor_mul(r_var[:], r_mean[:], r_mean[:])
                v.drain()
                S["ROW1"].inc(v.scalar_tensor_tensor(out=r_var[:], in0=r_msq[:],
                                                     scalar=1e-5, in1=r_var[:],
                                                     op0=ALU.add, op1=ALU.subtract))
                v.wait_ge(S["RSTD"].h, b0["RSTD"] + st + 1)
                S["MR"].inc(v.tensor_mul(r_mean[:], r_mean[:], r_rstd[:]))
                v.wait_ge(S["BC"].h, b0["BC"] + st + 1)
                if st >= 1:
                    v.wait_ge(S["PEXP"].h, b0["PEXP"] + kt_pref[16 * st])
                for kc in range(KC):
                    src2 = RES[:, kc * SEQ + stb: kc * SEQ + stb + STW]
                    v.tensor_mul(STs[0][:], src2, psC[:])
                    v.drain()
                    S["Y0"].inc(v.tensor_sub(Y0[:, kc * STW:(kc + 1) * STW], STs[0][:], psD[:]))
                    v.drain()
                for h in range(NH):
                    stats_dve(st, h)
                    if h == NH - 1:
                        c_write((st * 16 + h) * 4 + 3)
                    if h >= 1:
                        t2_dve(st, h - 1)
                t2_dve(st, NH - 1)
                for oc in range(KC):
                    pb = (psC, psD)[oc % 2]
                    v.wait_ge(S["WMM"].h, b0["WMM"] + st * 8 + oc + 1)
                    dst = RES[:, oc * SEQ + stb: oc * SEQ + stb + STW]
                    S["X1"].inc(v.tensor_add(dst, dst, pb[:]))


def phase_B1(nc, S, RES, ones_bf, ones_f, w1n, QZ, Kf, Vf,
             wq_f, wk_f, wv_f, cos_d, sin_d, tag):
    es = ExitStack()
    sbuf = lambda name, shape, dty=F32: es.enter_context(nc.sbuf_tensor(tag + name, shape, dty))
    psum = lambda name, shape, dty=F32: es.enter_context(nc.psum_tensor(tag + name, shape, dty))
    wqf_s = sbuf("wqf", [128, KC * D], BF16)
    wkf_s = sbuf("wkf", [128, KC * FFKV], BF16)
    wvf_s = sbuf("wvf", [128, KC * FFKV], BF16)
    cos_s = sbuf("cos", [128, 4 * STW])
    sin_s = sbuf("sin", [128, 4 * STW])
    y_st = sbuf("y", [128, KC * STW])
    xr = sbuf("xr", [128, KC * STW], BF16)
    xb = [sbuf(f"xb{i}", [128, STW], BF16) for i in range(2)]
    xq = [sbuf(f"xq{i}", [128, STW], BF16) for i in range(2)]
    zE = sbuf("zE", [128, STW])
    zO = sbuf("zO", [128, STW])
    tA = sbuf("tA", [128, STW])
    tB = sbuf("tB", [128, STW])
    tC = sbuf("tC", [128, STW])
    tD = sbuf("tD", [128, STW])
    r_mean = sbuf("rm", [1, STW])
    r_msq = sbuf("rq", [1, STW])
    r_var = sbuf("rv", [1, STW])
    r_rstd = sbuf("rr", [1, STW])
    psA = psum("psA", [128, 512])
    psB = psum("psB", [128, 512])
    psC = psum("psC", [128, 512])
    psD = psum("psD", [128, 512])

    b0 = {k: s.n for k, s in S.items()}
    NPJ = 14

    with es, nc.Block() as blk:
        @blk.gpsimd
        def _(gp):
            for w_d, w_s in ((wq_f, wqf_s), (wk_f, wkf_s), (wv_f, wvf_s)):
                S["WL1"].inc(gp.dma_start(out=w_s[:], in_=w_d.rearrange("p k d -> p (k d)")), 16)

        @blk.sync
        def _(sy):
            for st in range(NST):
                if st >= 1:
                    sy.wait_ge(S["T1"].h, b0["T1"] + st * 8)
                S["LD"].inc(sy.dma_start(out=cos_s[:].rearrange("p (c t) -> p c t", c=4),
                                         in_=cos_d[:, :, st * STW:(st + 1) * STW]), 16)
                S["LD"].inc(sy.dma_start(out=sin_s[:].rearrange("p (c t) -> p c t", c=4),
                                         in_=sin_d[:, :, st * STW:(st + 1) * STW]), 16)

        @blk.scalar
        def _(sc):
            for st in range(NST):
                stb = st * STW
                if st >= 1:
                    sc.wait_ge(S["STAT"].h, b0["STAT"] + st * 24)
                for kc in range(KC):
                    if kc >= 2:
                        sc.wait_ge(S["STAT"].h, b0["STAT"] + st * 24 + (kc - 2) * 2 + 2)
                    src = RES[:, kc * SEQ + stb: kc * SEQ + stb + STW]
                    S["CAST"].inc(sc.activation(out=xb[kc % 2][:], in_=src, func=AF.Copy))
                    S["CAST"].inc(sc.activation(out=xq[kc % 2][:], in_=src, func=AF.Square))
                sc.wait_ge(S["STAT"].h, b0["STAT"] + st * 24 + 16)
                sc.activation(out=r_mean[:], in_=psA[0:1, :], func=AF.Copy, scale=1.0 / 1024.0)
                S["ROWEV"].inc(sc.activation(out=r_msq[:], in_=psB[0:1, :], func=AF.Copy, scale=1.0 / 1024.0))
                sc.wait_ge(S["ROW1"].h, b0["ROW1"] + st * 2 + 1)
                sc.activation(out=r_var[:], in_=r_var[:], func=AF.Ln)
                sc.drain()
                S["RSTD"].inc(sc.activation(out=r_rstd[:], in_=r_var[:], func=AF.Exp, scale=-0.5))
                for kc in range(KC):
                    sc.wait_ge(S["Y0"].h, b0["Y0"] + st * 8 + kc + 1)
                    if kc >= 2:
                        sc.wait_ge(S["STAT"].h, b0["STAT"] + st * 24 + 16 + (kc - 2) + 1)
                    S["CAST"].inc(sc.activation(out=xq[kc % 2][:],
                                                in_=y_st[:, kc * STW:(kc + 1) * STW],
                                                func=AF.Square))
                sc.wait_ge(S["ROW1"].h, b0["ROW1"] + st * 2 + 2)
                sc.activation(out=r_var[:], in_=r_var[:], func=AF.Ln)
                sc.drain()
                S["RSTD"].inc(sc.activation(out=r_rstd[:], in_=r_var[:], func=AF.Exp, scale=-0.5))
                for gi in range(NPJ):
                    sc.wait_ge(S["PMM"].h, b0["PMM"] + st * NPJ + gi + 1)
                    pb = (psC, psD)[gi % 2]
                    if gi < 8:
                        ins = sc.activation(out=QZ[:, gi * SEQ + stb: gi * SEQ + stb + STW],
                                            in_=pb[:], func=AF.Copy, scale=0.125)
                    elif gi < 10:
                        oc = gi - 8
                        ins = sc.activation(out=Kf[:, oc * SEQ + stb: oc * SEQ + stb + STW],
                                            in_=pb[:], func=AF.Copy)
                    else:
                        tsub = gi - 10
                        ins = sc.activation(out=Vf[:, (st * 4 + tsub) * FFKV: (st * 4 + tsub + 1) * FFKV],
                                            in_=pb[:, 0:FFKV], func=AF.Copy)
                    S["PEV"].inc(ins)

        @blk.tensor
        def _(t):
            t.wait_ge(S["WL1"].h, b0["WL1"] + 48)
            for st in range(NST):
                if st >= 1:
                    t.wait_ge(S["Y0"].h, b0["Y0"] + st * 8)
                    t.wait_ge(S["T1"].h, b0["T1"] + st * 8)
                for kc in range(KC):
                    t.wait_ge(S["CAST"].h, b0["CAST"] + st * 24 + kc * 2 + 2)
                    i1 = t.matmul(psA[0:1, :], ones_bf[:], xb[kc % 2][:],
                                  start=(kc == 0), stop=(kc == KC - 1))
                    i2 = t.matmul(psB[0:1, :], ones_bf[:], xq[kc % 2][:],
                                  start=(kc == 0), stop=(kc == KC - 1))
                    S["STAT"].inc(i1, 1)
                    S["STAT"].inc(i2, 1)
                t.wait_ge(S["MR"].h, b0["MR"] + st + 1)
                i1 = t.matmul(psA[:], ones_f[:], r_rstd[:], start=True, stop=True)
                i2 = t.matmul(psB[:], ones_f[:], r_mean[:], start=True, stop=True)
                S["BC"].inc(i2, 1)
                t.wait_ge(S["Y0"].h, b0["Y0"] + st * 8 + 8)
                for kc in range(KC):
                    t.wait_ge(S["CAST"].h, b0["CAST"] + st * 24 + 16 + kc + 1)
                    ins = t.matmul(psB[0:1, :], ones_bf[:], xq[kc % 2][:],
                                   start=(kc == 0), stop=(kc == KC - 1))
                    S["STAT"].inc(ins, 1)
                t.wait_ge(S["RSTD"].h, b0["RSTD"] + st * 2 + 2)
                S["BC"].inc(t.matmul(psA[:], ones_f[:], r_rstd[:], start=True, stop=True), 1)
                t.wait_ge(S["T1"].h, b0["T1"] + st * 8 + 8)
                if st >= 1:
                    t.wait_ge(S["PEV"].h, b0["PEV"] + st * NPJ)
                for gi in range(NPJ):
                    pb = (psC, psD)[gi % 2]
                    if gi >= 2:
                        t.wait_ge(S["PEV"].h, b0["PEV"] + st * NPJ + gi - 1)
                    if gi < 8:
                        for kc in range(KC):
                            ins = t.matmul(pb[:], wqf_s[:, kc * D + gi * 128: kc * D + gi * 128 + 128],
                                           xr[:, kc * STW:(kc + 1) * STW],
                                           start=(kc == 0), stop=(kc == KC - 1))
                    elif gi < 10:
                        oc = gi - 8
                        for kc in range(KC):
                            ins = t.matmul(pb[:],
                                           wkf_s[:, kc * FFKV + oc * 128: kc * FFKV + oc * 128 + 128],
                                           xr[:, kc * STW:(kc + 1) * STW],
                                           start=(kc == 0), stop=(kc == KC - 1))
                    else:
                        tsub = gi - 10
                        for kc in range(KC):
                            ins = t.matmul(pb[:, 0:FFKV],
                                           xr[:, kc * STW + tsub * 128: kc * STW + tsub * 128 + 128],
                                           wvf_s[:, kc * FFKV: (kc + 1) * FFKV],
                                           start=(kc == 0), stop=(kc == KC - 1))
                    S["PMM"].inc(ins, 1)

        @blk.vector
        def _(v):
            for st in range(NST):
                stb = st * STW
                v.wait_ge(S["ROWEV"].h, b0["ROWEV"] + st + 1)
                v.tensor_mul(r_var[:], r_mean[:], r_mean[:])
                v.drain()
                S["ROW1"].inc(v.scalar_tensor_tensor(out=r_var[:], in0=r_msq[:],
                                                     scalar=1e-5, in1=r_var[:],
                                                     op0=ALU.add, op1=ALU.subtract))
                v.wait_ge(S["RSTD"].h, b0["RSTD"] + st * 2 + 1)
                S["MR"].inc(v.tensor_mul(r_mean[:], r_mean[:], r_rstd[:]))
                v.wait_ge(S["BC"].h, b0["BC"] + st * 2 + 1)
                if st >= 1:
                    v.wait_ge(S["CAST"].h, b0["CAST"] + st * 24)
                for kc in range(KC):
                    src = RES[:, kc * SEQ + stb: kc * SEQ + stb + STW]
                    v.tensor_mul(tA[:], src, psA[:])
                    v.drain()
                    S["Y0"].inc(v.tensor_sub(y_st[:, kc * STW:(kc + 1) * STW], tA[:], psB[:]))
                    v.drain()
                v.wait_ge(S["STAT"].h, b0["STAT"] + st * 24 + 24)
                S["ROW1"].inc(v.tensor_scalar(out=r_var[:], in0=psB[0:1, :],
                                              scalar1=1.0 / 1024.0, scalar2=1e-6,
                                              op0=ALU.mult, op1=ALU.add))
                v.wait_ge(S["BC"].h, b0["BC"] + st * 2 + 2)
                v.wait_ge(S["LD"].h, b0["LD"] + 32 * (st + 1))
                if st >= 1:
                    v.wait_ge(S["PMM"].h, b0["PMM"] + st * NPJ)
                for c in range(4):
                    yE = y_st[:, c * STW:(c + 1) * STW]
                    yO = y_st[:, (4 + c) * STW:(5 + c) * STW]
                    v.tensor_mul(zE[:], yE, psA[:])
                    v.tensor_mul(zO[:], yO, psA[:])
                    v.drain()
                    v.tensor_scalar(out=zE[:], in0=zE[:], scalar1=w1n[:, c:c + 1],
                                    scalar2=None, op0=ALU.mult)
                    v.tensor_scalar(out=zO[:], in0=zO[:], scalar1=w1n[:, 4 + c:5 + c],
                                    scalar2=None, op0=ALU.mult)
                    v.drain()
                    cc = cos_s[:, c * STW:(c + 1) * STW]
                    ss = sin_s[:, c * STW:(c + 1) * STW]
                    v.tensor_mul(tA[:], zE[:], cc)
                    v.tensor_mul(tB[:], zO[:], ss)
                    v.tensor_mul(tC[:], zE[:], ss)
                    v.tensor_mul(tD[:], zO[:], cc)
                    v.drain()
                    S["T1"].inc(v.tensor_sub(xr[:, c * STW:(c + 1) * STW], tA[:], tB[:]), 1)
                    S["T1"].inc(v.tensor_add(xr[:, (4 + c) * STW:(5 + c) * STW], tC[:], tD[:]), 1)
                    v.drain()


def phase_B2(nc, S, RES, id_bf, id_f, ones_bf, ones_f, w2n, rb_sb, QZ, Kf, Vf,
             comb_scr, seq, wo_f, rw_d, mD_d, mT_d, tag):
    es = ExitStack()
    sbuf = lambda name, shape, dty=F32: es.enter_context(nc.sbuf_tensor(tag + name, shape, dty))
    psum = lambda name, shape, dty=F32: es.enter_context(nc.psum_tensor(tag + name, shape, dty))
    wof_s = sbuf("wof", [128, KC * D], BF16)
    rw_s = sbuf("rw", [128, KC * NE])
    mD_s = sbuf("mD", [128, 4 * 512], BF16)
    mT_s = sbuf("mT", [128, 4 * 512], BF16)
    Ssb = [sbuf(f"Ss{i}", [128, SEQ]) for i in range(2)]
    STsb = sbuf("STsb", [128, 1024])
    STs = [STsb[:, 0:512], STsb[:, 512:1024]]
    PT2b = sbuf("PT2b", [128, 1024], BF16)
    PT2 = [PT2b[:, 0:512], PT2b[:, 512:1024]]
    c_col = sbuf("ccol", [128, 8])
    c_row = sbuf("crow", [1, 512])
    c_bcs = sbuf("cbcs", [128, 512])
    Ost = sbuf("O", [128, KC * STW], BF16)
    y_st = sbuf("y", [128, KC * STW])
    comb_stage = sbuf("cstg", [NE, 128])
    hb = [sbuf(f"hb{i}", [128, STW], BF16) for i in range(2)]
    xqs = [sbuf(f"xqs{i}", [128, STW], BF16) for i in range(2)]
    t1 = [sbuf(f"t1{i}", [128, STW]) for i in range(2)]
    lg_sb = sbuf("lg", [NE, STW])
    lt_sb = sbuf("lt", [128, 32])
    ctT = sbuf("ctT", [128, 32])
    m1r = sbuf("m1r", [128, 1])
    m2r = sbuf("m2r", [128, 1])
    d12 = sbuf("d12", [128, 1])
    w1r = sbuf("w1r", [128, 1])
    w2r = sbuf("w2r", [128, 1])
    eq1 = sbuf("eq1", [128, 8])
    eq2 = sbuf("eq2", [128, 8])
    l2s = sbuf("l2s", [128, 8])
    r_mean = sbuf("rm", [1, STW])
    r_msq = sbuf("rq", [1, STW])
    r_var = sbuf("rv", [1, STW])
    r_rstd = sbuf("rr", [1, STW])
    r_nmax = [sbuf(f"nm{i}", [128, 1]) for i in range(2)]
    r_sum = [sbuf(f"sum{i}", [128, 1]) for i in range(2)]
    r_lns = [sbuf(f"lns{i}", [128, 1]) for i in range(2)]
    psS = [psum(f"pss{i}", [128, 512]) for i in range(2)]
    STb = [psum(f"stb{i}", [128, 512]) for i in range(2)]
    AVb = [psum(f"avb{i}", [64, 512]) for i in range(2)]
    CBCb = psum("cbcb", [128, 512])
    miscb = psum("miscb", [128, 512])
    ps_crow = miscb[0:1, :]
    ps_r = miscb[0:NE, :]
    ps_lt = miscb[:, 16:24]
    ps_ct = miscb[0:NE, 128:256]

    b0 = {k: s.n for k, s in S.items()}

    sub_list = []
    blk_eng = []
    blk_cnt = []
    blk_pref = []
    acnt = dcnt = 0
    a_thru_sub = []
    for st in range(NST):
        for h in range(NH):
            for qt in range(4):
                gq = st * 4 + qt
                nb = gq // 4 + 1
                sub_list.append((st, h, qt, gq, nb))
                blk_pref.append(len(blk_eng))
                for b in range(nb):
                    if b == nb - 1:
                        blk_eng.append("D"); dcnt += 1; blk_cnt.append(dcnt)
                    else:
                        blk_eng.append("A"); acnt += 1; blk_cnt.append(acnt)
                a_thru_sub.append(acnt)
    blk_pref.append(len(blk_eng))
    NSUB = len(sub_list)
    v_nkt = [4 * (st + 1) for st in range(NST) for h in range(NH)]
    kt_pref = [0]
    for n in v_nkt:
        kt_pref.append(kt_pref[-1] + n)

    def blk_wait(eng, g):
        if g < 0:
            return
        nm = "SEV" if blk_eng[g] == "A" else "SEVD"
        eng.wait_ge(S[nm].h, b0[nm] + blk_cnt[g])

    with es, nc.Block() as blk:
        @blk.gpsimd
        def _(gp):
            S["WL1"].inc(gp.dma_start(out=wof_s[:], in_=wo_f.rearrange("p k d -> p (k d)")), 16)

        @blk.sync
        def _(sy):
            S["LD"].inc(sy.dma_start(out=rw_s[:], in_=rw_d.rearrange("p k e -> p (k e)")), 16)
            S["LD"].inc(sy.dma_start(out=mD_s[:].rearrange("p (i c) -> p i c", i=4), in_=mD_d), 16)
            S["LD"].inc(sy.dma_start(out=mT_s[:].rearrange("p (i c) -> p i c", i=4), in_=mT_d), 16)
            for st in range(NST):
                for tl in range(4):
                    sy.wait_ge(S["AUX"].h, b0["AUX"] + st * 4 + tl + 1)
                    S["EO"].inc(sy.dma_start(
                        out=comb_scr[0:NE, seq * SEQ + st * STW + tl * 128:
                                     seq * SEQ + st * STW + tl * 128 + 128],
                        in_=comb_stage[:]), 16)
            sy.wait_ge(S["EO"].h, S["EO"].n)

        @blk.tensor
        def _(t):
            t.wait_ge(S["LD"].h, b0["LD"] + 48)
            t.wait_ge(S["WL1"].h, b0["WL1"] + 16)

            def stats_pe(st, h):
                for qt in range(4):
                    w = (st * 16 + h) * 4 + qt
                    gq = st * 4 + qt
                    nb = gq // 4 + 1
                    kend = (gq + 1) * 128
                    qch, qba = qloc(h)
                    g = h // 4
                    lhs = QZ[qba:qba + 64,
                             qch * SEQ + gq * 128: qch * SEQ + gq * 128 + 128]
                    for b in range(nb):
                        gb = blk_pref[w] + b
                        blk_wait(t, gb - 2)
                        wdt = min(512, kend - b * 512)
                        rhs = Kf[64 * (g % 2):64 * (g % 2) + 64,
                                 (g // 2) * SEQ + b * 512:(g // 2) * SEQ + b * 512 + wdt]
                        S["SMM"].inc(t.matmul(psS[gb % 2][:, 0:wdt], lhs, rhs,
                                              start=True, stop=True), 1)

            def t2_pe(st, h):
                v = st * 16 + h
                qch, qba = qloc(h)
                g = h // 4
                t.wait_ge(S["PN"].h, b0["PN"] + 4 * (v + 1))
                if v >= 1:
                    t.wait_ge(S["CRC"].h, b0["CRC"] + v)
                if h == 0 and st >= 1:
                    t.wait_ge(S["SIL"].h, b0["SIL"] + st)
                    t.wait_ge(S["AUX3"].h, b0["AUX3"] + st * 4)
                    t.wait_ge(S["AUX"].h, b0["AUX"] + st * 4)
                for qt in range(4):
                    ins = t.transpose(ps_crow[0:1, qt * 128:(qt + 1) * 128],
                                      c_col[:, (v % 2) * 4 + qt:(v % 2) * 4 + qt + 1], id_f[:])
                S["CTM"].inc(ins, 1)
                t.wait_ge(S["CRC"].h, b0["CRC"] + v + 1)
                if v >= 1:
                    t.wait_ge(S["CBC2"].h, b0["CBC2"] + v)
                S["CBM"].inc(t.matmul(CBCb[:], ones_f[:], c_row[:], start=True, stop=True), 1)
                nkt = v_nkt[v]
                for kt in range(nkt + 2):
                    if kt < nkt:
                        m = kt_pref[v] + kt
                        if m >= 2:
                            t.wait_ge(S["STEV"].h, b0["STEV"] + m - 1)
                        rhs = QZ[qba:qba + 64, qch * SEQ + st * STW: qch * SEQ + (st + 1) * STW]
                        lhsK = Kf[64 * (g % 2):64 * (g % 2) + 64,
                                  (g // 2) * SEQ + kt * 128:(g // 2) * SEQ + kt * 128 + 128]
                        S["STM"].inc(t.matmul(STb[m % 2][:], lhsK, rhs, start=True, stop=True), 1)
                    if kt >= 2:
                        ka = kt - 2
                        m = kt_pref[v] + ka
                        t.wait_ge(S["PEXP"].h, b0["PEXP"] + m // 2 + 1)
                        if ka == 0:
                            t.wait_ge(S["OEV"].h, b0["OEV"] + max(0, v - 1))
                        S["AVM"].inc(t.matmul(AVb[v % 2][:],
                                              Vf[:, ka * FFKV + 64 * g: ka * FFKV + 64 * g + 64],
                                              PT2[m % 2],
                                              start=(ka == 0), stop=(ka == nkt - 1)), 1)

            for st in range(NST):
                if st >= 1:
                    t.wait_ge(S["H1"].h, b0["H1"] + st * 8)
                    t.wait_ge(S["T1"].h, b0["T1"] + st * 8)
                for h in range(NH):
                    stats_pe(st, h)
                    if h >= 1:
                        t2_pe(st, h - 1)
                t2_pe(st, NH - 1)
                stb = st * STW
                t.wait_ge(S["STEV"].h, b0["STEV"] + kt_pref[st * 16 + 16])
                if st >= 1:
                    t.wait_ge(S["Y0"].h, b0["Y0"] + st * 8)
                for kc in range(KC):
                    t.wait_ge(S["CAST"].h, b0["CAST"] + st * 24 + kc * 2 + 2)
                    i1 = t.matmul(STb[0][0:1, :], ones_bf[:], hb[kc % 2][:],
                                  start=(kc == 0), stop=(kc == KC - 1))
                    i2 = t.matmul(STb[1][0:1, :], ones_bf[:], xqs[kc % 2][:],
                                  start=(kc == 0), stop=(kc == KC - 1))
                    S["STAT"].inc(i1, 1)
                    S["STAT"].inc(i2, 1)
                t.wait_ge(S["MR"].h, b0["MR"] + st + 1)
                i1 = t.matmul(STb[0][:], ones_f[:], r_rstd[:], start=True, stop=True)
                i2 = t.matmul(STb[1][:], ones_f[:], r_mean[:], start=True, stop=True)
                S["BC"].inc(i2, 1)
                t.wait_ge(S["OEV"].h, b0["OEV"] + st * 16 + 16)
                if st >= 1:
                    t.wait_ge(S["H1"].h, b0["H1"] + st * 8)
                for oc in range(KC):
                    pb = psS[oc % 2]
                    if oc >= 2:
                        t.wait_ge(S["H1"].h, b0["H1"] + st * 8 + oc - 1)
                    for kc in range(KC):
                        ins = t.matmul(pb[:], wof_s[:, kc * D + oc * 128: kc * D + oc * 128 + 128],
                                       Ost[:, kc * STW:(kc + 1) * STW],
                                       start=(kc == 0), stop=(kc == KC - 1))
                    S["WMM"].inc(ins, 1)
                t.wait_ge(S["H1"].h, b0["H1"] + st * 8 + 8)
                for kc in range(KC):
                    t.wait_ge(S["CAST"].h, b0["CAST"] + st * 24 + 16 + kc + 1)
                    ins = t.matmul(STb[1][0:1, :], ones_bf[:], xqs[kc % 2][:],
                                   start=(kc == 0), stop=(kc == KC - 1))
                    S["STAT"].inc(ins, 1)
                t.wait_ge(S["RSTD"].h, b0["RSTD"] + st * 2 + 2)
                t.wait_ge(S["Y0"].h, b0["Y0"] + st * 8 + 8)
                S["BC"].inc(t.matmul(STb[0][:], ones_f[:], r_rstd[:], start=True, stop=True), 1)
                if st >= 1:
                    t.wait_ge(S["AUX2"].h, b0["AUX2"] + st)
                for oc in range(KC):
                    t.wait_ge(S["T1"].h, b0["T1"] + st * 8 + oc + 1)
                    ins = t.matmul(ps_r, rw_s[:, oc * NE:(oc + 1) * NE], t1[oc % 2][:],
                                   start=(oc == 0), stop=(oc == KC - 1))
                    S["RMM"].inc(ins, 1)
                t.wait_ge(S["AUX2"].h, b0["AUX2"] + st + 1)
                for tl in range(4):
                    if st * 4 + tl >= 1:
                        t.wait_ge(S["AUX3"].h, b0["AUX3"] + st * 4 + tl)
                        t.wait_ge(S["AUX"].h, b0["AUX"] + st * 4 + tl)
                    S["H2"].inc(t.transpose(ps_lt, lg_sb[:, tl * 128:(tl + 1) * 128], id_f[0:8, 0:8]), 1)
                    t.wait_ge(S["CBMM"].h, b0["CBMM"] + st * 4 + tl + 1)
                    S["EPI"].inc(t.transpose(ps_ct, ctT[:, tl * 8:(tl + 1) * 8], id_f[:]), 1)

        @blk.scalar
        def _(sc):
            def stats_act(st, h):
                for qt in range(4):
                    w = (st * 16 + h) * 4 + qt
                    gq = st * 4 + qt
                    nb = gq // 4 + 1
                    kend = (gq + 1) * 128
                    for b in range(nb - 1):
                        gb = blk_pref[w] + b
                        sc.wait_ge(S["SMM"].h, b0["SMM"] + gb + 1)
                        if w >= 2:
                            sc.wait_ge(S["P"].h, b0["P"] + w - 1)
                        S["SEV"].inc(sc.activation(out=Ssb[w % 2][:, b * 512:(b + 1) * 512],
                                                   in_=psS[gb % 2][:], func=AF.Copy))
                    sc.wait_ge(S["NM"].h, b0["NM"] + w + 1)
                    sc.activation(out=Ssb[w % 2][:, 0:kend], in_=Ssb[w % 2][:, 0:kend],
                                  func=AF.Exp, bias=r_nmax[w % 2][:], accum_out=r_sum[w % 2][:])
                    sc.drain()
                    S["P"].inc(sc.activation(out=r_lns[w % 2][:], in_=r_sum[w % 2][:], func=AF.Ln))

            def t2_act(st, h):
                v = st * 16 + h
                sc.wait_ge(S["CTM"].h, b0["CTM"] + v + 1)
                S["CRC"].inc(sc.activation(out=c_row[:], in_=ps_crow, func=AF.Copy))
                sc.wait_ge(S["CBM"].h, b0["CBM"] + v + 1)
                if v >= 1:
                    sc.wait_ge(S["STEV"].h, b0["STEV"] + kt_pref[v])
                S["CBC2"].inc(sc.activation(out=c_bcs[:], in_=CBCb[:], func=AF.Copy))
                nkt = v_nkt[v]
                for kt in range(0, nkt, 2):
                    m = kt_pref[v] + kt
                    sc.wait_ge(S["STEV"].h, b0["STEV"] + m + 2)
                    if m >= 2:
                        sc.wait_ge(S["AVM"].h, b0["AVM"] + m)
                    S["PEXP"].inc(sc.activation(out=PT2b[:], in_=STsb[:], func=AF.Exp))
                sc.wait_ge(S["AVM"].h, b0["AVM"] + kt_pref[v + 1])
                dst = Ost[64 * (h % 2):64 * (h % 2) + 64, (h // 2) * STW:(h // 2 + 1) * STW]
                S["OEV"].inc(sc.activation(out=dst, in_=AVb[v % 2][:], func=AF.Copy))

            for st in range(NST):
                for h in range(NH):
                    stats_act(st, h)
                    if h >= 1:
                        t2_act(st, h - 1)
                t2_act(st, NH - 1)
                stb = st * STW
                if st >= 1:
                    sc.wait_ge(S["STAT"].h, b0["STAT"] + st * 24)
                for kc in range(KC):
                    if kc >= 2:
                        sc.wait_ge(S["STAT"].h, b0["STAT"] + st * 24 + (kc - 2) * 2 + 2)
                    src2 = RES[:, kc * SEQ + stb: kc * SEQ + stb + STW]
                    S["CAST"].inc(sc.activation(out=hb[kc % 2][:], in_=src2, func=AF.Copy))
                    S["CAST"].inc(sc.activation(out=xqs[kc % 2][:], in_=src2, func=AF.Square))
                sc.wait_ge(S["STAT"].h, b0["STAT"] + st * 24 + 16)
                sc.activation(out=r_mean[:], in_=STb[0][0:1, :], func=AF.Copy, scale=1.0 / 1024.0)
                S["ROWEV"].inc(sc.activation(out=r_msq[:], in_=STb[1][0:1, :], func=AF.Copy, scale=1.0 / 1024.0))
                sc.wait_ge(S["ROW1"].h, b0["ROW1"] + st * 2 + 1)
                sc.activation(out=r_var[:], in_=r_var[:], func=AF.Ln)
                sc.drain()
                S["RSTD"].inc(sc.activation(out=r_rstd[:], in_=r_var[:], func=AF.Exp, scale=-0.5))
                for kc in range(KC):
                    sc.wait_ge(S["H1"].h, b0["H1"] + st * 8 + kc + 1)
                    if kc >= 2:
                        sc.wait_ge(S["STAT"].h, b0["STAT"] + st * 24 + 16 + (kc - 2) + 1)
                    S["CAST"].inc(sc.activation(out=xqs[kc % 2][:],
                                                in_=y_st[:, kc * STW:(kc + 1) * STW],
                                                func=AF.Square))
                sc.wait_ge(S["ROW1"].h, b0["ROW1"] + st * 2 + 2)
                sc.activation(out=r_var[:], in_=r_var[:], func=AF.Ln)
                sc.drain()
                S["RSTD"].inc(sc.activation(out=r_rstd[:], in_=r_var[:], func=AF.Exp, scale=-0.5))
                for oc in range(KC):
                    sc.wait_ge(S["T1"].h, b0["T1"] + st * 8 + oc + 1)
                    S["ZC"].inc(sc.activation(out=QZ[:, oc * SEQ + stb: oc * SEQ + stb + STW],
                                              in_=t1[oc % 2][:], func=AF.Copy))
                sc.wait_ge(S["RMM"].h, b0["RMM"] + st * 8 + 8)
                S["SIL"].inc(sc.activation(out=lg_sb[:], in_=ps_r, func=AF.Copy))
                if st >= 1:
                    sc.wait_ge(S["CBMM"].h, b0["CBMM"] + st * 4)
                for tl in range(4):
                    sc.wait_ge(S["H2"].h, b0["H2"] + st * 4 + tl + 1)
                    S["AUX3"].inc(sc.activation(out=lt_sb[:, tl * 8:(tl + 1) * 8], in_=ps_lt, func=AF.Copy))
                    sc.wait_ge(S["HST"].h, b0["HST"] + st * 4 + tl + 1)
                    S["CBEV"].inc(sc.activation(out=w1r[:], in_=d12[:], func=AF.Sigmoid))
                    sc.wait_ge(S["EPI"].h, b0["EPI"] + st * 4 + tl + 1)
                    if st * 4 + tl >= 1:
                        sc.wait_ge(S["EO"].h, b0["EO"] + 16 * (st * 4 + tl))
                    S["AUX"].inc(sc.activation(out=comb_stage[:], in_=ps_ct, func=AF.Copy))

        @blk.vector
        def _(v):
            v.wait_ge(S["LD"].h, b0["LD"] + 48)

            def c_write(w2):
                vv, qt2 = w2 // 4, w2 % 4
                v.wait_ge(S["P"].h, b0["P"] + w2 + 1)
                if qt2 == 0 and vv >= 2:
                    v.wait_ge(S["CTM"].h, b0["CTM"] + vv - 1)
                S["PN"].inc(v.tensor_sub(c_col[:, (vv % 2) * 4 + qt2:(vv % 2) * 4 + qt2 + 1],
                                         r_nmax[w2 % 2][:], r_lns[w2 % 2][:]))

            def stats_dve(st, h):
                for qt in range(4):
                    w = (st * 16 + h) * 4 + qt
                    gq = st * 4 + qt
                    nb = gq // 4 + 1
                    kend = (gq + 1) * 128
                    dlt = gq % 4
                    lw = kend - (nb - 1) * 512
                    gb = blk_pref[w] + nb - 1
                    v.wait_ge(S["SMM"].h, b0["SMM"] + gb + 1)
                    if w >= 2:
                        v.wait_ge(S["P"].h, b0["P"] + w - 1)
                    S["SEVD"].inc(v.tensor_add(Ssb[w % 2][:, (nb - 1) * 512:(nb - 1) * 512 + lw],
                                               psS[gb % 2][:, 0:lw],
                                               mD_s[:, dlt * 512: dlt * 512 + lw]))
                    v.drain()
                    v.wait_ge(S["SEV"].h, b0["SEV"] + a_thru_sub[w])
                    S["NM"].inc(v.reduce_max(r_nmax[w % 2][:], Ssb[w % 2][:, 0:kend],
                                             axis=AX, negate=True))
                    if w >= 1:
                        c_write(w - 1)

            def t2_dve(st, h):
                v2 = st * 16 + h
                nkt = v_nkt[v2]
                for kt in range(nkt):
                    m = kt_pref[v2] + kt
                    v.wait_ge(S["STM"].h, b0["STM"] + m + 1)
                    v.wait_ge(S["CBC2"].h, b0["CBC2"] + v2 + 1)
                    if m >= 2:
                        v.wait_ge(S["PEXP"].h, b0["PEXP"] + (m - 2) // 2 + 1)
                    dlt = kt - 4 * st
                    if dlt >= 0:
                        v.tensor_add(STs[m % 2], STb[m % 2][:], c_bcs[:])
                        v.drain()
                        ins = v.tensor_add(STs[m % 2], STs[m % 2],
                                           mT_s[:, dlt * 512:(dlt + 1) * 512])
                    else:
                        ins = v.tensor_add(STs[m % 2], STb[m % 2][:], c_bcs[:])
                    S["STEV"].inc(ins)

            for st in range(NST):
                for h in range(NH):
                    stats_dve(st, h)
                    if h == NH - 1:
                        c_write((st * 16 + h) * 4 + 3)
                    if h >= 1:
                        t2_dve(st, h - 1)
                t2_dve(st, NH - 1)
                stb = st * STW
                v.wait_ge(S["ROWEV"].h, b0["ROWEV"] + st + 1)
                v.tensor_mul(r_var[:], r_mean[:], r_mean[:])
                v.drain()
                S["ROW1"].inc(v.scalar_tensor_tensor(out=r_var[:], in0=r_msq[:],
                                                     scalar=1e-5, in1=r_var[:],
                                                     op0=ALU.add, op1=ALU.subtract))
                v.wait_ge(S["RSTD"].h, b0["RSTD"] + st * 2 + 1)
                S["MR"].inc(v.tensor_mul(r_mean[:], r_mean[:], r_rstd[:]))
                v.wait_ge(S["BC"].h, b0["BC"] + st * 2 + 1)
                if st >= 1:
                    v.wait_ge(S["CAST"].h, b0["CAST"] + st * 24)
                    v.wait_ge(S["ZC"].h, b0["ZC"] + st * 8)
                    v.wait_ge(S["RMM"].h, b0["RMM"] + st * 8)
                for oc in range(KC):
                    src2 = RES[:, oc * SEQ + stb: oc * SEQ + stb + STW]
                    ych = y_st[:, oc * STW:(oc + 1) * STW]
                    v.tensor_mul(t1[0][:], src2, STb[0][:])
                    v.drain()
                    S["Y0"].inc(v.tensor_sub(ych, t1[0][:], STb[1][:]))
                    v.drain()
                    v.wait_ge(S["WMM"].h, b0["WMM"] + st * 8 + oc + 1)
                    S["H1"].inc(v.tensor_add(ych, ych, psS[oc % 2][:]))
                    v.drain()
                    S["X1"].inc(v.tensor_add(src2, src2, ych))
                    v.drain()
                v.wait_ge(S["STAT"].h, b0["STAT"] + st * 24 + 24)
                S["ROW1"].inc(v.tensor_scalar(out=r_var[:], in0=STb[1][0:1, :],
                                              scalar1=1.0 / 1024.0, scalar2=1e-6,
                                              op0=ALU.mult, op1=ALU.add))
                v.wait_ge(S["BC"].h, b0["BC"] + st * 2 + 2)
                for oc in range(KC):
                    if oc >= 2:
                        v.wait_ge(S["RMM"].h, b0["RMM"] + st * 8 + oc - 1)
                        v.wait_ge(S["ZC"].h, b0["ZC"] + st * 8 + oc - 1)
                    ych = y_st[:, oc * STW:(oc + 1) * STW]
                    v.tensor_mul(t1[oc % 2][:], ych, STb[0][:])
                    v.drain()
                    S["T1"].inc(v.tensor_scalar(out=t1[oc % 2][:], in0=t1[oc % 2][:],
                                                scalar1=w2n[:, oc:oc + 1], scalar2=None,
                                                op0=ALU.mult))
                v.wait_ge(S["SIL"].h, b0["SIL"] + st + 1)
                S["AUX2"].inc(v.tensor_scalar(out=lg_sb[:], in0=lg_sb[:],
                                              scalar1=rb_sb[0:NE, 0:1], scalar2=None,
                                              op0=ALU.add))
                for tl in range(4):
                    v.wait_ge(S["AUX3"].h, b0["AUX3"] + st * 4 + tl + 1)
                    if st * 4 + tl >= 4:
                        v.wait_ge(S["EPI"].h, b0["EPI"] + st * 4 + tl - 3)
                    lg = lt_sb[:, tl * 8:(tl + 1) * 8]
                    v.reduce_max(m1r[:], lg, axis=AX)
                    v.drain()
                    v.tensor_scalar(out=eq1[:], in0=lg, scalar1=m1r[:], scalar2=None, op0=ALU.is_ge)
                    v.drain()
                    v.scalar_tensor_tensor(out=l2s[:], in0=eq1[:], scalar=-1e30, in1=lg,
                                           op0=ALU.mult, op1=ALU.add)
                    v.drain()
                    v.reduce_max(m2r[:], l2s[:], axis=AX)
                    v.drain()
                    v.tensor_scalar(out=eq2[:], in0=l2s[:], scalar1=m2r[:], scalar2=None, op0=ALU.is_ge)
                    S["HST"].inc(v.tensor_sub(d12[:], m1r[:], m2r[:]))
                    v.wait_ge(S["CBEV"].h, b0["CBEV"] + st * 4 + tl + 1)
                    v.tensor_scalar(out=w2r[:], in0=w1r[:], scalar1=-1.0, scalar2=1.0,
                                    op0=ALU.mult, op1=ALU.add)
                    v.drain()
                    v.tensor_scalar(out=eq1[:], in0=eq1[:], scalar1=w1r[:], scalar2=None, op0=ALU.mult)
                    v.tensor_scalar(out=eq2[:], in0=eq2[:], scalar1=w2r[:], scalar2=None, op0=ALU.mult)
                    v.drain()
                    S["CBMM"].inc(v.tensor_add(ctT[:, tl * 8:(tl + 1) * 8], eq1[:], eq2[:]))


def phase_C(nc, S, RES, ones_f, QZ, comb_scr_h, w1m, w2m, w3m,
            b1t_d, b2t_d, b3t_d, seq, tag):
    es = ExitStack()
    sbuf = lambda name, shape, dty=F32: es.enter_context(nc.sbuf_tensor(tag + name, shape, dty))
    psum = lambda name, shape, dty=F32: es.enter_context(nc.psum_tensor(tag + name, shape, dty))
    w1b = [sbuf(f"w1b{i}", [128, 2 * KC * FBW], BF16) for i in range(2)]
    w2b = [sbuf(f"w2b{i}", [128, 2 * KC * FBW], BF16) for i in range(2)]
    w3b = sbuf("w3b", [128, 8 * D], BF16)
    hpair = sbuf("hp", [128, 8 * FBW], BF16)
    s1 = [sbuf(f"s1{i}", [128, FBW]) for i in range(2)]
    tmp = [sbuf(f"tmp{i}", [128, FBW]) for i in range(4)]
    comb_bc = [sbuf(f"cbc{i}", [128, FBW]) for i in range(2)]
    b1t_s = sbuf("b1t", [128, NE * 32])
    b2t_s = sbuf("b2t", [128, NE * 32])
    b3t_s = sbuf("b3t", [128, NE * KC])
    psh1 = [psum(f"ph1{i}", [128, 512]) for i in range(2)]
    psh2 = [psum(f"ph2{i}", [128, 512]) for i in range(2)]
    pseo = [psum(f"peo{i}", [128, 512]) for i in range(4)]

    b0 = {k: s.n for k, s in S.items()}
    NU = 128

    with es, nc.Block() as blk:
        @blk.gpsimd
        def _(gp):
            for d, s_ in ((b1t_d, b1t_s), (b2t_d, b2t_s), (b3t_d, b3t_s)):
                S["MSK"].inc(gp.dma_start(out=s_[:], in_=d.rearrange("p a b -> p (a b)")), 16)
            for q in range(33):
                if q < 32:
                    e, p = q // 4, q % 4
                    if q >= 2:
                        gp.wait_ge(S["PAIRD"].h, b0["PAIRD"] + q - 1)
                    s1sem = S["WL1"] if q % 2 == 0 else S["W1B"]
                    s2sem = S["WL2"] if q % 2 == 0 else S["W2B"]
                    for s_ in range(2):
                        fb = 2 * p + s_
                        s1sem.inc(gp.dma_start(
                            out=w1b[q % 2][:, s_ * KC * FBW:(s_ + 1) * KC * FBW].rearrange(
                                "p (k c) -> p k c", k=KC),
                            in_=w1m[e, fb]), 16)
                    for s_ in range(2):
                        fb = 2 * p + s_
                        s2sem.inc(gp.dma_start(
                            out=w2b[q % 2][:, s_ * KC * FBW:(s_ + 1) * KC * FBW].rearrange(
                                "p (k c) -> p k c", k=KC),
                            in_=w2m[e, fb]), 16)
                if q >= 1:
                    qq = q - 1
                    for u in range(qq * 4, qq * 4 + 4):
                        e_, p_, tt = u // 16, (u // 4) % 4, u % 4
                        for gidx in range(8):
                            dcv = gidx
                            gp.wait_ge(S["EPI"].h, b0["EPI"] + 8 * u + gidx + 1)
                            dst = RES[:, dcv * SEQ + tt * FBW: dcv * SEQ + tt * FBW + FBW]
                            S["RADD"].inc(gp.tensor_add(dst, dst, tmp[gidx % 4][:]))
                if q < 32:
                    if q >= 1:
                        gp.wait_ge(S["EO"].h, b0["EO"] + 32 * q)
                    e, p = q // 4, q % 4
                    S["WL3"].inc(gp.dma_start(
                        out=w3b[:].rearrange("p (f k d) -> p f k d", f=2, k=4),
                        in_=w3m[e, 2 * p:2 * p + 2].rearrange("f p k d -> p f k d")), 16)

        @blk.sync
        def _(sy):
            for u in range(NU):
                e, tt = u // 16, u % 4
                if u >= 2:
                    sy.wait_ge(S["EPI"].h, b0["EPI"] + 8 * (u - 1))
                csem = S["CBROW"] if u % 2 == 0 else S["CROWB"]
                bcast = bass.AP(comb_scr_h, e * 2 * SEQ + seq * SEQ + tt * FBW,
                                [[0, 128], [1, FBW]])
                csem.inc(sy.dma_start(out=comb_bc[u % 2][:], in_=bcast), 16)

        @blk.tensor
        def _(t):
            t.wait_ge(S["MSK"].h, b0["MSK"] + 48)
            for u in range(NU):
                e, p, tt = u // 16, (u // 4) % 4, u % 4
                q = u // 4
                if tt == 0:
                    s1sem = "WL1" if q % 2 == 0 else "W1B"
                    s2sem = "WL2" if q % 2 == 0 else "W2B"
                    t.wait_ge(S[s1sem].h, b0[s1sem] + 32 * (q // 2 + 1))
                    t.wait_ge(S[s2sem].h, b0[s2sem] + 32 * (q // 2 + 1))
                for s_ in range(2):
                    for fc in range(4):
                        j = 8 * u + 4 * s_ + fc
                        if j >= 2:
                            t.wait_ge(S["SIL"].h, b0["SIL"] + j - 1)
                        for kc in range(KC):
                            ins = t.matmul(psh1[j % 2][:],
                                           w1b[q % 2][:, (s_ * KC + kc) * FBW + fc * 128:
                                                       (s_ * KC + kc) * FBW + fc * 128 + 128],
                                           QZ[:, kc * SEQ + tt * FBW: kc * SEQ + tt * FBW + FBW],
                                           start=(kc == 0), stop=(kc == KC - 1))
                        S["H1"].inc(ins, 1)
                        if j >= 2:
                            t.wait_ge(S["HST"].h, b0["HST"] + j - 1)
                        for kc in range(KC):
                            ins = t.matmul(psh2[j % 2][:],
                                           w2b[q % 2][:, (s_ * KC + kc) * FBW + fc * 128:
                                                       (s_ * KC + kc) * FBW + fc * 128 + 128],
                                           QZ[:, kc * SEQ + tt * FBW: kc * SEQ + tt * FBW + FBW],
                                           start=(kc == 0), stop=(kc == KC - 1))
                        S["H2"].inc(ins, 1)
                if tt == 3:
                    S["PAIRD"].inc(t.nop(), 1)
                t.wait_ge(S["HST"].h, b0["HST"] + 8 * (u + 1))
                t.wait_ge(S["WL3"].h, b0["WL3"] + 16 * (q + 1))
                for r in range(2):
                    t.wait_ge(S["EPI"].h, b0["EPI"] + 8 * u + 4 * r)
                    for dci in range(4):
                        dc = 4 * r + dci
                        for ks in range(8):
                            ins = t.matmul(pseo[dci][:],
                                           w3b[:, ks * D + dc * 128: ks * D + dc * 128 + 128],
                                           hpair[:, ks * FBW:(ks + 1) * FBW],
                                           start=(ks == 0), stop=(ks == KC - 1))
                        S["EO"].inc(ins, 1)

        @blk.scalar
        def _(sc):
            sc.wait_ge(S["MSK"].h, b0["MSK"] + 48)
            for u in range(NU):
                e, p, tt = u // 16, (u // 4) % 4, u % 4
                for s_ in range(2):
                    for fc in range(4):
                        j = 8 * u + 4 * s_ + fc
                        fbch = (2 * p + s_) * 4 + fc
                        sc.wait_ge(S["H1"].h, b0["H1"] + j + 1)
                        if j >= 2:
                            sc.wait_ge(S["HST"].h, b0["HST"] + j - 1)
                        S["SIL"].inc(sc.activation(out=s1[j % 2][:], in_=psh1[j % 2][:],
                                                   func=AF.Silu,
                                                   bias=b1t_s[:, e * 32 + fbch: e * 32 + fbch + 1]))

        @blk.vector
        def _(v):
            v.wait_ge(S["MSK"].h, b0["MSK"] + 48)
            for u in range(NU):
                e, p, tt = u // 16, (u // 4) % 4, u % 4
                for s_ in range(2):
                    for fc in range(4):
                        j = 8 * u + 4 * s_ + fc
                        fbch = (2 * p + s_) * 4 + fc
                        v.wait_ge(S["H2"].h, b0["H2"] + j + 1)
                        v.wait_ge(S["SIL"].h, b0["SIL"] + j + 1)
                        S["HST"].inc(v.scalar_tensor_tensor(
                            out=hpair[:, (s_ * 4 + fc) * FBW:(s_ * 4 + fc + 1) * FBW],
                            in0=psh2[j % 2][:],
                            scalar=b2t_s[:, e * 32 + fbch: e * 32 + fbch + 1],
                            in1=s1[j % 2][:], op0=ALU.add, op1=ALU.mult))
                for r in range(2):
                    for dci in range(4):
                        dc = 4 * r + dci
                        gidx = 8 * u + 4 * r + dci
                        v.wait_ge(S["EO"].h, b0["EO"] + gidx + 1)
                        if r == 0 and dci == 0:
                            csem = "CBROW" if u % 2 == 0 else "CROWB"
                            v.wait_ge(S[csem].h, b0[csem] + 16 * (u // 2 + 1))
                        if gidx >= 4:
                            v.wait_ge(S["RADD"].h, b0["RADD"] + gidx - 3)
                        if p == 0:
                            ins = v.scalar_tensor_tensor(
                                out=tmp[gidx % 4][:], in0=pseo[dci][:],
                                scalar=b3t_s[:, e * KC + dc: e * KC + dc + 1],
                                in1=comb_bc[u % 2][:], op0=ALU.add, op1=ALU.mult)
                        else:
                            ins = v.tensor_mul(tmp[gidx % 4][:], pseo[dci][:], comb_bc[u % 2][:])
                        S["EPI"].inc(ins)


def _tile_w(w):
    n = w.shape[1]
    return np.ascontiguousarray(w.reshape(KC, 128, n).transpose(1, 0, 2))


def kernel(L0_attn_hidden, L0_attn_Wq, L0_attn_Wk, L0_attn_Wv, L0_attn_Wo,
           L0_ffn_norm1_w, L0_ffn_Wq, L0_ffn_Wk, L0_ffn_Wv, L0_ffn_Wo,
           L0_ffn_norm2_w, L0_ffn_router_w, L0_ffn_router_b, L0_ffn_W1,
           L0_ffn_b1, L0_ffn_W2, L0_ffn_b2, L0_ffn_W3, L0_ffn_b3,
           _stop=None):
    sig = np.concatenate([np.arange(0, D, 2), np.arange(1, D, 2)])
    sinv = np.argsort(sig)

    x = np.asarray(L0_attn_hidden, dtype=np.float32)
    f32 = lambda a: np.asarray(a, dtype=np.float32)

    shared = {}
    shared["wq_o"] = _tile_w(np.ascontiguousarray(f32(L0_attn_Wq).T[sig]))
    shared["wk_o"] = _tile_w(np.ascontiguousarray(f32(L0_attn_Wk).T[sig]))
    shared["wv_o"] = _tile_w(np.ascontiguousarray(f32(L0_attn_Wv).T[sig]))
    shared["wo_o"] = _tile_w(np.ascontiguousarray(f32(L0_attn_Wo).T[:, sig]))
    qperm = np.empty(D, dtype=np.int64)
    for h_ in range(16):
        chn = (h_ % 4) + 4 * (h_ // 8)
        bas = 64 * ((h_ // 4) % 2)
        qperm[chn * 128 + bas: chn * 128 + bas + 64] = np.arange(64 * h_, 64 * h_ + 64)
    shared["wq_f"] = _tile_w(np.ascontiguousarray(f32(L0_ffn_Wq)[:, qperm]))
    shared["wk_f"] = _tile_w(f32(L0_ffn_Wk))
    shared["wv_f"] = _tile_w(f32(L0_ffn_Wv))
    shared["wo_f"] = _tile_w(np.ascontiguousarray(f32(L0_ffn_Wo)[:, sig]))
    shared["w1n"] = np.ascontiguousarray(f32(L0_ffn_norm1_w)[sig].reshape(KC, 128).T)
    shared["w2n"] = np.ascontiguousarray(f32(L0_ffn_norm2_w)[sig].reshape(KC, 128).T)
    shared["rw"] = _tile_w(np.ascontiguousarray(f32(L0_ffn_router_w)[sig]))
    shared["rb"] = np.ascontiguousarray(f32(L0_ffn_router_b).reshape(NE, 1))
    w1p = f32(L0_ffn_W1)[:, sig, :]
    w2p = f32(L0_ffn_W2)[:, sig, :]
    w3p = f32(L0_ffn_W3)[:, :, sig]
    shared["w1m"] = np.ascontiguousarray(
        w1p.reshape(NE, KC, 128, NFB, FBW).transpose(0, 3, 2, 1, 4))
    shared["w2m"] = np.ascontiguousarray(
        w2p.reshape(NE, KC, 128, NFB, FBW).transpose(0, 3, 2, 1, 4))
    shared["w3m"] = np.ascontiguousarray(
        w3p.reshape(NE, NFB, 4, 128, D).transpose(0, 1, 3, 2, 4))
    shared["b1t"] = np.ascontiguousarray(
        f32(L0_ffn_b1).reshape(NE, 32, 128).transpose(2, 0, 1))
    shared["b2t"] = np.ascontiguousarray(
        f32(L0_ffn_b2).reshape(NE, 32, 128).transpose(2, 0, 1))
    shared["b3t"] = np.ascontiguousarray(
        f32(L0_ffn_b3)[:, sig].reshape(NE, KC, 128).transpose(2, 0, 1))
    half = D // 2
    theta = (1.0 / (np.float32(10000.0) **
                    (np.arange(half, dtype=np.float32) / np.float32(half)))).astype(np.float32)
    pos = np.arange(SEQ, dtype=np.float32)[:, None]
    ang = (pos * theta[None, :]).astype(np.float32)
    cos = np.cos(ang).astype(np.float32).T
    sin = np.sin(ang).astype(np.float32).T
    shared["cos_t"] = np.ascontiguousarray(cos.reshape(4, 128, SEQ).transpose(1, 0, 2))
    shared["sin_t"] = np.ascontiguousarray(sin.reshape(4, 128, SEQ).transpose(1, 0, 2))
    shared["id_f"] = np.eye(128, dtype=np.float32)
    shared["ones_bf"] = np.ones((128, 1), dtype=ml_dtypes.bfloat16)
    shared["ones_f"] = np.ones((1, 128), dtype=np.float32)
    p = np.arange(128)[:, None]
    c0 = np.arange(128)[None, :]
    cD = np.arange(512)[None, :]
    mD = np.stack([np.where(cD <= dlt + p, 0.0, -1e9).astype(ml_dtypes.bfloat16)
                   for dlt in (0, 128, 256, 384)], axis=1)
    shared["mD"] = np.ascontiguousarray(mD)
    mT = np.stack([np.where(cD >= dlt + p, 0.0, -1e9).astype(ml_dtypes.bfloat16)
                   for dlt in (0, 128, 256, 384)], axis=1)
    shared["mT"] = np.ascontiguousarray(mT)
    mA = np.stack([np.where((cD >= dlt + p) & (cD <= dlt + p + 255), 0.0, -1e9).astype(ml_dtypes.bfloat16)
                   for dlt in (-256, -128, 0, 128, 256, 384)], axis=1)
    shared["mA"] = np.ascontiguousarray(mA)

    in_maps = []
    for c in range(NCORES):
        xc = x[2 * c:2 * c + 2]
        xc = xc.transpose(2, 0, 1).reshape(D, 2 * SEQ)[sig]
        xc = np.ascontiguousarray(xc.reshape(KC, 128, 2 * SEQ).transpose(1, 0, 2))
        m = dict(shared)
        m["x"] = xc
        in_maps.append(m)

    nc = build_program(stop_phase=_stop)
    res = run_bass_kernel_spmd(nc, in_maps, core_ids=list(range(NCORES)))

    out = np.empty((16, SEQ, D), dtype=np.float32)
    for c in range(NCORES):
        oc = res.results[c]["out"]
        oc = oc.transpose(1, 0, 2).reshape(D, 2, SEQ)
        out[2 * c:2 * c + 2] = oc[sinv].transpose(1, 2, 0)
    return out


# version 10
# speedup vs baseline: 1.0951x; 1.0209x over previous
import numpy as np
import ml_dtypes
import concourse.bass as bass
import concourse.mybir as mybir
from concourse.bass_utils import run_bass_kernel_spmd
from contextlib import ExitStack

F32 = mybir.dt.float32
BF16 = mybir.dt.bfloat16
AF = mybir.ActivationFunctionType
ALU = mybir.AluOpType
AX = mybir.AxisListType.X

NCORES = 8
SEQ = 2048
D = 1024
KC = 8
NH = 16
DK = 64
WIN = 256
NST = 4
STW = 512
NQT = 16
FFKV = 256
NE = 8
DFF = 4096
NFB = 8
FBW = 512


class SemT:
    __slots__ = ("h", "n")

    def __init__(self, h):
        self.h = h
        self.n = 0

    def inc(self, instr, k=1):
        instr.then_inc(self.h, k)
        self.n += k
        return instr


def a_nch(Q):
    return min(Q + 1, 3)


def qloc(h):
    return (h % 4) + 4 * (h // 8), 64 * ((h // 4) % 2)


def build_program(stop_phase=None, nseqs=2):
    nc = bass.Bass("TRN2", target_bir_lowering=False)

    dt = lambda name, shape, dty=F32, kind="ExternalInput": nc.dram_tensor(name, shape, dty, kind=kind).ap()
    x_d = dt("x", [128, KC, 2 * SEQ])
    out_d = dt("out", [128, KC, 2 * SEQ], F32, "ExternalOutput")
    wq_o = dt("wq_o", [128, KC, D])
    wk_o = dt("wk_o", [128, KC, D])
    wv_o = dt("wv_o", [128, KC, D])
    wo_o = dt("wo_o", [128, KC, D])
    wq_f = dt("wq_f", [128, KC, D])
    wk_f = dt("wk_f", [128, KC, FFKV])
    wv_f = dt("wv_f", [128, KC, FFKV])
    wo_f = dt("wo_f", [128, KC, D])
    w1n_d = dt("w1n", [128, KC])
    w2n_d = dt("w2n", [128, KC])
    rw_d = dt("rw", [128, KC, NE])
    rb_d = dt("rb", [NE, 1])
    w1m = dt("w1m", [NE, NFB, 128, KC, FBW])
    w2m = dt("w2m", [NE, NFB, 128, KC, FBW])
    w3m = dt("w3m", [NE, NFB, 128, 4, D])
    b1t_d = dt("b1t", [128, NE, 32])
    b2t_d = dt("b2t", [128, NE, 32])
    b3t_d = dt("b3t", [128, NE, KC])
    cos_d = dt("cos_t", [128, 4, SEQ])
    sin_d = dt("sin_t", [128, 4, SEQ])
    id_f_d = dt("id_f", [128, 128])
    ones_bf_d = dt("ones_bf", [128, 1], BF16)
    ones_f_d = dt("ones_f", [1, 128])
    comb_scr_h = nc.dram_tensor("comb_scr", [NE, 2 * SEQ], F32, kind="Internal")
    comb_scr = comb_scr_h.ap()
    mD_d = dt("mD", [128, 4, 512], BF16)
    mT_d = dt("mT", [128, 4, 512], BF16)
    mA_d = dt("mA", [128, 6, 512], BF16)

    g = ExitStack()
    sbuf = lambda name, shape, dty=F32: g.enter_context(nc.sbuf_tensor(name, shape, dty))
    RES = sbuf("RES", [128, KC * SEQ])
    id_bf = None
    id_f = sbuf("id_f_s", [128, 128])
    ones_bf = sbuf("ones_bf_s", [128, 1], BF16)
    ones_f = sbuf("ones_f_s", [1, 128])
    w1n = sbuf("w1n_s", [128, KC])
    w2n = sbuf("w2n_s", [128, KC])
    rb_sb = sbuf("rb_s", [NE, 1])

    sems = {}
    for nm in ["LD", "CAST", "STAT", "ROWEV", "ROW1", "RSTD", "MR", "BC", "Y0",
               "PMM", "PEV", "SMM", "SEV", "MSK", "NM", "P", "PN", "PTMM",
               "PTEV", "AVMM", "OEV", "WMM", "X1", "WL1", "WL2", "WL3",
               "H1", "H2", "EO", "SIL", "HST", "EPI", "CBROW", "CBMM", "CBEV",
               "PAIRD", "STO", "AUX", "AUX2", "AUX3", "RADD", "T1", "ZC", "RMM",
               "W1B", "W2B", "CROWB", "SEVD", "CTM", "CRC", "CBM", "CBC2",
               "STM", "STEV", "PEXP", "AVM"]:
        sems[nm] = SemT(g.enter_context(nc.semaphore(nm)))

    S = sems

    def base():
        return {k: s.n for k, s in sems.items()}

    def store_block(seq):
        with nc.Block() as blk:
            @blk.sync
            def _(sy):
                S["STO"].inc(
                    sy.dma_start(out=out_d[:, :, seq * SEQ:(seq + 1) * SEQ],
                                 in_=RES[:].rearrange("p (k t) -> p k t", k=KC)), 16)

    with g:
        for seq in range(nseqs):
            tok0 = seq * SEQ

            with nc.Block() as blk:
                @blk.sync
                def _(sy, tok0=tok0, seq=seq):
                    if seq > 0:
                        sy.wait_ge(S["STO"].h, S["STO"].n)
                    S["LD"].inc(sy.dma_start(out=RES[:].rearrange("p (k t) -> p k t", k=KC),
                                             in_=x_d[:, :, tok0:tok0 + SEQ]), 16)
                    if seq == 0:
                        for dst, srcd in ((id_f, id_f_d),
                                          (ones_bf, ones_bf_d), (ones_f, ones_f_d),
                                          (w1n, w1n_d), (w2n, w2n_d), (rb_sb, rb_d)):
                            S["LD"].inc(sy.dma_start(out=dst[:], in_=srcd), 16)
                    sy.wait_ge(S["LD"].h, S["LD"].n)

            phase_A(nc, S, RES, id_bf, id_f, ones_bf, ones_f,
                    wq_o, wk_o, wv_o, wo_o, mA_d, tag=f"s{seq}")

            if stop_phase == "A":
                store_block(seq)
                continue

            seq_es = ExitStack()
            QZ = seq_es.enter_context(nc.sbuf_tensor(f"s{seq}QZ", [128, KC * SEQ], BF16))
            with seq_es:
                b_es = ExitStack()
                Kf = b_es.enter_context(nc.sbuf_tensor(f"s{seq}Kf", [128, 2 * SEQ], BF16))
                Vf = b_es.enter_context(nc.sbuf_tensor(f"s{seq}Vf", [128, 16 * FFKV], BF16))
                with b_es:
                    phase_B1(nc, S, RES, ones_bf, ones_f, w1n, QZ, Kf, Vf,
                             wq_f, wk_f, wv_f, cos_d, sin_d, tag=f"s{seq}B1")
                    phase_B2(nc, S, RES, id_bf, id_f, ones_bf, ones_f, w2n, rb_sb,
                             QZ, Kf, Vf, comb_scr, seq, wo_f, rw_d, mD_d, mT_d,
                             tag=f"s{seq}B2")
                if stop_phase == "B":
                    store_block(seq)
                    continue
                phase_C(nc, S, RES, ones_f, QZ, comb_scr_h, w1m, w2m, w3m,
                        b1t_d, b2t_d, b3t_d, seq, tag=f"s{seq}C")
                store_block(seq)

        with nc.Block() as blk:
            @blk.sync
            def _(sy):
                for nm in ("STO", "LD", "WL1", "WL2", "WL3", "W1B", "W2B",
                           "MSK", "CBROW", "CROWB", "EO"):
                    if S[nm].n:
                        sy.wait_ge(S[nm].h, S[nm].n)

    return nc


def phase_A(nc, S, RES, id_bf, id_f, ones_bf, ones_f, wq_o, wk_o, wv_o, wo_o,
            mA_d, tag=""):
    es = ExitStack()
    sbuf = lambda name, shape, dty=F32: es.enter_context(nc.sbuf_tensor(tag + name, shape, dty))
    psum = lambda name, shape, dty=F32: es.enter_context(nc.psum_tensor(tag + name, shape, dty))
    wq_s = sbuf("Awq", [128, KC * D], BF16)
    wk_s = sbuf("Awk", [128, KC * D], BF16)
    wv_s = sbuf("Awv", [128, KC * D], BF16)
    wo_s = sbuf("Awo", [128, KC * D], BF16)
    Kb = [sbuf(f"AK{i}", [128, KC * STW], BF16) for i in range(2)]
    Vb = [sbuf(f"AV{i}", [128, 4 * D], BF16) for i in range(2)]
    Qst = sbuf("AQ", [128, KC * STW], BF16)
    Y0 = sbuf("AY0", [128, KC * STW], BF16)
    Ost = sbuf("AO", [128, KC * STW], BF16)

    STs = [sbuf(f"ASTs{i}", [128, 512]) for i in range(2)]
    PT2 = [sbuf(f"APT2{i}", [128, 512], BF16) for i in range(2)]
    c_col = sbuf("Accol", [128, 8])
    c_row = sbuf("Acrow", [1, 512])
    c_bcs = sbuf("Acbcs", [128, 512])


    mA_s = sbuf("AmA", [128, 6 * 512], BF16)
    mF_s = mA_s[:, 3 * 512 + 127: 3 * 512 + 511]
    m0_s = mA_s[:, 3 * 512 + 383: 3 * 512 + 511]
    m1_s = mA_s[:, 3 * 512 + 255: 3 * 512 + 511]
    r_mean = sbuf("Arm", [1, STW])
    r_msq = sbuf("Arq", [1, STW])
    r_var = c_row
    r_rstd = r_msq
    r_nmax = [sbuf(f"Anm{i}", [128, 1]) for i in range(2)]
    r_sum = [sbuf(f"Asum{i}", [128, 1]) for i in range(2)]
    r_lns = [sbuf(f"Alns{i}", [128, 1]) for i in range(2)]
    psC = psum("ApsC", [128, 512])
    psD = psum("ApsD", [128, 512])
    psS0 = psum("ApsS", [128, 384])
    STb = [psum(f"AST{i}", [128, 512]) for i in range(2)]
    AVb = psum("AAV", [64, 512])
    CBCb = psum("ACBC", [128, 512])
    miscb = psum("Amisc", [128, 512])
    ps_crow = miscb[0:1, :]

    xb = [Y0[:, 0:STW], Y0[:, STW:2 * STW]]
    xq = [Y0[:, 2 * STW:3 * STW], Y0[:, 3 * STW:4 * STW]]

    psS = [psS0, psC]

    b0 = {k: s.n for k, s in S.items()}

    def kt_range(st):
        kts = list(range(max(0, 4 * st - 2), 4 * st + 4))
        kts.remove(4 * st)
        return [4 * st] + kts

    def kt_cols(st, kt):
        di = kt - 4 * st + 2
        c0, c1 = {0: (0, 128), 1: (0, 256), 2: (0, 512), 3: (0, 512),
                  4: (256, 512), 5: (384, 512)}[di]
        return di, c0, c1 - c0

    v_nkt = [len(kt_range(st)) for st in range(NST) for h in range(NH)]
    kt_pref = [0]
    for n in v_nkt:
        kt_pref.append(kt_pref[-1] + n)

    with es, nc.Block() as blk:
        @blk.gpsimd
        def _(gp):
            for w_d, w_s in ((wq_o, wq_s), (wk_o, wk_s), (wv_o, wv_s), (wo_o, wo_s)):
                S["WL1"].inc(gp.dma_start(out=w_s[:], in_=w_d.rearrange("p k d -> p (k d)")), 16)
            S["WL1"].inc(gp.dma_start(out=mA_s[:].rearrange("p (i c) -> p i c", i=6), in_=mA_d), 16)

        @blk.tensor
        def _(t):
            t.wait_ge(S["WL1"].h, b0["WL1"] + 16 * 5)

            def stats_pe(st, h):
                for qt in range(4):
                    w = (st * 16 + h) * 4 + qt
                    Q = st * 4 + qt
                    nch = a_nch(Q)
                    kt0 = Q - nch + 1
                    if w >= 2:
                        t.wait_ge(S["P"].h, b0["P"] + w - 1)
                    col = 0
                    seg_start = kt0
                    lhs = Qst[64 * (h % 2):64 * (h % 2) + 64,
                              (h // 2) * STW + qt * 128:(h // 2) * STW + qt * 128 + 128]
                    while seg_start <= Q:
                        seg_st = seg_start // 4
                        seg_end = min(Q, seg_st * 4 + 3)
                        n = (seg_end - seg_start + 1)
                        kb = Kb[seg_st % 2]
                        rel = seg_start - seg_st * 4
                        rhs = kb[64 * (h % 2):64 * (h % 2) + 64,
                                 (h // 2) * STW + rel * 128:(h // 2) * STW + rel * 128 + n * 128]
                        ins = t.matmul(psS[w % 2][:, col:col + n * 128], lhs, rhs,
                                       start=True, stop=True)
                        col += n * 128
                        seg_start = seg_end + 1
                    S["SMM"].inc(ins, 1)

            def t2_pe(st, h):
                v = st * 16 + h
                t.wait_ge(S["PN"].h, b0["PN"] + 4 * (v + 1))
                if v >= 1:
                    t.wait_ge(S["CRC"].h, b0["CRC"] + v)
                for qt in range(4):
                    ins = t.transpose(ps_crow[0:1, qt * 128:(qt + 1) * 128],
                                      c_col[:, (v % 2) * 4 + qt:(v % 2) * 4 + qt + 1], id_f[:])
                S["CTM"].inc(ins, 1)
                t.wait_ge(S["CRC"].h, b0["CRC"] + v + 1)
                if v >= 1:
                    t.wait_ge(S["CBC2"].h, b0["CBC2"] + v)
                S["CBM"].inc(t.matmul(CBCb[:], ones_f[:], c_row[:], start=True, stop=True), 1)
                kts = list(kt_range(st))
                nkt = len(kts)
                for i in range(nkt + 2):
                    if i < nkt:
                        kt = kts[i]
                        di, c0, wc = kt_cols(st, kt)
                        m = kt_pref[v] + i
                        if m >= 2:
                            t.wait_ge(S["STEV"].h, b0["STEV"] + m - 1)
                        s_k = kt // 4
                        rhs = Qst[64 * (h % 2):64 * (h % 2) + 64,
                                  (h // 2) * STW + c0:(h // 2) * STW + c0 + wc]
                        lhsK = Kb[s_k % 2][64 * (h % 2):64 * (h % 2) + 64,
                                           (h // 2) * STW + (kt % 4) * 128:
                                           (h // 2) * STW + (kt % 4) * 128 + 128]
                        S["STM"].inc(t.matmul(STb[m % 2][:, c0:c0 + wc], lhsK, rhs,
                                              start=True, stop=True), 1)
                    if i >= 2:
                        ka = kts[i - 2]
                        di, c0, wc = kt_cols(st, ka)
                        m = kt_pref[v] + i - 2
                        t.wait_ge(S["PEXP"].h, b0["PEXP"] + m + 1)
                        if i == 2:
                            t.wait_ge(S["OEV"].h, b0["OEV"] + v)
                        s_k = ka // 4
                        S["AVM"].inc(t.matmul(AVb[0:64, c0:c0 + wc],
                                              Vb[s_k % 2][:, (ka % 4) * D + 64 * h:
                                                          (ka % 4) * D + 64 * h + 64],
                                              PT2[m % 2][:, c0:c0 + wc],
                                              start=(i == 2), stop=(i == nkt + 1)), 1)

            for st in range(NST):
                stb = st * STW
                if st >= 1:
                    t.wait_ge(S["Y0"].h, b0["Y0"] + st * 8)
                    t.wait_ge(S["X1"].h, b0["X1"] + st * 8)
                    t.wait_ge(S["P"].h, b0["P"] + st * 64)
                for kc in range(KC):
                    t.wait_ge(S["CAST"].h, b0["CAST"] + st * 16 + kc * 2 + 2)
                    i1 = t.matmul(psC[0:1, :], ones_bf[:], xb[kc % 2],
                                  start=(kc == 0), stop=(kc == KC - 1))
                    i2 = t.matmul(psD[0:1, :], ones_bf[:], xq[kc % 2],
                                  start=(kc == 0), stop=(kc == KC - 1))
                    S["STAT"].inc(i1, 1)
                    S["STAT"].inc(i2, 1)
                t.wait_ge(S["MR"].h, b0["MR"] + st + 1)
                i1 = t.matmul(psC[:], ones_f[:], r_rstd[:], start=True, stop=True)
                i2 = t.matmul(psD[:], ones_f[:], r_mean[:], start=True, stop=True)
                S["BC"].inc(i2, 1)
                t.wait_ge(S["Y0"].h, b0["Y0"] + st * 8 + 8)
                for gi in range(24):
                    pb = (psC, psD)[gi % 2]
                    if gi >= 2:
                        t.wait_ge(S["PEV"].h, b0["PEV"] + st * 24 + gi - 1)
                    if gi < 16:
                        w_s = wq_s if gi < 8 else wk_s
                        oc = gi % 8
                        for kc in range(KC):
                            ins = t.matmul(pb[:], w_s[:, kc * D + oc * 128: kc * D + oc * 128 + 128],
                                           Y0[:, kc * STW:(kc + 1) * STW],
                                           start=(kc == 0), stop=(kc == KC - 1))
                    else:
                        vi = gi - 16
                        tsub, half = vi // 2, vi % 2
                        for kc in range(KC):
                            ins = t.matmul(pb[:],
                                           Y0[:, kc * STW + tsub * 128: kc * STW + tsub * 128 + 128],
                                           wv_s[:, kc * D + half * 512: kc * D + half * 512 + 512],
                                           start=(kc == 0), stop=(kc == KC - 1))
                    S["PMM"].inc(ins, 1)
                t.wait_ge(S["PEV"].h, b0["PEV"] + st * 24 + 24)
                for h in range(NH):
                    stats_pe(st, h)
                    if h >= 1:
                        t2_pe(st, h - 1)
                t2_pe(st, NH - 1)
                t.wait_ge(S["OEV"].h, b0["OEV"] + st * 16 + 16)
                for oc in range(KC):
                    pb = (psC, psD)[oc % 2]
                    if oc >= 2:
                        t.wait_ge(S["X1"].h, b0["X1"] + st * 8 + oc - 1)
                    for kc in range(KC):
                        ins = t.matmul(pb[:], wo_s[:, kc * D + oc * 128: kc * D + oc * 128 + 128],
                                       Ost[:, kc * STW:(kc + 1) * STW],
                                       start=(kc == 0), stop=(kc == KC - 1))
                    S["WMM"].inc(ins, 1)

        @blk.scalar
        def _(sc):
            sc.wait_ge(S["WL1"].h, b0["WL1"] + 16 * 5)

            def stats_act(st, h):
                for qt in range(4):
                    w = (st * 16 + h) * 4 + qt
                    Q = st * 4 + qt
                    nch = a_nch(Q)
                    wdt = nch * 128
                    sc.wait_ge(S["NM"].h, b0["NM"] + w + 1)
                    sc.activation(out=psS[w % 2][:, 0:wdt], in_=psS[w % 2][:, 0:wdt],
                                  func=AF.Exp, bias=r_nmax[w % 2][:], accum_out=r_sum[w % 2][:])
                    sc.drain()
                    S["P"].inc(sc.activation(out=r_lns[w % 2][:], in_=r_sum[w % 2][:], func=AF.Ln))

            def t2_act(st, h):
                v = st * 16 + h
                sc.wait_ge(S["CTM"].h, b0["CTM"] + v + 1)
                S["CRC"].inc(sc.activation(out=c_row[:], in_=ps_crow, func=AF.Copy))
                sc.wait_ge(S["CBM"].h, b0["CBM"] + v + 1)
                if v >= 1:
                    sc.wait_ge(S["STEV"].h, b0["STEV"] + kt_pref[v])
                S["CBC2"].inc(sc.activation(out=c_bcs[:], in_=CBCb[:], func=AF.Copy))
                nkt = v_nkt[v]
                kts = list(kt_range(st))
                for i in range(nkt):
                    kt = kts[i]
                    di, c0, wc = kt_cols(st, kt)
                    m = kt_pref[v] + i
                    sc.wait_ge(S["STEV"].h, b0["STEV"] + m + 1)
                    if m >= 2:
                        sc.wait_ge(S["AVM"].h, b0["AVM"] + m - 1)
                    S["PEXP"].inc(sc.activation(out=PT2[m % 2][:, c0:c0 + wc],
                                                in_=STs[m % 2][:, c0:c0 + wc], func=AF.Exp))
                sc.wait_ge(S["AVM"].h, b0["AVM"] + kt_pref[v + 1])
                dst = Ost[64 * (h % 2):64 * (h % 2) + 64, (h // 2) * STW:(h // 2 + 1) * STW]
                S["OEV"].inc(sc.activation(out=dst, in_=AVb[:], func=AF.Copy))

            for st in range(NST):
                stb = st * STW
                if st >= 1:
                    sc.wait_ge(S["PMM"].h, b0["PMM"] + st * 24)
                for kc in range(KC):
                    if kc >= 2:
                        sc.wait_ge(S["STAT"].h, b0["STAT"] + st * 16 + (kc - 2) * 2 + 2)
                    src2 = RES[:, kc * SEQ + stb: kc * SEQ + stb + STW]
                    S["CAST"].inc(sc.activation(out=xb[kc % 2], in_=src2, func=AF.Copy))
                    S["CAST"].inc(sc.activation(out=xq[kc % 2], in_=src2, func=AF.Square))
                sc.wait_ge(S["STAT"].h, b0["STAT"] + st * 16 + 16)
                sc.activation(out=r_mean[:], in_=psC[0:1, :], func=AF.Copy, scale=1.0 / 1024.0)
                S["ROWEV"].inc(sc.activation(out=r_msq[:], in_=psD[0:1, :], func=AF.Copy, scale=1.0 / 1024.0))
                sc.wait_ge(S["ROW1"].h, b0["ROW1"] + st + 1)
                sc.activation(out=r_var[:], in_=r_var[:], func=AF.Ln)
                sc.drain()
                S["RSTD"].inc(sc.activation(out=r_rstd[:], in_=r_var[:], func=AF.Exp, scale=-0.5))
                for gi in range(24):
                    sc.wait_ge(S["PMM"].h, b0["PMM"] + st * 24 + gi + 1)
                    pb = (psC, psD)[gi % 2]
                    if gi < 8:
                        dest = Qst[:, gi * STW:(gi + 1) * STW]
                        ins = sc.activation(out=dest, in_=pb[:], func=AF.Copy, scale=0.125)
                    elif gi < 16:
                        oc = gi - 8
                        dest = Kb[st % 2][:, oc * STW:(oc + 1) * STW]
                        ins = sc.activation(out=dest, in_=pb[:], func=AF.Copy)
                    else:
                        vi = gi - 16
                        tsub, half = vi // 2, vi % 2
                        dest = Vb[st % 2][:, tsub * D + half * 512: tsub * D + half * 512 + 512]
                        ins = sc.activation(out=dest, in_=pb[:], func=AF.Copy)
                    S["PEV"].inc(ins)
                for h in range(NH):
                    stats_act(st, h)
                    if h >= 1:
                        t2_act(st, h - 1)
                t2_act(st, NH - 1)

        @blk.vector
        def _(v):
            v.wait_ge(S["WL1"].h, b0["WL1"] + 16 * 5)

            def c_write(w2):
                vv, qt2 = w2 // 4, w2 % 4
                v.wait_ge(S["P"].h, b0["P"] + w2 + 1)
                if qt2 == 0 and vv >= 2:
                    v.wait_ge(S["CTM"].h, b0["CTM"] + vv - 1)
                S["PN"].inc(v.tensor_sub(c_col[:, (vv % 2) * 4 + qt2:(vv % 2) * 4 + qt2 + 1],
                                         r_nmax[w2 % 2][:], r_lns[w2 % 2][:]))

            def stats_dve(st, h):
                for qt in range(4):
                    w = (st * 16 + h) * 4 + qt
                    Q = st * 4 + qt
                    nch = a_nch(Q)
                    wdt = nch * 128
                    msk = (m0_s, m1_s, mF_s)[nch - 1]
                    v.wait_ge(S["SMM"].h, b0["SMM"] + w + 1)
                    if w >= 2:
                        v.wait_ge(S["P"].h, b0["P"] + w - 1)
                    v.tensor_add(psS[w % 2][:, 0:wdt], psS[w % 2][:, 0:wdt], msk)
                    v.drain()
                    S["NM"].inc(v.reduce_max(r_nmax[w % 2][:], psS[w % 2][:, 0:wdt],
                                             axis=AX, negate=True))
                    if w >= 1:
                        c_write(w - 1)

            def t2_dve(st, h):
                v2 = st * 16 + h
                kts = list(kt_range(st))
                for i, kt in enumerate(kts):
                    di, c0, wc = kt_cols(st, kt)
                    m = kt_pref[v2] + i
                    v.wait_ge(S["STM"].h, b0["STM"] + m + 1)
                    v.wait_ge(S["CBC2"].h, b0["CBC2"] + v2 + 1)
                    if m >= 2:
                        v.wait_ge(S["PEXP"].h, b0["PEXP"] + m - 1)
                    v.tensor_add(STs[m % 2][:, c0:c0 + wc], STb[m % 2][:, c0:c0 + wc],
                                 c_bcs[:, c0:c0 + wc])
                    v.drain()
                    S["STEV"].inc(v.tensor_add(STs[m % 2][:, c0:c0 + wc],
                                               STs[m % 2][:, c0:c0 + wc],
                                               mA_s[:, di * 512 + c0: di * 512 + c0 + wc]))

            for st in range(NST):
                stb = st * STW
                v.wait_ge(S["ROWEV"].h, b0["ROWEV"] + st + 1)
                if st >= 1:
                    v.wait_ge(S["CBM"].h, b0["CBM"] + 16 * st)
                v.tensor_mul(r_var[:], r_mean[:], r_mean[:])
                v.drain()
                S["ROW1"].inc(v.scalar_tensor_tensor(out=r_var[:], in0=r_msq[:],
                                                     scalar=1e-5, in1=r_var[:],
                                                     op0=ALU.add, op1=ALU.subtract))
                v.wait_ge(S["RSTD"].h, b0["RSTD"] + st + 1)
                S["MR"].inc(v.tensor_mul(r_mean[:], r_mean[:], r_rstd[:]))
                v.wait_ge(S["BC"].h, b0["BC"] + st + 1)
                if st >= 1:
                    v.wait_ge(S["PEXP"].h, b0["PEXP"] + kt_pref[16 * st])
                for kc in range(KC):
                    src2 = RES[:, kc * SEQ + stb: kc * SEQ + stb + STW]
                    v.tensor_mul(STs[0][:], src2, psC[:])
                    v.drain()
                    S["Y0"].inc(v.tensor_sub(Y0[:, kc * STW:(kc + 1) * STW], STs[0][:], psD[:]))
                    v.drain()
                for h in range(NH):
                    stats_dve(st, h)
                    if h == NH - 1:
                        c_write((st * 16 + h) * 4 + 3)
                    if h >= 1:
                        t2_dve(st, h - 1)
                t2_dve(st, NH - 1)
                for oc in range(KC):
                    pb = (psC, psD)[oc % 2]
                    v.wait_ge(S["WMM"].h, b0["WMM"] + st * 8 + oc + 1)
                    dst = RES[:, oc * SEQ + stb: oc * SEQ + stb + STW]
                    S["X1"].inc(v.tensor_add(dst, dst, pb[:]))


def phase_B1(nc, S, RES, ones_bf, ones_f, w1n, QZ, Kf, Vf,
             wq_f, wk_f, wv_f, cos_d, sin_d, tag):
    es = ExitStack()
    sbuf = lambda name, shape, dty=F32: es.enter_context(nc.sbuf_tensor(tag + name, shape, dty))
    psum = lambda name, shape, dty=F32: es.enter_context(nc.psum_tensor(tag + name, shape, dty))
    wqf_s = sbuf("wqf", [128, KC * D], BF16)
    wkf_s = sbuf("wkf", [128, KC * FFKV], BF16)
    wvf_s = sbuf("wvf", [128, KC * FFKV], BF16)
    cos_s = sbuf("cos", [128, 4 * STW])
    sin_s = sbuf("sin", [128, 4 * STW])
    y_st = sbuf("y", [128, KC * STW])
    xr = sbuf("xr", [128, KC * STW], BF16)
    xb = [sbuf(f"xb{i}", [128, STW], BF16) for i in range(2)]
    xq = [sbuf(f"xq{i}", [128, STW], BF16) for i in range(2)]
    zE = sbuf("zE", [128, STW])
    zO = sbuf("zO", [128, STW])
    tA = sbuf("tA", [128, STW])
    tB = sbuf("tB", [128, STW])
    tC = sbuf("tC", [128, STW])
    tD = sbuf("tD", [128, STW])
    r_mean = sbuf("rm", [1, STW])
    r_msq = sbuf("rq", [1, STW])
    r_var = sbuf("rv", [1, STW])
    r_rstd = sbuf("rr", [1, STW])
    psA = psum("psA", [128, 512])
    psB = psum("psB", [128, 512])
    psC = psum("psC", [128, 512])
    psD = psum("psD", [128, 512])

    b0 = {k: s.n for k, s in S.items()}
    NPJ = 14

    with es, nc.Block() as blk:
        @blk.gpsimd
        def _(gp):
            for w_d, w_s in ((wq_f, wqf_s), (wk_f, wkf_s), (wv_f, wvf_s)):
                S["WL1"].inc(gp.dma_start(out=w_s[:], in_=w_d.rearrange("p k d -> p (k d)")), 16)

        @blk.sync
        def _(sy):
            for st in range(NST):
                if st >= 1:
                    sy.wait_ge(S["T1"].h, b0["T1"] + st * 8)
                S["LD"].inc(sy.dma_start(out=cos_s[:].rearrange("p (c t) -> p c t", c=4),
                                         in_=cos_d[:, :, st * STW:(st + 1) * STW]), 16)
                S["LD"].inc(sy.dma_start(out=sin_s[:].rearrange("p (c t) -> p c t", c=4),
                                         in_=sin_d[:, :, st * STW:(st + 1) * STW]), 16)

        @blk.scalar
        def _(sc):
            for st in range(NST):
                stb = st * STW
                if st >= 1:
                    sc.wait_ge(S["STAT"].h, b0["STAT"] + st * 24)
                for kc in range(KC):
                    if kc >= 2:
                        sc.wait_ge(S["STAT"].h, b0["STAT"] + st * 24 + (kc - 2) * 2 + 2)
                    src = RES[:, kc * SEQ + stb: kc * SEQ + stb + STW]
                    S["CAST"].inc(sc.activation(out=xb[kc % 2][:], in_=src, func=AF.Copy))
                    S["CAST"].inc(sc.activation(out=xq[kc % 2][:], in_=src, func=AF.Square))
                sc.wait_ge(S["STAT"].h, b0["STAT"] + st * 24 + 16)
                sc.activation(out=r_mean[:], in_=psA[0:1, :], func=AF.Copy, scale=1.0 / 1024.0)
                S["ROWEV"].inc(sc.activation(out=r_msq[:], in_=psB[0:1, :], func=AF.Copy, scale=1.0 / 1024.0))
                sc.wait_ge(S["ROW1"].h, b0["ROW1"] + st * 2 + 1)
                sc.activation(out=r_var[:], in_=r_var[:], func=AF.Ln)
                sc.drain()
                S["RSTD"].inc(sc.activation(out=r_rstd[:], in_=r_var[:], func=AF.Exp, scale=-0.5))
                for kc in range(KC):
                    sc.wait_ge(S["Y0"].h, b0["Y0"] + st * 8 + kc + 1)
                    if kc >= 2:
                        sc.wait_ge(S["STAT"].h, b0["STAT"] + st * 24 + 16 + (kc - 2) + 1)
                    S["CAST"].inc(sc.activation(out=xq[kc % 2][:],
                                                in_=y_st[:, kc * STW:(kc + 1) * STW],
                                                func=AF.Square))
                sc.wait_ge(S["ROW1"].h, b0["ROW1"] + st * 2 + 2)
                sc.activation(out=r_var[:], in_=r_var[:], func=AF.Ln)
                sc.drain()
                S["RSTD"].inc(sc.activation(out=r_rstd[:], in_=r_var[:], func=AF.Exp, scale=-0.5))
                for gi in range(NPJ):
                    sc.wait_ge(S["PMM"].h, b0["PMM"] + st * NPJ + gi + 1)
                    pb = (psC, psD)[gi % 2]
                    if gi < 8:
                        ins = sc.activation(out=QZ[:, gi * SEQ + stb: gi * SEQ + stb + STW],
                                            in_=pb[:], func=AF.Copy, scale=0.125)
                    elif gi < 10:
                        oc = gi - 8
                        ins = sc.activation(out=Kf[:, oc * SEQ + stb: oc * SEQ + stb + STW],
                                            in_=pb[:], func=AF.Copy)
                    else:
                        tsub = gi - 10
                        ins = sc.activation(out=Vf[:, (st * 4 + tsub) * FFKV: (st * 4 + tsub + 1) * FFKV],
                                            in_=pb[:, 0:FFKV], func=AF.Copy)
                    S["PEV"].inc(ins)

        @blk.tensor
        def _(t):
            t.wait_ge(S["WL1"].h, b0["WL1"] + 48)
            for st in range(NST):
                if st >= 1:
                    t.wait_ge(S["Y0"].h, b0["Y0"] + st * 8)
                    t.wait_ge(S["T1"].h, b0["T1"] + st * 8)
                for kc in range(KC):
                    t.wait_ge(S["CAST"].h, b0["CAST"] + st * 24 + kc * 2 + 2)
                    i1 = t.matmul(psA[0:1, :], ones_bf[:], xb[kc % 2][:],
                                  start=(kc == 0), stop=(kc == KC - 1))
                    i2 = t.matmul(psB[0:1, :], ones_bf[:], xq[kc % 2][:],
                                  start=(kc == 0), stop=(kc == KC - 1))
                    S["STAT"].inc(i1, 1)
                    S["STAT"].inc(i2, 1)
                t.wait_ge(S["MR"].h, b0["MR"] + st + 1)
                i1 = t.matmul(psA[:], ones_f[:], r_rstd[:], start=True, stop=True)
                i2 = t.matmul(psB[:], ones_f[:], r_mean[:], start=True, stop=True)
                S["BC"].inc(i2, 1)
                t.wait_ge(S["Y0"].h, b0["Y0"] + st * 8 + 8)
                for kc in range(KC):
                    t.wait_ge(S["CAST"].h, b0["CAST"] + st * 24 + 16 + kc + 1)
                    ins = t.matmul(psB[0:1, :], ones_bf[:], xq[kc % 2][:],
                                   start=(kc == 0), stop=(kc == KC - 1))
                    S["STAT"].inc(ins, 1)
                t.wait_ge(S["RSTD"].h, b0["RSTD"] + st * 2 + 2)
                S["BC"].inc(t.matmul(psA[:], ones_f[:], r_rstd[:], start=True, stop=True), 1)
                t.wait_ge(S["T1"].h, b0["T1"] + st * 8 + 8)
                if st >= 1:
                    t.wait_ge(S["PEV"].h, b0["PEV"] + st * NPJ)
                for gi in range(NPJ):
                    pb = (psC, psD)[gi % 2]
                    if gi >= 2:
                        t.wait_ge(S["PEV"].h, b0["PEV"] + st * NPJ + gi - 1)
                    if gi < 8:
                        for kc in range(KC):
                            ins = t.matmul(pb[:], wqf_s[:, kc * D + gi * 128: kc * D + gi * 128 + 128],
                                           xr[:, kc * STW:(kc + 1) * STW],
                                           start=(kc == 0), stop=(kc == KC - 1))
                    elif gi < 10:
                        oc = gi - 8
                        for kc in range(KC):
                            ins = t.matmul(pb[:],
                                           wkf_s[:, kc * FFKV + oc * 128: kc * FFKV + oc * 128 + 128],
                                           xr[:, kc * STW:(kc + 1) * STW],
                                           start=(kc == 0), stop=(kc == KC - 1))
                    else:
                        tsub = gi - 10
                        for kc in range(KC):
                            ins = t.matmul(pb[:, 0:FFKV],
                                           xr[:, kc * STW + tsub * 128: kc * STW + tsub * 128 + 128],
                                           wvf_s[:, kc * FFKV: (kc + 1) * FFKV],
                                           start=(kc == 0), stop=(kc == KC - 1))
                    S["PMM"].inc(ins, 1)

        @blk.vector
        def _(v):
            for st in range(NST):
                stb = st * STW
                v.wait_ge(S["ROWEV"].h, b0["ROWEV"] + st + 1)
                v.tensor_mul(r_var[:], r_mean[:], r_mean[:])
                v.drain()
                S["ROW1"].inc(v.scalar_tensor_tensor(out=r_var[:], in0=r_msq[:],
                                                     scalar=1e-5, in1=r_var[:],
                                                     op0=ALU.add, op1=ALU.subtract))
                v.wait_ge(S["RSTD"].h, b0["RSTD"] + st * 2 + 1)
                S["MR"].inc(v.tensor_mul(r_mean[:], r_mean[:], r_rstd[:]))
                v.wait_ge(S["BC"].h, b0["BC"] + st * 2 + 1)
                if st >= 1:
                    v.wait_ge(S["CAST"].h, b0["CAST"] + st * 24)
                for kc in range(KC):
                    src = RES[:, kc * SEQ + stb: kc * SEQ + stb + STW]
                    v.tensor_mul(tA[:], src, psA[:])
                    v.drain()
                    S["Y0"].inc(v.tensor_sub(y_st[:, kc * STW:(kc + 1) * STW], tA[:], psB[:]))
                    v.drain()
                v.wait_ge(S["STAT"].h, b0["STAT"] + st * 24 + 24)
                S["ROW1"].inc(v.tensor_scalar(out=r_var[:], in0=psB[0:1, :],
                                              scalar1=1.0 / 1024.0, scalar2=1e-6,
                                              op0=ALU.mult, op1=ALU.add))
                v.wait_ge(S["BC"].h, b0["BC"] + st * 2 + 2)
                v.wait_ge(S["LD"].h, b0["LD"] + 32 * (st + 1))
                if st >= 1:
                    v.wait_ge(S["PMM"].h, b0["PMM"] + st * NPJ)
                for c in range(4):
                    yE = y_st[:, c * STW:(c + 1) * STW]
                    yO = y_st[:, (4 + c) * STW:(5 + c) * STW]
                    v.tensor_mul(zE[:], yE, psA[:])
                    v.tensor_mul(zO[:], yO, psA[:])
                    v.drain()
                    v.tensor_scalar(out=zE[:], in0=zE[:], scalar1=w1n[:, c:c + 1],
                                    scalar2=None, op0=ALU.mult)
                    v.tensor_scalar(out=zO[:], in0=zO[:], scalar1=w1n[:, 4 + c:5 + c],
                                    scalar2=None, op0=ALU.mult)
                    v.drain()
                    cc = cos_s[:, c * STW:(c + 1) * STW]
                    ss = sin_s[:, c * STW:(c + 1) * STW]
                    v.tensor_mul(tA[:], zE[:], cc)
                    v.tensor_mul(tB[:], zO[:], ss)
                    v.tensor_mul(tC[:], zE[:], ss)
                    v.tensor_mul(tD[:], zO[:], cc)
                    v.drain()
                    S["T1"].inc(v.tensor_sub(xr[:, c * STW:(c + 1) * STW], tA[:], tB[:]), 1)
                    S["T1"].inc(v.tensor_add(xr[:, (4 + c) * STW:(5 + c) * STW], tC[:], tD[:]), 1)
                    v.drain()


def phase_B2(nc, S, RES, id_bf, id_f, ones_bf, ones_f, w2n, rb_sb, QZ, Kf, Vf,
             comb_scr, seq, wo_f, rw_d, mD_d, mT_d, tag):
    es = ExitStack()
    sbuf = lambda name, shape, dty=F32: es.enter_context(nc.sbuf_tensor(tag + name, shape, dty))
    psum = lambda name, shape, dty=F32: es.enter_context(nc.psum_tensor(tag + name, shape, dty))
    wof_s = sbuf("wof", [128, KC * D], BF16)
    rw_s = sbuf("rw", [128, KC * NE])
    mD_s = sbuf("mD", [128, 4 * 512], BF16)
    mT_s = sbuf("mT", [128, 4 * 512], BF16)
    Ssb = [sbuf(f"Ss{i}", [128, SEQ]) for i in range(2)]
    STsb = sbuf("STsb", [128, 1024])
    STs = [STsb[:, 0:512], STsb[:, 512:1024]]
    PT2b = sbuf("PT2b", [128, 1024], BF16)
    PT2 = [PT2b[:, 0:512], PT2b[:, 512:1024]]
    c_col = sbuf("ccol", [128, 8])
    c_row = sbuf("crow", [1, 512])
    c_bcs = sbuf("cbcs", [128, 512])
    Ost = sbuf("O", [128, KC * STW], BF16)
    y_st = sbuf("y", [128, KC * STW])
    comb_stage = sbuf("cstg", [NE, 128])
    hb = [sbuf(f"hb{i}", [128, STW], BF16) for i in range(2)]
    xqs = [sbuf(f"xqs{i}", [128, STW], BF16) for i in range(2)]
    t1 = [sbuf(f"t1{i}", [128, STW]) for i in range(2)]
    lg_sb = sbuf("lg", [NE, STW])
    lt_sb = sbuf("lt", [128, 32])
    ctT = sbuf("ctT", [128, 32])
    m1r = sbuf("m1r", [128, 1])
    m2r = sbuf("m2r", [128, 1])
    d12 = sbuf("d12", [128, 1])
    w1r = sbuf("w1r", [128, 1])
    w2r = sbuf("w2r", [128, 1])
    eq1 = sbuf("eq1", [128, 8])
    eq2 = sbuf("eq2", [128, 8])
    l2s = sbuf("l2s", [128, 8])
    r_mean = sbuf("rm", [1, STW])
    r_msq = sbuf("rq", [1, STW])
    r_var = sbuf("rv", [1, STW])
    r_rstd = sbuf("rr", [1, STW])
    r_nmax = [sbuf(f"nm{i}", [128, 1]) for i in range(2)]
    r_sum = [sbuf(f"sum{i}", [128, 1]) for i in range(2)]
    r_lns = [sbuf(f"lns{i}", [128, 1]) for i in range(2)]
    psS = [psum(f"pss{i}", [128, 512]) for i in range(2)]
    STb = [psum(f"stb{i}", [128, 512]) for i in range(2)]
    AVb = [psum(f"avb{i}", [64, 512]) for i in range(2)]
    CBCb = psum("cbcb", [128, 512])
    miscb = psum("miscb", [128, 512])
    ps_crow = miscb[0:1, :]
    ps_r = miscb[0:NE, :]
    ps_lt = miscb[:, 16:24]
    ps_ct = miscb[0:NE, 128:256]

    b0 = {k: s.n for k, s in S.items()}

    sub_list = []
    blk_eng = []
    blk_cnt = []
    blk_pref = []
    acnt = dcnt = 0
    a_thru_sub = []
    for st in range(NST):
        for h in range(NH):
            for qt in range(4):
                gq = st * 4 + qt
                nb = gq // 4 + 1
                sub_list.append((st, h, qt, gq, nb))
                blk_pref.append(len(blk_eng))
                for b in range(nb):
                    if b == nb - 1:
                        blk_eng.append("D"); dcnt += 1; blk_cnt.append(dcnt)
                    else:
                        blk_eng.append("A"); acnt += 1; blk_cnt.append(acnt)
                a_thru_sub.append(acnt)
    blk_pref.append(len(blk_eng))
    NSUB = len(sub_list)
    v_nkt = [4 * (st + 1) for st in range(NST) for h in range(NH)]
    kt_pref = [0]
    for n in v_nkt:
        kt_pref.append(kt_pref[-1] + n)

    def blk_wait(eng, g):
        if g < 0:
            return
        nm = "SEV" if blk_eng[g] == "A" else "SEVD"
        eng.wait_ge(S[nm].h, b0[nm] + blk_cnt[g])

    with es, nc.Block() as blk:
        @blk.gpsimd
        def _(gp):
            S["WL1"].inc(gp.dma_start(out=wof_s[:], in_=wo_f.rearrange("p k d -> p (k d)")), 16)

        @blk.sync
        def _(sy):
            S["LD"].inc(sy.dma_start(out=rw_s[:], in_=rw_d.rearrange("p k e -> p (k e)")), 16)
            S["LD"].inc(sy.dma_start(out=mD_s[:].rearrange("p (i c) -> p i c", i=4), in_=mD_d), 16)
            S["LD"].inc(sy.dma_start(out=mT_s[:].rearrange("p (i c) -> p i c", i=4), in_=mT_d), 16)
            for st in range(NST):
                for tl in range(4):
                    sy.wait_ge(S["AUX"].h, b0["AUX"] + st * 4 + tl + 1)
                    S["EO"].inc(sy.dma_start(
                        out=comb_scr[0:NE, seq * SEQ + st * STW + tl * 128:
                                     seq * SEQ + st * STW + tl * 128 + 128],
                        in_=comb_stage[:]), 16)
            sy.wait_ge(S["EO"].h, S["EO"].n)

        @blk.tensor
        def _(t):
            t.wait_ge(S["LD"].h, b0["LD"] + 48)
            t.wait_ge(S["WL1"].h, b0["WL1"] + 16)

            def stats_pe(st, h):
                for qt in range(4):
                    w = (st * 16 + h) * 4 + qt
                    gq = st * 4 + qt
                    nb = gq // 4 + 1
                    kend = (gq + 1) * 128
                    qch, qba = qloc(h)
                    g = h // 4
                    lhs = QZ[qba:qba + 64,
                             qch * SEQ + gq * 128: qch * SEQ + gq * 128 + 128]
                    for b in range(nb):
                        gb = blk_pref[w] + b
                        blk_wait(t, gb - 2)
                        wdt = min(512, kend - b * 512)
                        rhs = Kf[64 * (g % 2):64 * (g % 2) + 64,
                                 (g // 2) * SEQ + b * 512:(g // 2) * SEQ + b * 512 + wdt]
                        S["SMM"].inc(t.matmul(psS[gb % 2][:, 0:wdt], lhs, rhs,
                                              start=True, stop=True), 1)

            def t2_pe(st, h):
                v = st * 16 + h
                qch, qba = qloc(h)
                g = h // 4
                t.wait_ge(S["PN"].h, b0["PN"] + 4 * (v + 1))
                if v >= 1:
                    t.wait_ge(S["CRC"].h, b0["CRC"] + v)
                if h == 0 and st >= 1:
                    t.wait_ge(S["SIL"].h, b0["SIL"] + st)
                    t.wait_ge(S["AUX3"].h, b0["AUX3"] + st * 4)
                    t.wait_ge(S["AUX"].h, b0["AUX"] + st * 4)
                for qt in range(4):
                    ins = t.transpose(ps_crow[0:1, qt * 128:(qt + 1) * 128],
                                      c_col[:, (v % 2) * 4 + qt:(v % 2) * 4 + qt + 1], id_f[:])
                S["CTM"].inc(ins, 1)
                t.wait_ge(S["CRC"].h, b0["CRC"] + v + 1)
                if v >= 1:
                    t.wait_ge(S["CBC2"].h, b0["CBC2"] + v)
                S["CBM"].inc(t.matmul(CBCb[:], ones_f[:], c_row[:], start=True, stop=True), 1)
                nkt = v_nkt[v]
                for kt in range(nkt + 2):
                    if kt < nkt:
                        m = kt_pref[v] + kt
                        if m >= 2:
                            t.wait_ge(S["STEV"].h, b0["STEV"] + m - 1)
                        rhs = QZ[qba:qba + 64, qch * SEQ + st * STW: qch * SEQ + (st + 1) * STW]
                        lhsK = Kf[64 * (g % 2):64 * (g % 2) + 64,
                                  (g // 2) * SEQ + kt * 128:(g // 2) * SEQ + kt * 128 + 128]
                        S["STM"].inc(t.matmul(STb[m % 2][:], lhsK, rhs, start=True, stop=True), 1)
                    if kt >= 2:
                        ka = kt - 2
                        m = kt_pref[v] + ka
                        t.wait_ge(S["PEXP"].h, b0["PEXP"] + m // 2 + 1)
                        if ka == 0:
                            t.wait_ge(S["OEV"].h, b0["OEV"] + max(0, v - 1))
                        S["AVM"].inc(t.matmul(AVb[v % 2][:],
                                              Vf[:, ka * FFKV + 64 * g: ka * FFKV + 64 * g + 64],
                                              PT2[m % 2],
                                              start=(ka == 0), stop=(ka == nkt - 1)), 1)

            for st in range(NST):
                if st >= 1:
                    t.wait_ge(S["H1"].h, b0["H1"] + st * 8)
                    t.wait_ge(S["T1"].h, b0["T1"] + st * 8)
                for h in range(NH):
                    stats_pe(st, h)
                    if h >= 1:
                        t2_pe(st, h - 1)
                t2_pe(st, NH - 1)
                stb = st * STW
                t.wait_ge(S["STEV"].h, b0["STEV"] + kt_pref[st * 16 + 16])
                if st >= 1:
                    t.wait_ge(S["Y0"].h, b0["Y0"] + st * 8)
                for kc in range(KC):
                    t.wait_ge(S["CAST"].h, b0["CAST"] + st * 24 + kc * 2 + 2)
                    i1 = t.matmul(STb[0][0:1, :], ones_bf[:], hb[kc % 2][:],
                                  start=(kc == 0), stop=(kc == KC - 1))
                    i2 = t.matmul(STb[1][0:1, :], ones_bf[:], xqs[kc % 2][:],
                                  start=(kc == 0), stop=(kc == KC - 1))
                    S["STAT"].inc(i1, 1)
                    S["STAT"].inc(i2, 1)
                t.wait_ge(S["MR"].h, b0["MR"] + st + 1)
                i1 = t.matmul(STb[0][:], ones_f[:], r_rstd[:], start=True, stop=True)
                i2 = t.matmul(STb[1][:], ones_f[:], r_mean[:], start=True, stop=True)
                S["BC"].inc(i2, 1)
                t.wait_ge(S["OEV"].h, b0["OEV"] + st * 16 + 16)
                if st >= 1:
                    t.wait_ge(S["H1"].h, b0["H1"] + st * 8)
                for oc in range(KC):
                    pb = psS[oc % 2]
                    if oc >= 2:
                        t.wait_ge(S["H1"].h, b0["H1"] + st * 8 + oc - 1)
                    for kc in range(KC):
                        ins = t.matmul(pb[:], wof_s[:, kc * D + oc * 128: kc * D + oc * 128 + 128],
                                       Ost[:, kc * STW:(kc + 1) * STW],
                                       start=(kc == 0), stop=(kc == KC - 1))
                    S["WMM"].inc(ins, 1)
                t.wait_ge(S["H1"].h, b0["H1"] + st * 8 + 8)
                for kc in range(KC):
                    t.wait_ge(S["CAST"].h, b0["CAST"] + st * 24 + 16 + kc + 1)
                    ins = t.matmul(STb[1][0:1, :], ones_bf[:], xqs[kc % 2][:],
                                   start=(kc == 0), stop=(kc == KC - 1))
                    S["STAT"].inc(ins, 1)
                t.wait_ge(S["RSTD"].h, b0["RSTD"] + st * 2 + 2)
                t.wait_ge(S["Y0"].h, b0["Y0"] + st * 8 + 8)
                S["BC"].inc(t.matmul(STb[0][:], ones_f[:], r_rstd[:], start=True, stop=True), 1)
                if st >= 1:
                    t.wait_ge(S["AUX2"].h, b0["AUX2"] + st)
                for oc in range(KC):
                    t.wait_ge(S["T1"].h, b0["T1"] + st * 8 + oc + 1)
                    ins = t.matmul(ps_r, rw_s[:, oc * NE:(oc + 1) * NE], t1[oc % 2][:],
                                   start=(oc == 0), stop=(oc == KC - 1))
                    S["RMM"].inc(ins, 1)
                t.wait_ge(S["AUX2"].h, b0["AUX2"] + st + 1)
                for tl in range(4):
                    if st * 4 + tl >= 1:
                        t.wait_ge(S["AUX3"].h, b0["AUX3"] + st * 4 + tl)
                        t.wait_ge(S["AUX"].h, b0["AUX"] + st * 4 + tl)
                    S["H2"].inc(t.transpose(ps_lt, lg_sb[:, tl * 128:(tl + 1) * 128], id_f[0:8, 0:8]), 1)
                    t.wait_ge(S["CBMM"].h, b0["CBMM"] + st * 4 + tl + 1)
                    S["EPI"].inc(t.transpose(ps_ct, ctT[:, tl * 8:(tl + 1) * 8], id_f[:]), 1)

        @blk.scalar
        def _(sc):
            def stats_act(st, h):
                for qt in range(4):
                    w = (st * 16 + h) * 4 + qt
                    gq = st * 4 + qt
                    nb = gq // 4 + 1
                    kend = (gq + 1) * 128
                    for b in range(nb - 1):
                        gb = blk_pref[w] + b
                        sc.wait_ge(S["SMM"].h, b0["SMM"] + gb + 1)
                        if w >= 2:
                            sc.wait_ge(S["P"].h, b0["P"] + w - 1)
                        S["SEV"].inc(sc.activation(out=Ssb[w % 2][:, b * 512:(b + 1) * 512],
                                                   in_=psS[gb % 2][:], func=AF.Copy))
                    sc.wait_ge(S["NM"].h, b0["NM"] + w + 1)
                    sc.activation(out=Ssb[w % 2][:, 0:kend], in_=Ssb[w % 2][:, 0:kend],
                                  func=AF.Exp, bias=r_nmax[w % 2][:], accum_out=r_sum[w % 2][:])
                    sc.drain()
                    S["P"].inc(sc.activation(out=r_lns[w % 2][:], in_=r_sum[w % 2][:], func=AF.Ln))

            def t2_act(st, h):
                v = st * 16 + h
                sc.wait_ge(S["CTM"].h, b0["CTM"] + v + 1)
                S["CRC"].inc(sc.activation(out=c_row[:], in_=ps_crow, func=AF.Copy))
                sc.wait_ge(S["CBM"].h, b0["CBM"] + v + 1)
                if v >= 1:
                    sc.wait_ge(S["STEV"].h, b0["STEV"] + kt_pref[v])
                S["CBC2"].inc(sc.activation(out=c_bcs[:], in_=CBCb[:], func=AF.Copy))
                nkt = v_nkt[v]
                for kt in range(0, nkt, 2):
                    m = kt_pref[v] + kt
                    sc.wait_ge(S["STEV"].h, b0["STEV"] + m + 2)
                    if m >= 2:
                        sc.wait_ge(S["AVM"].h, b0["AVM"] + m)
                    S["PEXP"].inc(sc.activation(out=PT2b[:], in_=STsb[:], func=AF.Exp))
                sc.wait_ge(S["AVM"].h, b0["AVM"] + kt_pref[v + 1])
                dst = Ost[64 * (h % 2):64 * (h % 2) + 64, (h // 2) * STW:(h // 2 + 1) * STW]
                S["OEV"].inc(sc.activation(out=dst, in_=AVb[v % 2][:], func=AF.Copy))

            for st in range(NST):
                for h in range(NH):
                    stats_act(st, h)
                    if h >= 1:
                        t2_act(st, h - 1)
                t2_act(st, NH - 1)
                stb = st * STW
                if st >= 1:
                    sc.wait_ge(S["STAT"].h, b0["STAT"] + st * 24)
                for kc in range(KC):
                    if kc >= 2:
                        sc.wait_ge(S["STAT"].h, b0["STAT"] + st * 24 + (kc - 2) * 2 + 2)
                    src2 = RES[:, kc * SEQ + stb: kc * SEQ + stb + STW]
                    S["CAST"].inc(sc.activation(out=hb[kc % 2][:], in_=src2, func=AF.Copy))
                    S["CAST"].inc(sc.activation(out=xqs[kc % 2][:], in_=src2, func=AF.Square))
                sc.wait_ge(S["STAT"].h, b0["STAT"] + st * 24 + 16)
                sc.activation(out=r_mean[:], in_=STb[0][0:1, :], func=AF.Copy, scale=1.0 / 1024.0)
                S["ROWEV"].inc(sc.activation(out=r_msq[:], in_=STb[1][0:1, :], func=AF.Copy, scale=1.0 / 1024.0))
                sc.wait_ge(S["ROW1"].h, b0["ROW1"] + st * 2 + 1)
                sc.activation(out=r_var[:], in_=r_var[:], func=AF.Ln)
                sc.drain()
                S["RSTD"].inc(sc.activation(out=r_rstd[:], in_=r_var[:], func=AF.Exp, scale=-0.5))
                for kc in range(KC):
                    sc.wait_ge(S["H1"].h, b0["H1"] + st * 8 + kc + 1)
                    if kc >= 2:
                        sc.wait_ge(S["STAT"].h, b0["STAT"] + st * 24 + 16 + (kc - 2) + 1)
                    S["CAST"].inc(sc.activation(out=xqs[kc % 2][:],
                                                in_=y_st[:, kc * STW:(kc + 1) * STW],
                                                func=AF.Square))
                sc.wait_ge(S["ROW1"].h, b0["ROW1"] + st * 2 + 2)
                sc.activation(out=r_var[:], in_=r_var[:], func=AF.Ln)
                sc.drain()
                S["RSTD"].inc(sc.activation(out=r_rstd[:], in_=r_var[:], func=AF.Exp, scale=-0.5))
                for oc in range(KC):
                    sc.wait_ge(S["T1"].h, b0["T1"] + st * 8 + oc + 1)
                    S["ZC"].inc(sc.activation(out=QZ[:, oc * SEQ + stb: oc * SEQ + stb + STW],
                                              in_=t1[oc % 2][:], func=AF.Copy))
                sc.wait_ge(S["RMM"].h, b0["RMM"] + st * 8 + 8)
                S["SIL"].inc(sc.activation(out=lg_sb[:], in_=ps_r, func=AF.Copy))
                if st >= 1:
                    sc.wait_ge(S["CBMM"].h, b0["CBMM"] + st * 4)
                for tl in range(4):
                    sc.wait_ge(S["H2"].h, b0["H2"] + st * 4 + tl + 1)
                    S["AUX3"].inc(sc.activation(out=lt_sb[:, tl * 8:(tl + 1) * 8], in_=ps_lt, func=AF.Copy))
                    sc.wait_ge(S["HST"].h, b0["HST"] + st * 4 + tl + 1)
                    S["CBEV"].inc(sc.activation(out=w1r[:], in_=d12[:], func=AF.Sigmoid))
                    sc.wait_ge(S["EPI"].h, b0["EPI"] + st * 4 + tl + 1)
                    if st * 4 + tl >= 1:
                        sc.wait_ge(S["EO"].h, b0["EO"] + 16 * (st * 4 + tl))
                    S["AUX"].inc(sc.activation(out=comb_stage[:], in_=ps_ct, func=AF.Copy))

        @blk.vector
        def _(v):
            v.wait_ge(S["LD"].h, b0["LD"] + 48)

            def c_write(w2):
                vv, qt2 = w2 // 4, w2 % 4
                v.wait_ge(S["P"].h, b0["P"] + w2 + 1)
                if qt2 == 0 and vv >= 2:
                    v.wait_ge(S["CTM"].h, b0["CTM"] + vv - 1)
                S["PN"].inc(v.tensor_sub(c_col[:, (vv % 2) * 4 + qt2:(vv % 2) * 4 + qt2 + 1],
                                         r_nmax[w2 % 2][:], r_lns[w2 % 2][:]))

            def stats_dve(st, h):
                for qt in range(4):
                    w = (st * 16 + h) * 4 + qt
                    gq = st * 4 + qt
                    nb = gq // 4 + 1
                    kend = (gq + 1) * 128
                    dlt = gq % 4
                    lw = kend - (nb - 1) * 512
                    gb = blk_pref[w] + nb - 1
                    v.wait_ge(S["SMM"].h, b0["SMM"] + gb + 1)
                    if w >= 2:
                        v.wait_ge(S["P"].h, b0["P"] + w - 1)
                    S["SEVD"].inc(v.tensor_add(Ssb[w % 2][:, (nb - 1) * 512:(nb - 1) * 512 + lw],
                                               psS[gb % 2][:, 0:lw],
                                               mD_s[:, dlt * 512: dlt * 512 + lw]))
                    v.drain()
                    v.wait_ge(S["SEV"].h, b0["SEV"] + a_thru_sub[w])
                    S["NM"].inc(v.reduce_max(r_nmax[w % 2][:], Ssb[w % 2][:, 0:kend],
                                             axis=AX, negate=True))
                    if w >= 1:
                        c_write(w - 1)

            def t2_dve(st, h):
                v2 = st * 16 + h
                nkt = v_nkt[v2]
                for kt in range(nkt):
                    m = kt_pref[v2] + kt
                    v.wait_ge(S["STM"].h, b0["STM"] + m + 1)
                    v.wait_ge(S["CBC2"].h, b0["CBC2"] + v2 + 1)
                    if m >= 2:
                        v.wait_ge(S["PEXP"].h, b0["PEXP"] + (m - 2) // 2 + 1)
                    dlt = kt - 4 * st
                    if dlt >= 0:
                        v.tensor_add(STs[m % 2], STb[m % 2][:], c_bcs[:])
                        v.drain()
                        ins = v.tensor_add(STs[m % 2], STs[m % 2],
                                           mT_s[:, dlt * 512:(dlt + 1) * 512])
                    else:
                        ins = v.tensor_add(STs[m % 2], STb[m % 2][:], c_bcs[:])
                    S["STEV"].inc(ins)

            for st in range(NST):
                for h in range(NH):
                    stats_dve(st, h)
                    if h == NH - 1:
                        c_write((st * 16 + h) * 4 + 3)
                    if h >= 1:
                        t2_dve(st, h - 1)
                t2_dve(st, NH - 1)
                stb = st * STW
                v.wait_ge(S["ROWEV"].h, b0["ROWEV"] + st + 1)
                v.tensor_mul(r_var[:], r_mean[:], r_mean[:])
                v.drain()
                S["ROW1"].inc(v.scalar_tensor_tensor(out=r_var[:], in0=r_msq[:],
                                                     scalar=1e-5, in1=r_var[:],
                                                     op0=ALU.add, op1=ALU.subtract))
                v.wait_ge(S["RSTD"].h, b0["RSTD"] + st * 2 + 1)
                S["MR"].inc(v.tensor_mul(r_mean[:], r_mean[:], r_rstd[:]))
                v.wait_ge(S["BC"].h, b0["BC"] + st * 2 + 1)
                if st >= 1:
                    v.wait_ge(S["CAST"].h, b0["CAST"] + st * 24)
                    v.wait_ge(S["ZC"].h, b0["ZC"] + st * 8)
                    v.wait_ge(S["RMM"].h, b0["RMM"] + st * 8)
                for oc in range(KC):
                    src2 = RES[:, oc * SEQ + stb: oc * SEQ + stb + STW]
                    ych = y_st[:, oc * STW:(oc + 1) * STW]
                    v.tensor_mul(t1[0][:], src2, STb[0][:])
                    v.drain()
                    S["Y0"].inc(v.tensor_sub(ych, t1[0][:], STb[1][:]))
                    v.drain()
                    v.wait_ge(S["WMM"].h, b0["WMM"] + st * 8 + oc + 1)
                    S["H1"].inc(v.tensor_add(ych, ych, psS[oc % 2][:]))
                    v.drain()
                    S["X1"].inc(v.tensor_add(src2, src2, ych))
                    v.drain()
                v.wait_ge(S["STAT"].h, b0["STAT"] + st * 24 + 24)
                S["ROW1"].inc(v.tensor_scalar(out=r_var[:], in0=STb[1][0:1, :],
                                              scalar1=1.0 / 1024.0, scalar2=1e-6,
                                              op0=ALU.mult, op1=ALU.add))
                v.wait_ge(S["BC"].h, b0["BC"] + st * 2 + 2)
                for oc in range(KC):
                    if oc >= 2:
                        v.wait_ge(S["RMM"].h, b0["RMM"] + st * 8 + oc - 1)
                        v.wait_ge(S["ZC"].h, b0["ZC"] + st * 8 + oc - 1)
                    ych = y_st[:, oc * STW:(oc + 1) * STW]
                    v.tensor_mul(t1[oc % 2][:], ych, STb[0][:])
                    v.drain()
                    S["T1"].inc(v.tensor_scalar(out=t1[oc % 2][:], in0=t1[oc % 2][:],
                                                scalar1=w2n[:, oc:oc + 1], scalar2=None,
                                                op0=ALU.mult))
                v.wait_ge(S["SIL"].h, b0["SIL"] + st + 1)
                S["AUX2"].inc(v.tensor_scalar(out=lg_sb[:], in0=lg_sb[:],
                                              scalar1=rb_sb[0:NE, 0:1], scalar2=None,
                                              op0=ALU.add))
                for tl in range(4):
                    v.wait_ge(S["AUX3"].h, b0["AUX3"] + st * 4 + tl + 1)
                    if st * 4 + tl >= 4:
                        v.wait_ge(S["EPI"].h, b0["EPI"] + st * 4 + tl - 3)
                    lg = lt_sb[:, tl * 8:(tl + 1) * 8]
                    v.reduce_max(m1r[:], lg, axis=AX)
                    v.drain()
                    v.tensor_scalar(out=eq1[:], in0=lg, scalar1=m1r[:], scalar2=None, op0=ALU.is_ge)
                    v.drain()
                    v.scalar_tensor_tensor(out=l2s[:], in0=eq1[:], scalar=-1e30, in1=lg,
                                           op0=ALU.mult, op1=ALU.add)
                    v.drain()
                    v.reduce_max(m2r[:], l2s[:], axis=AX)
                    v.drain()
                    v.tensor_scalar(out=eq2[:], in0=l2s[:], scalar1=m2r[:], scalar2=None, op0=ALU.is_ge)
                    S["HST"].inc(v.tensor_sub(d12[:], m1r[:], m2r[:]))
                    v.wait_ge(S["CBEV"].h, b0["CBEV"] + st * 4 + tl + 1)
                    v.tensor_scalar(out=w2r[:], in0=w1r[:], scalar1=-1.0, scalar2=1.0,
                                    op0=ALU.mult, op1=ALU.add)
                    v.drain()
                    v.tensor_scalar(out=eq1[:], in0=eq1[:], scalar1=w1r[:], scalar2=None, op0=ALU.mult)
                    v.tensor_scalar(out=eq2[:], in0=eq2[:], scalar1=w2r[:], scalar2=None, op0=ALU.mult)
                    v.drain()
                    S["CBMM"].inc(v.tensor_add(ctT[:, tl * 8:(tl + 1) * 8], eq1[:], eq2[:]))


def phase_C(nc, S, RES, ones_f, QZ, comb_scr_h, w1m, w2m, w3m,
            b1t_d, b2t_d, b3t_d, seq, tag):
    es = ExitStack()
    sbuf = lambda name, shape, dty=F32: es.enter_context(nc.sbuf_tensor(tag + name, shape, dty))
    psum = lambda name, shape, dty=F32: es.enter_context(nc.psum_tensor(tag + name, shape, dty))
    w1b = [sbuf(f"w1b{i}", [128, 2 * KC * FBW], BF16) for i in range(2)]
    w2b = [sbuf(f"w2b{i}", [128, 2 * KC * FBW], BF16) for i in range(2)]
    w3b = sbuf("w3b", [128, 8 * D], BF16)
    hpair = sbuf("hp", [128, 8 * FBW], BF16)
    s1 = [sbuf(f"s1{i}", [128, FBW]) for i in range(2)]
    tmp = [sbuf(f"tmp{i}", [128, FBW]) for i in range(2)]
    comb_bc = [sbuf(f"cbc{i}", [128, FBW]) for i in range(2)]
    b1t_s = sbuf("b1t", [128, NE * 32])
    b2t_s = sbuf("b2t", [128, NE * 32])
    b3t_s = sbuf("b3t", [128, NE * KC])
    psh1 = [psum(f"ph1{i}", [128, 512]) for i in range(2)]
    psh2 = [psum(f"ph2{i}", [128, 512]) for i in range(2)]
    pseo = [psum(f"peo{i}", [128, 512]) for i in range(4)]

    b0 = {k: s.n for k, s in S.items()}
    NU = 128

    with es, nc.Block() as blk:
        @blk.gpsimd
        def _(gp):
            for d, s_ in ((b1t_d, b1t_s), (b2t_d, b2t_s), (b3t_d, b3t_s)):
                S["MSK"].inc(gp.dma_start(out=s_[:], in_=d.rearrange("p a b -> p (a b)")), 16)
            for q in range(33):
                if q < 32:
                    e, p = q // 4, q % 4
                    if q >= 2:
                        gp.wait_ge(S["PAIRD"].h, b0["PAIRD"] + q - 1)
                    s1sem = S["WL1"] if q % 2 == 0 else S["W1B"]
                    s2sem = S["WL2"] if q % 2 == 0 else S["W2B"]
                    for s_ in range(2):
                        fb = 2 * p + s_
                        s1sem.inc(gp.dma_start(
                            out=w1b[q % 2][:, s_ * KC * FBW:(s_ + 1) * KC * FBW].rearrange(
                                "p (k c) -> p k c", k=KC),
                            in_=w1m[e, fb]), 16)
                    for s_ in range(2):
                        fb = 2 * p + s_
                        s2sem.inc(gp.dma_start(
                            out=w2b[q % 2][:, s_ * KC * FBW:(s_ + 1) * KC * FBW].rearrange(
                                "p (k c) -> p k c", k=KC),
                            in_=w2m[e, fb]), 16)
                if q >= 1:
                    qq = q - 1
                    for u in range(qq * 4, qq * 4 + 4):
                        e_, p_, tt = u // 16, (u // 4) % 4, u % 4
                        for gidx in range(8):
                            dc = (gidx % 4) + 4 * (gidx // 4)
                            dcv = [0, 1, 2, 3, 4, 5, 6, 7][gidx]
                            gp.wait_ge(S["EPI"].h, b0["EPI"] + 8 * u + gidx + 1)
                            dst = RES[:, dcv * SEQ + tt * FBW: dcv * SEQ + tt * FBW + FBW]
                            S["RADD"].inc(gp.tensor_add(dst, dst, tmp[gidx % 2][:]))
                if q < 32:
                    if q >= 1:
                        gp.wait_ge(S["EO"].h, b0["EO"] + 32 * q)
                    e, p = q // 4, q % 4
                    S["WL3"].inc(gp.dma_start(
                        out=w3b[:].rearrange("p (f k d) -> p f k d", f=2, k=4),
                        in_=w3m[e, 2 * p:2 * p + 2].rearrange("f p k d -> p f k d")), 16)

        @blk.sync
        def _(sy):
            for u in range(NU):
                e, tt = u // 16, u % 4
                if u >= 2:
                    sy.wait_ge(S["EPI"].h, b0["EPI"] + 8 * (u - 1))
                csem = S["CBROW"] if u % 2 == 0 else S["CROWB"]
                bcast = bass.AP(comb_scr_h, e * 2 * SEQ + seq * SEQ + tt * FBW,
                                [[0, 128], [1, FBW]])
                csem.inc(sy.dma_start(out=comb_bc[u % 2][:], in_=bcast), 16)

        @blk.tensor
        def _(t):
            t.wait_ge(S["MSK"].h, b0["MSK"] + 48)
            for u in range(NU):
                e, p, tt = u // 16, (u // 4) % 4, u % 4
                q = u // 4
                if tt == 0:
                    s1sem = "WL1" if q % 2 == 0 else "W1B"
                    s2sem = "WL2" if q % 2 == 0 else "W2B"
                    t.wait_ge(S[s1sem].h, b0[s1sem] + 32 * (q // 2 + 1))
                    t.wait_ge(S[s2sem].h, b0[s2sem] + 32 * (q // 2 + 1))
                for s_ in range(2):
                    for fc in range(4):
                        j = 8 * u + 4 * s_ + fc
                        if j >= 2:
                            t.wait_ge(S["SIL"].h, b0["SIL"] + j - 1)
                        for kc in range(KC):
                            ins = t.matmul(psh1[j % 2][:],
                                           w1b[q % 2][:, (s_ * KC + kc) * FBW + fc * 128:
                                                       (s_ * KC + kc) * FBW + fc * 128 + 128],
                                           QZ[:, kc * SEQ + tt * FBW: kc * SEQ + tt * FBW + FBW],
                                           start=(kc == 0), stop=(kc == KC - 1))
                        S["H1"].inc(ins, 1)
                        if j >= 2:
                            t.wait_ge(S["HST"].h, b0["HST"] + j - 1)
                        for kc in range(KC):
                            ins = t.matmul(psh2[j % 2][:],
                                           w2b[q % 2][:, (s_ * KC + kc) * FBW + fc * 128:
                                                       (s_ * KC + kc) * FBW + fc * 128 + 128],
                                           QZ[:, kc * SEQ + tt * FBW: kc * SEQ + tt * FBW + FBW],
                                           start=(kc == 0), stop=(kc == KC - 1))
                        S["H2"].inc(ins, 1)
                if tt == 3:
                    S["PAIRD"].inc(t.nop(), 1)
                t.wait_ge(S["HST"].h, b0["HST"] + 8 * (u + 1))
                t.wait_ge(S["WL3"].h, b0["WL3"] + 16 * (q + 1))
                for r in range(2):
                    for dci in range(4):
                        dc = 4 * r + dci
                        if r == 0:
                            if u >= 1:
                                t.wait_ge(S["EPI"].h, b0["EPI"] + 8 * (u - 1) + 4 + dci + 1)
                        else:
                            t.wait_ge(S["EPI"].h, b0["EPI"] + 8 * u + dci + 1)
                        for ks in range(8):
                            ins = t.matmul(pseo[dci][:],
                                           w3b[:, ks * D + dc * 128: ks * D + dc * 128 + 128],
                                           hpair[:, ks * FBW:(ks + 1) * FBW],
                                           start=(ks == 0), stop=(ks == KC - 1))
                        S["EO"].inc(ins, 1)

        @blk.scalar
        def _(sc):
            sc.wait_ge(S["MSK"].h, b0["MSK"] + 48)
            for u in range(NU):
                e, p, tt = u // 16, (u // 4) % 4, u % 4
                for s_ in range(2):
                    for fc in range(4):
                        j = 8 * u + 4 * s_ + fc
                        fbch = (2 * p + s_) * 4 + fc
                        sc.wait_ge(S["H1"].h, b0["H1"] + j + 1)
                        if j >= 2:
                            sc.wait_ge(S["HST"].h, b0["HST"] + j - 1)
                        S["SIL"].inc(sc.activation(out=s1[j % 2][:], in_=psh1[j % 2][:],
                                                   func=AF.Silu,
                                                   bias=b1t_s[:, e * 32 + fbch: e * 32 + fbch + 1]))

        @blk.vector
        def _(v):
            v.wait_ge(S["MSK"].h, b0["MSK"] + 48)
            for u in range(NU):
                e, p, tt = u // 16, (u // 4) % 4, u % 4
                for s_ in range(2):
                    for fc in range(4):
                        j = 8 * u + 4 * s_ + fc
                        fbch = (2 * p + s_) * 4 + fc
                        v.wait_ge(S["H2"].h, b0["H2"] + j + 1)
                        v.wait_ge(S["SIL"].h, b0["SIL"] + j + 1)
                        S["HST"].inc(v.scalar_tensor_tensor(
                            out=hpair[:, (s_ * 4 + fc) * FBW:(s_ * 4 + fc + 1) * FBW],
                            in0=psh2[j % 2][:],
                            scalar=b2t_s[:, e * 32 + fbch: e * 32 + fbch + 1],
                            in1=s1[j % 2][:], op0=ALU.add, op1=ALU.mult))
                for r in range(2):
                    for dci in range(4):
                        dc = 4 * r + dci
                        gidx = 8 * u + 4 * r + dci
                        v.wait_ge(S["EO"].h, b0["EO"] + gidx + 1)
                        if r == 0 and dci == 0:
                            csem = "CBROW" if u % 2 == 0 else "CROWB"
                            v.wait_ge(S[csem].h, b0[csem] + 16 * (u // 2 + 1))
                        if gidx >= 2:
                            v.wait_ge(S["RADD"].h, b0["RADD"] + gidx - 1)
                        if p == 0:
                            ins = v.scalar_tensor_tensor(
                                out=tmp[gidx % 2][:], in0=pseo[dci][:],
                                scalar=b3t_s[:, e * KC + dc: e * KC + dc + 1],
                                in1=comb_bc[u % 2][:], op0=ALU.add, op1=ALU.mult)
                        else:
                            ins = v.tensor_mul(tmp[gidx % 2][:], pseo[dci][:], comb_bc[u % 2][:])
                        S["EPI"].inc(ins)


def _tile_w(w):
    n = w.shape[1]
    return np.ascontiguousarray(w.reshape(KC, 128, n).transpose(1, 0, 2))


def kernel(L0_attn_hidden, L0_attn_Wq, L0_attn_Wk, L0_attn_Wv, L0_attn_Wo,
           L0_ffn_norm1_w, L0_ffn_Wq, L0_ffn_Wk, L0_ffn_Wv, L0_ffn_Wo,
           L0_ffn_norm2_w, L0_ffn_router_w, L0_ffn_router_b, L0_ffn_W1,
           L0_ffn_b1, L0_ffn_W2, L0_ffn_b2, L0_ffn_W3, L0_ffn_b3,
           _stop=None):
    sig = np.concatenate([np.arange(0, D, 2), np.arange(1, D, 2)])
    sinv = np.argsort(sig)

    x = np.asarray(L0_attn_hidden, dtype=np.float32)
    f32 = lambda a: np.asarray(a, dtype=np.float32)

    shared = {}
    shared["wq_o"] = _tile_w(np.ascontiguousarray(f32(L0_attn_Wq).T[sig]))
    shared["wk_o"] = _tile_w(np.ascontiguousarray(f32(L0_attn_Wk).T[sig]))
    shared["wv_o"] = _tile_w(np.ascontiguousarray(f32(L0_attn_Wv).T[sig]))
    shared["wo_o"] = _tile_w(np.ascontiguousarray(f32(L0_attn_Wo).T[:, sig]))
    qperm = np.empty(D, dtype=np.int64)
    for h_ in range(16):
        chn = (h_ % 4) + 4 * (h_ // 8)
        bas = 64 * ((h_ // 4) % 2)
        qperm[chn * 128 + bas: chn * 128 + bas + 64] = np.arange(64 * h_, 64 * h_ + 64)
    shared["wq_f"] = _tile_w(np.ascontiguousarray(f32(L0_ffn_Wq)[:, qperm]))
    shared["wk_f"] = _tile_w(f32(L0_ffn_Wk))
    shared["wv_f"] = _tile_w(f32(L0_ffn_Wv))
    shared["wo_f"] = _tile_w(np.ascontiguousarray(f32(L0_ffn_Wo)[:, sig]))
    shared["w1n"] = np.ascontiguousarray(f32(L0_ffn_norm1_w)[sig].reshape(KC, 128).T)
    shared["w2n"] = np.ascontiguousarray(f32(L0_ffn_norm2_w)[sig].reshape(KC, 128).T)
    shared["rw"] = _tile_w(np.ascontiguousarray(f32(L0_ffn_router_w)[sig]))
    shared["rb"] = np.ascontiguousarray(f32(L0_ffn_router_b).reshape(NE, 1))
    w1p = f32(L0_ffn_W1)[:, sig, :]
    w2p = f32(L0_ffn_W2)[:, sig, :]
    w3p = f32(L0_ffn_W3)[:, :, sig]
    shared["w1m"] = np.ascontiguousarray(
        w1p.reshape(NE, KC, 128, NFB, FBW).transpose(0, 3, 2, 1, 4))
    shared["w2m"] = np.ascontiguousarray(
        w2p.reshape(NE, KC, 128, NFB, FBW).transpose(0, 3, 2, 1, 4))
    shared["w3m"] = np.ascontiguousarray(
        w3p.reshape(NE, NFB, 4, 128, D).transpose(0, 1, 3, 2, 4))
    shared["b1t"] = np.ascontiguousarray(
        f32(L0_ffn_b1).reshape(NE, 32, 128).transpose(2, 0, 1))
    shared["b2t"] = np.ascontiguousarray(
        f32(L0_ffn_b2).reshape(NE, 32, 128).transpose(2, 0, 1))
    shared["b3t"] = np.ascontiguousarray(
        f32(L0_ffn_b3)[:, sig].reshape(NE, KC, 128).transpose(2, 0, 1))
    half = D // 2
    theta = (1.0 / (np.float32(10000.0) **
                    (np.arange(half, dtype=np.float32) / np.float32(half)))).astype(np.float32)
    pos = np.arange(SEQ, dtype=np.float32)[:, None]
    ang = (pos * theta[None, :]).astype(np.float32)
    cos = np.cos(ang).astype(np.float32).T
    sin = np.sin(ang).astype(np.float32).T
    shared["cos_t"] = np.ascontiguousarray(cos.reshape(4, 128, SEQ).transpose(1, 0, 2))
    shared["sin_t"] = np.ascontiguousarray(sin.reshape(4, 128, SEQ).transpose(1, 0, 2))
    shared["id_f"] = np.eye(128, dtype=np.float32)
    shared["ones_bf"] = np.ones((128, 1), dtype=ml_dtypes.bfloat16)
    shared["ones_f"] = np.ones((1, 128), dtype=np.float32)
    p = np.arange(128)[:, None]
    c0 = np.arange(128)[None, :]
    cD = np.arange(512)[None, :]
    mD = np.stack([np.where(cD <= dlt + p, 0.0, -1e9).astype(ml_dtypes.bfloat16)
                   for dlt in (0, 128, 256, 384)], axis=1)
    shared["mD"] = np.ascontiguousarray(mD)
    mT = np.stack([np.where(cD >= dlt + p, 0.0, -1e9).astype(ml_dtypes.bfloat16)
                   for dlt in (0, 128, 256, 384)], axis=1)
    shared["mT"] = np.ascontiguousarray(mT)
    mA = np.stack([np.where((cD >= dlt + p) & (cD <= dlt + p + 255), 0.0, -1e9).astype(ml_dtypes.bfloat16)
                   for dlt in (-256, -128, 0, 128, 256, 384)], axis=1)
    shared["mA"] = np.ascontiguousarray(mA)

    in_maps = []
    for c in range(NCORES):
        xc = x[2 * c:2 * c + 2]
        xc = xc.transpose(2, 0, 1).reshape(D, 2 * SEQ)[sig]
        xc = np.ascontiguousarray(xc.reshape(KC, 128, 2 * SEQ).transpose(1, 0, 2))
        m = dict(shared)
        m["x"] = xc
        in_maps.append(m)

    nc = build_program(stop_phase=_stop)
    res = run_bass_kernel_spmd(nc, in_maps, core_ids=list(range(NCORES)))

    out = np.empty((16, SEQ, D), dtype=np.float32)
    for c in range(NCORES):
        oc = res.results[c]["out"]
        oc = oc.transpose(1, 0, 2).reshape(D, 2, SEQ)
        out[2 * c:2 * c + 2] = oc[sinv].transpose(1, 2, 0)
    return out
